# version 02
# baseline (speedup 1.0000x reference)
import numpy as np
import ml_dtypes

import concourse.bass as bass
import concourse.mybir as mybir
from concourse.bass_utils import run_bass_kernel_spmd

F32 = mybir.dt.float32
BF16 = mybir.dt.bfloat16
ACT = mybir.ActivationFunctionType
ALU = mybir.AluOpType
AX = mybir.AxisListType.X

MAGIC = float(np.float32(12582912.0))
B, T, D = 4, 4096, 1024
HID, QK, INNER = 2048, 128, 4096
DI, DS, DTR, NE = 2048, 16, 64, 4
QBLOCKS = [[0, 3, 4, 7], [1, 2, 5, 6]]
NT, TS = 16, 2048
NPH = 99
bf16 = ml_dtypes.bfloat16


def tglob(tt, h):
    return QBLOCKS[h][tt // 4] * 512 + (tt % 4) * 128


class Build:
    def __init__(self, nphase=NPH):
        self.nphase = nphase
        self.nc = bass.Bass("TRN2", target_bir_lowering=False, debug=False)
        self.S = {}
        self._cm = []
        self._pid = {}
        self.bank_ev = {}
        self.syq = {0: [], 1: []}
        self._roleq = None

    def enter(self, cm):
        self._cm.append(cm)
        return cm.__enter__()

    def close(self):
        for cm in reversed(self._cm):
            cm.__exit__(None, None, None)

    def sem(self, name):
        setattr(self, name, self.enter(self.nc.semaphore(name)))
        self.S[name] = 0

    def wL(self, eng):
        if eng is not self.syp:
            self.syp.wait_ge(self.sL, self.S["sL"])
        eng.wait_ge(self.sL, self.S["sL"])
        eng.wait_ge(self.sLg, self.S["sLg"])

    def wS(self, eng):
        if eng is not self.syp:
            self.syp.wait_ge(self.sS, self.S["sS"])
        eng.wait_ge(self.sS, self.S["sS"])

    def inc(self, n, by=1):
        self.S[n] += by
        return self.S[n]

    def dma(self, eng, out, in_, sname):
        if eng is self.syp:
            self.sydma(out, in_, sname)
            return
        if eng is self.g and sname == "sL":
            sname = "sLg"
        eng.dma_start(out=out, in_=in_).then_inc(getattr(self, sname), 16)
        self.inc(sname, 16)
        if sname == "sLg":
            eng.wait_ge(self.sLg, self.S["sLg"])

    def pidmod2(self, eng):
        return self.nc.partition_id([eng.engine]) % 2

    class _RecInstr:
        def __init__(self, op):
            self.op = op

        def then_inc(self, sem, n):
            self.op[3].append((sem, n))
            return self

    class _Rec:
        def __init__(self, b, name):
            self._b = b
            self._name = name
            self.pre = []
            self.role = {0: [], 1: []}
            self.post = []
            self.mode = "pre"
            self.rolecur = 0

        def _q(self):
            if self.mode == "role":
                return self.role[self.rolecur]
            return self.pre if self.mode == "pre" else self.post

        def __getattr__(self, attr):
            if attr.startswith("_") or attr in ("pre", "role", "post", "mode",
                                                "rolecur"):
                raise AttributeError(attr)

            def call(*a, **k):
                op = [attr, a, k, []]
                self._q().append(op)
                return Build._RecInstr(op)
            return call

    @staticmethod
    def _replay_ops(eng, ops):
        for (name, a, k, incs) in ops:
            inst = getattr(eng, name)(*a, **k)
            for (sem, n) in incs:
                inst.then_inc(sem, n)

    def replay_all(self, blk):
        nc = self.nc
        for nm in ["g", "tn", "sc", "ve"]:
            rec = getattr(self, nm)
            dec = {"g": blk.gpsimd, "tn": blk.tensor, "sc": blk.scalar,
                   "ve": blk.vector}[nm]

            def body(eng, rec=rec):
                self._replay_ops(eng, rec.pre)
                if rec.role[0] or rec.role[1]:
                    idx = nc.partition_id([eng.engine]) % 2
                    for h in eng.Switch(index=idx, n=2):
                        self._replay_ops(eng, rec.role[h])
                self._replay_ops(eng, rec.post)
            dec(body)

        def sybody(sy):
            idx = nc.partition_id([sy.engine]) % 2
            for h in sy.Switch(index=idx, n=2):
                for op in self.syq[h]:
                    if op[0] == "w":
                        sy.wait_ge(op[1], op[2])
                    else:
                        sy.dma_start(out=op[1], in_=op[2]).then_inc(op[3], 16)
        blk.sync(sybody)

    def role_region(self, recorders):
        snap = end0 = None
        bsnap = bend0 = None
        for h in (0, 1):
            for r in recorders:
                r.mode = "role"
                r.rolecur = h
            self._roleq = h
            if h == 0:
                snap = dict(self.S)
                bsnap = dict(self.bank_ev)
            else:
                end0 = dict(self.S)
                bend0 = dict(self.bank_ev)
                self.S = dict(snap)
                self.bank_ev = dict(bsnap)
            yield h
            if h == 1:
                assert self.S == end0, f"role mismatch {self.S} vs {end0}"
                assert self.bank_ev == bend0, "bank_ev role mismatch"
        for r in recorders:
            r.mode = "post"
        self._roleq = None

    class _SyP:
        def __init__(self, b):
            self.b = b

        def wait_ge(self, sem, val):
            for q in self.b._syq_targets():
                q.append(("w", sem, val))

    def _syq_targets(self):
        if self._roleq is None:
            return [self.syq[0], self.syq[1]]
        return [self.syq[self._roleq]]

    def sydma(self, out, in_, sname):
        sem = getattr(self, sname)
        for q in self._syq_targets():
            q.append(("d", out, in_, sem))
        self.inc(sname, 16)

    def sydma2(self, out_fn, in_fn, sname):
        assert self._roleq is None
        sem = getattr(self, sname)
        for h in (0, 1):
            o = out_fn(h) if callable(out_fn) else out_fn
            i = in_fn(h) if callable(in_fn) else in_fn
            self.syq[h].append(("d", o, i, sem))
        self.inc(sname, 16)

    def flush_syq(self):
        sy = self.sy
        for h in self.rswitch_noassert([sy]):
            for op in self.syq[h]:
                if op[0] == "w":
                    sy.wait_ge(op[1], op[2])
                else:
                    sy.dma_start(out=op[1], in_=op[2]).then_inc(op[3], 16)

    def rswitch_noassert(self, engines):
        idx = [self.pidmod2(e) for e in engines]
        gen = self.nc.Switch(engines=engines, index=idx, n=2)
        for h in gen:
            yield h

    def rswitch(self, engines):
        idx = [self.pidmod2(e) for e in engines]
        gen = self.nc.Switch(engines=engines, index=idx, n=2)
        snap = end0 = None
        for h in gen:
            if h == 0:
                snap = dict(self.S)
            else:
                end0 = dict(self.S)
                self.S = dict(snap)
            yield h
        if end0 is not None:
            assert self.S == end0, f"role sem mismatch: {self.S} != {end0}"

    def build(self):
        nc = self.nc
        dr = {}

        def din(name, shape, dtype=F32):
            dr[name] = nc.dram_tensor(name, list(shape), dtype,
                                      kind="ExternalInput").ap()

        def dsh(name, shape, dtype=F32):
            dr[name] = nc.dram_tensor(name, list(shape), dtype, kind="Internal",
                                      addr_space="Shared").ap()

        def dlo(name, shape, dtype=F32):
            dr[name] = nc.dram_tensor(name, list(shape), dtype,
                                      kind="Internal").ap()

        din("x_sh", [TS, D])
        din("w_hid", [D, 2 * HID]); din("w_qk", [D, QK])
        din("osg0", [128, 1]); din("osb0", [128, 1])
        din("osg1", [128, 1]); din("osb1", [128, 1])
        din("w_out0", [HID, D])
        din("wg_half", [HID, D]); din("wu_half", [HID, D])
        din("wd_half", [512, INNER])
        din("w_in", [D, 2 * DI])
        din("convw", [128, 64]); din("convb", [128, 16])
        din("w_xp", [DI, 96]); din("w_dt", [DTR, DI]); din("bdt_c", [128, 16])
        din("alog_c", [128, 128]); din("dskip_c", [128, 8])
        din("w_out1", [DI, D]); din("w_rt", [D, NE])
        din("w1", [NE, D, INNER]); din("wo", [INNER, D])
        din("b_hid_r", [1, 2 * HID]); din("b_qk_r", [1, QK])
        din("b_out0_r", [1, D]); din("b1_r", [NE, INNER]); din("bo_r", [1, D])
        din("ln_g_r", [1, D]); din("ln_b_r", [1, D])
        din("ln1_g_r", [1, D]); din("ln1_b_r", [1, D])
        din("ln2_g_r", [1, D]); din("ln2_b_r", [1, D])
        din("id_bf", [128, 128], BF16); din("id_f32", [128, 128], F32)
        din("onesb_r", [1, 128], BF16); din("onesf_r", [1, 128], F32)
        din("onesf_c", [128, 1], F32)
        din("invq2_c", [128, NT]); din("dmask", [128, 2048], BF16)
        out_sh = nc.dram_tensor("out_sh", [TS, D], F32,
                                kind="ExternalOutput").ap()

        dsh("s_v", [T, HID], BF16); dsh("s_kT", [QK, T], BF16)
        dsh("s_wsum", [2, 16])
        dsh("s_wgT", [D, INNER], BF16); dsh("s_wuT", [D, INNER], BF16)
        dsh("s_wdT", [INNER, D], BF16)
        dsh("s_u", [DI, T + 3]); dsh("s_z", [DI, T])
        dsh("s_dlt", [DI, T]); dsh("s_uc", [DI, T])
        dsh("s_bc", [32, T]); dsh("s_g", [DI, T], BF16)
        dlo("l_gate", [NT, 128, HID], BF16)
        dlo("l_xa", [NT, 128, D]); dlo("l_x2", [NT, 128, D])
        dlo("l_xa2", [NT, 128, D])
        dlo("l_glin", [NT, 128, INNER], BF16)
        dlo("l_ulin", [NT, 128, INNER], BF16)
        dlo("l_xq2T", [NT, 128, INNER], BF16)
        dlo("l_x3", [NT, 128, D]); dlo("l_hs", [NT, 128, D])
        dlo("l_part", [NT, 128, D])
        dlo("l_inter", [16, NT, 128, 256], BF16)
        for i in range(8):
            dlo(f"cc_i{i}", [1, 8]); dlo(f"cc_o{i}", [1, 8])
        self.dr = dr

        for s in ["sL", "sLg", "sS", "sT", "sV", "sC", "sB"]:
            self.sem(s)

        t = lambda nm, sh, dt=F32: self.enter(nc.sbuf_tensor(nm, list(sh), dt))
        self.cst = t("cst", [128, 256], BF16)
        self.cstf = t("cstf", [128, 256])
        self.onesbr = t("onesbr", [1, 128], BF16)
        self.onesfr = t("onesfr", [1, 128])
        self.cols = t("cols", [128, 64])
        self.rowb = t("rowb", [1, 8192], BF16)
        self.rowf = t("rowf", [1, 2048])
        self.srow = t("srow", [1, 64])
        self.dmsk = t("dmsk", [128, 2048], BF16)
        self.carry = t("carry", [128, 130])
        self.gbc = t("gbc", [128, 2 * D])
        self.tA = t("tA", [128, 1024])
        self.tB = t("tB", [128, 1024])
        self.tC = t("tC", [128, 1024])
        self.tDmp = t("tDmp", [128, 1024])
        self.bfA = t("bfA", [128, 4096], BF16)
        self.bfB = t("bfB", [128, 4096], BF16)
        self.wsm = t("wsm", [128, 1024], BF16)
        self.ARf = t("ARf", [128, 19456])
        self.ARb = t("ARb", [128, 29696], BF16)
        p = lambda nm, sh, dt=F32: self.enter(nc.psum_tensor(nm, list(sh), dt))
        self.psA = p("psA", [128, 2048])
        self.psB = p("psB", [128, 512])
        self.psC = p("psC", [128, 512])
        self.psTa = p("psTa", [128, 1024], BF16)
        self.psTb = p("psTb", [128, 1024], BF16)

        self.g = self._Rec(self, "g")
        self.tn = self._Rec(self, "tn")
        self.sc = self._Rec(self, "sc")
        self.ve = self._Rec(self, "ve")
        self.syp = self._SyP(self)

        self.emit(out_sh)

        blk = self.enter(nc.Block())
        self.replay_all(blk)
        self.close()
        return nc

    def barrier(self, idx):
        g, dr = self.g, self.dr
        self.wS(g)
        g.memset(self.srow[0:1, 32:40], 1.0)
        g.drain()
        g.dma_start(out=dr[f"cc_i{idx}"][0:1, 0:8],
                    in_=self.srow[0:1, 32:40]).then_inc(self.sLg, 16)
        self.inc("sLg", 16)
        g.wait_ge(self.sLg, self.S["sLg"])
        g.collective_compute(
            "AllReduce", ALU.add,
            replica_groups=[[0, 1], [2, 3], [4, 5], [6, 7]],
            ins=[dr[f"cc_i{idx}"]], outs=[dr[f"cc_o{idx}"]],
        ).then_inc(self.sB, 1)
        self.inc("sB")
        for e in [self.syp, self.g, self.tn, self.sc, self.ve]:
            e.wait_ge(self.sB, self.S["sB"])

    def mm_done(self):
        q = self.tn._q()
        q[-1][3].append((self.sT, 1))
        return self.inc("sT")

    def evac(self, out_ap, psum_ap, func=ACT.Copy, scale=1.0, bias=0.0,
             tcount=None):
        sc = self.sc
        if func == ACT.Copy:
            bias = float(bias)
            if isinstance(scale, (float, np.floating)):
                scale = float(scale)
        else:
            if isinstance(scale, (float, np.floating)):
                scale = np.float32(scale)
            if isinstance(bias, (float, np.floating)):
                bias = np.float32(bias)
        sc.wait_ge(self.sT, tcount if tcount is not None else self.S["sT"])
        sc.activation(out=out_ap, in_=psum_ap, func=func, scale=scale,
                      bias=bias)
        sc.drain().then_inc(self.sC, 1)
        return self.inc("sC")

    def ln_stats(self, x_ap, eps, n=1024):
        sc, ve = self.sc, self.ve
        mu = self.cols[:, 20:21]
        rc = self.cols[:, 21:22]
        tc = self.cols[:, 22:23]
        sc.wait_ge(self.sV, self.S["sV"])
        sc.activation(out=self.tDmp[:, 0:n], in_=x_ap, func=ACT.Copy,
                      accum_out=mu)
        sc.drain()
        sc.activation(out=self.tDmp[:, 0:n], in_=x_ap, func=ACT.Square,
                      accum_out=tc)
        sc.drain().then_inc(self.sC, 1)
        ve.wait_ge(self.sC, self.inc("sC"))
        ve.tensor_scalar(out=mu, in0=mu, scalar1=1.0 / n, scalar2=None,
                         op0=ALU.mult)
        ve.drain()
        ve.tensor_scalar(out=tc, in0=tc, scalar1=1.0 / n, scalar2=None,
                         op0=ALU.mult)
        ve.drain()
        ve.scalar_tensor_tensor(out=rc, in0=mu, scalar=-1.0, in1=mu,
                                op0=ALU.mult, op1=ALU.mult)
        ve.drain()
        ve.tensor_tensor(out=tc, in0=tc, in1=rc, op=ALU.add)
        ve.drain().then_inc(self.sV, 1)
        sc.wait_ge(self.sV, self.inc("sV"))
        sc.activation(out=rc, in_=tc, func=ACT.Sqrt, bias=np.float32(eps), scale=np.float32(1.0))
        sc.drain().then_inc(self.sC, 1)
        ve.wait_ge(self.sC, self.inc("sC"))
        ve.reciprocal(out=rc, in_=rc)
        ve.drain()
        return mu, rc

    def ln_norm(self, x_ap, out_ap, mu, rc):
        self.ve.tensor_scalar(out=out_ap, in0=x_ap, scalar1=mu, scalar2=rc,
                              op0=ALU.subtract, op1=ALU.mult)
        self.ve.drain()

    def bcast_f32(self, row_ap, out_ap, n):
        tn = self.tn
        tn.wait_ge(self.sC, self.S["sC"])
        tn.matmul(out=self.psB[:, 0:n], lhsT=self.onesfr[0:1, :],
                  rhs=row_ap, start=True, stop=True)
        self.evac(out_ap, self.psB[:, 0:n], tcount=self.mm_done())

    def load_gbc(self, grow, brow):
        for hh in range(2):
            self.bcast_f32(grow[0:1, hh * 512:(hh + 1) * 512],
                           self.gbc[:, hh * 512:(hh + 1) * 512], 512)
            self.bcast_f32(brow[0:1, hh * 512:(hh + 1) * 512],
                           self.gbc[:, D + hh * 512:D + (hh + 1) * 512], 512)

    def transpose_to(self, src_fn, nblk, dst_fn, f32=False):
        tn, sc = self.tn, self.sc
        ident = self.cstf[:, 0:128] if f32 else self.cst[:, 0:128]
        pos, bank = 0, 0
        grp = 4 if f32 else 8
        while pos < nblk:
            cnt = min(grp, nblk - pos)
            if f32:
                ps = self.psA[:, bank * 512:(bank + 1) * 512]
            else:
                ps = (self.psTa, self.psTb)[bank][:, 0:1024]
            tn.wait_ge(self.sC, self.S["sC"])
            for i in range(cnt):
                tn.transpose(out=ps[:, i * 128:(i + 1) * 128],
                             in_=src_fn(pos + i), identity=ident)
            tcnt = self.mm_done()
            sc.wait_ge(self.sT, tcnt)
            for i in range(cnt):
                sc.copy(out=dst_fn(pos + i), in_=ps[:, i * 128:(i + 1) * 128])
            sc.drain().then_inc(self.sC, 1)
            self.inc("sC")
            pos += cnt
            bank ^= 1

    def emit(self, out_sh):
        dr = self.dr
        sy, g, tn, sc, ve = self.syp, self.g, self.tn, self.sc, self.ve
        xres = self.ARf[:, 0:16384]

        xsv = dr["x_sh"].rearrange("(a p) m -> p a m", p=128)
        for gg in range(4):
            self.dma(sy, xres[:, gg * 4096:(gg + 1) * 4096]
                     .rearrange("p (a m) -> p a m", a=4),
                     xsv[:, gg * 4:(gg + 1) * 4, :], "sL")
        self.dma(sy, self.cstf[:, 0:128], dr["id_f32"], "sL")
        self.dma(sy, self.cstf[:, 128:129], dr["onesf_c"], "sL")
        self.dma(sy, self.onesfr[:, :], dr["onesf_r"], "sL")
        self.dma(sy, self.dmsk[:, :], dr["dmask"], "sL")
        for i, nm in enumerate(["osg0", "osb0", "osg1", "osb1"]):
            self.dma(sy, self.cols[:, i:i + 1], dr[nm], "sL")
        self.dma(sy, self.cols[:, 4:4 + NT], dr["invq2_c"], "sL")
        self.dma(sy, self.rowf[0:1, 0:D], dr["ln_g_r"], "sL")
        self.dma(sy, self.rowf[0:1, D:2 * D], dr["ln_b_r"], "sL")
        self.dma(g, self.cst[:, 0:128], dr["id_bf"], "sL")
        self.dma(g, self.onesbr[:, :], dr["onesb_r"], "sL")
        self.dma(g, self.wsm[:, 0:8 * QK].rearrange("p (c o) -> p c o", c=8),
                 dr["w_qk"].rearrange("(c p) o -> p c o", p=128), "sL")
        self.dma(g, self.rowb[0:1, 0:2 * HID], dr["b_hid_r"], "sL")
        self.dma(g, self.rowb[0:1, 4096:4096 + QK], dr["b_qk_r"], "sL")
        self.dma(g, self.rowb[0:1, 4224:4224 + D], dr["b_out0_r"], "sL")
        self.dma(g, self.rowb[0:1, 5504:5504 + D], dr["bo_r"], "sL")
        for e in [tn, sc, ve]:
            self.wL(e)
        if self.nphase >= 0:
            self.load_gbc(self.rowf[0:1, 0:D], self.rowf[0:1, D:2 * D])

        if self.nphase <= 0:
            return self.finish(out_sh, xres)
        self.phase_a1()
        if self.nphase == 1:
            return self.finish(out_sh, xres)
        self.barrier(0)
        self.phase_prep()
        self.barrier(1)
        self.phase_a2()
        if self.nphase == 2:
            return self.finish(out_sh, xres)
        self.phase_a3()
        if self.nphase == 3:
            return self.finish(out_sh, xres)
        self.phase_b()
        self.barrier(2)
        self.phase_c()
        self.barrier(3)
        if self.nphase == 4:
            return self.finish(out_sh, xres, "l_x2")
        self.phase_d()
        self.barrier(4)
        if self.nphase == 5:
            return self.finish(out_sh, xres, "l_x2")
        self.phase_e(out_sh)

    def finish(self, out_sh, xres, reload=None):
        sy = self.syp
        sy.wait_ge(self.sV, self.S["sV"])
        sy.wait_ge(self.sC, self.S["sC"])
        sy.wait_ge(self.sS, self.S["sS"])
        if reload:
            xi = self.dr[reload].rearrange("a p m -> p a m")
            for gg in range(4):
                self.dma(sy, xres[:, gg * 4096:(gg + 1) * 4096]
                         .rearrange("p (a m) -> p a m", a=4),
                         xi[:, gg * 4:(gg + 1) * 4, :], "sL")
            self.wL(sy)
        xo = out_sh.rearrange("(a p) m -> p a m", p=128)
        for gg in range(4):
            self.dma(sy, xo[:, gg * 4:(gg + 1) * 4, :],
                     xres[:, gg * 4096:(gg + 1) * 4096]
                     .rearrange("p (a m) -> p a m", a=4), "sS")
        sy.wait_ge(self.sS, self.S["sS"])

    def phase_a1(self):
        dr = self.dr
        sy, g, tn, sc, ve = self.syp, self.g, self.tn, self.sc, self.ve
        xres = self.ARf[:, 0:16384]
        naT = self.ARb[:, 0:16384]
        QT = self.ARb[:, 16384:18432]
        kTl = self.ARb[:, 18432:20480]
        wch = [self.ARb[:, 20480:24576], self.ARb[:, 24576:28672]]
        nabf = self.bfA[:, 1024:2048]
        qkbf = self.bfB[:, 2048:2176]
        qkT = self.bfB[:, 2176:2304]
        evb = self.bfB[:, 2304:2816]
        psB, psA = self.psB, self.psA

        for tt in range(NT):
            x_ap = xres[:, tt * 1024:(tt + 1) * 1024]
            xa = self.tA[:, 0:1024]
            self.wS(ve)
            mu, rc = self.ln_stats(x_ap, 1e-5)
            self.ln_norm(x_ap, xa, mu, rc)
            ve.drain().then_inc(self.sV, 1)
            sy.wait_ge(self.sV, self.inc("sV"))
            self.dma(sy, dr["l_xa"][tt], xa, "sS")
            mu, rc = self.ln_stats(xa, 1e-5)
            self.ln_norm(xa, self.tB[:, 0:1024], mu, rc)
            ve.tensor_tensor(out=self.tB[:, 0:1024], in0=self.tB[:, 0:1024],
                             in1=self.gbc[:, 0:1024], op=ALU.mult)
            ve.drain()
            ve.tensor_tensor(out=nabf, in0=self.tB[:, 0:1024],
                             in1=self.gbc[:, 1024:2048], op=ALU.add)
            ve.drain().then_inc(self.sV, 1)
            tn.wait_ge(self.sV, self.inc("sV"))
            self.transpose_to(lambda i: nabf[:, i * 128:(i + 1) * 128], 8,
                              lambda i: naT[:, tt * 1024 + i * 128:
                                            tt * 1024 + (i + 1) * 128])
            tn.wait_ge(self.sC, self.S["sC"])
            tn.matmul(out=psB[:, 0:QK], lhsT=self.onesbr[0:1, :],
                      rhs=self.rowb[0:1, 4096:4096 + QK], start=True, stop=False)
            for ic in range(8):
                tn.matmul(out=psB[:, 0:QK],
                          lhsT=naT[:, tt * 1024 + ic * 128:
                                    tt * 1024 + (ic + 1) * 128],
                          rhs=self.wsm[:, ic * QK:(ic + 1) * QK],
                          start=False, stop=(ic == 7))
            self.evac(qkbf, psB[:, 0:QK], func=ACT.Silu, tcount=self.mm_done())
            self.transpose_to(lambda i: qkbf, 1, lambda i: qkT)
            ve.wait_ge(self.sC, self.S["sC"])
            ve.tensor_scalar(out=QT[:, tt * 128:(tt + 1) * 128], in0=qkT,
                             scalar1=self.cols[:, 0:1], scalar2=self.cols[:, 1:2],
                             op0=ALU.mult, op1=ALU.add)
            ve.drain()
            ve.tensor_scalar(out=kTl[:, tt * 128:(tt + 1) * 128], in0=qkT,
                             scalar1=self.cols[:, 2:3], scalar2=self.cols[:, 3:4],
                             op0=ALU.mult, op1=ALU.add)
            ve.drain().then_inc(self.sV, 1)
            self.inc("sV")

        sy.wait_ge(self.sV, self.S["sV"])
        for bi in range(4):
            self.sydma2(lambda h, bi=bi: dr["s_kT"][:, QBLOCKS[h][bi] * 512:
                                                    QBLOCKS[h][bi] * 512 + 512],
                        kTl[:, bi * 512:(bi + 1) * 512], "sS")

        whid = dr["w_hid"].rearrange("(c p) o -> p c o", p=128)
        for oc in range(8):
            buf = wch[oc % 2]
            if oc >= 2:
                g.wait_ge(self.sT, self.S["sT"])
            self.dma(g, buf.rearrange("p (c o) -> p c o", c=8),
                     whid[:, :, oc * 512:(oc + 1) * 512], "sL")
            self.wL(tn)
            for tt in range(NT):
                bid = ("psA", tt % 4)
                pslot = psA[:, (tt % 4) * 512:((tt % 4) + 1) * 512]
                if bid in self.bank_ev:
                    tn.wait_ge(self.sC, self.bank_ev[bid])
                tn.matmul(out=pslot, lhsT=self.onesbr[0:1, :],
                          rhs=self.rowb[0:1, oc * 512:(oc + 1) * 512],
                          start=True, stop=False)
                for ic in range(8):
                    tn.matmul(out=pslot,
                              lhsT=naT[:, tt * 1024 + ic * 128:
                                        tt * 1024 + (ic + 1) * 128],
                              rhs=buf[:, ic * 512:(ic + 1) * 512],
                              start=False, stop=(ic == 7))
                tcnt = self.mm_done()
                self.wS(sc)
                self.bank_ev[bid] = self.evac(evb, pslot, func=ACT.Silu,
                                              tcount=tcnt)
                sy.wait_ge(self.sC, self.S["sC"])
                if oc < 4:
                    self.sydma2(lambda h, tt=tt, oc=oc:
                                dr["s_v"][tglob(tt, h):tglob(tt, h) + 128,
                                          oc * 512:(oc + 1) * 512],
                                evb, "sS")
                else:
                    self.dma(sy, dr["l_gate"][tt][:, (oc - 4) * 512:
                                                  (oc - 3) * 512], evb, "sS")

        acc, red = self.cols[:, 24:25], self.cols[:, 25:26]
        for wi, nm in enumerate(["wg_half", "wu_half", "wd_half"]):
            ve.wait_ge(self.sT, self.S["sT"])
            ve.memset(acc, 0.0)
            ve.drain()
            if wi < 2:
                wap = dr[nm].rearrange("(a p) m -> p a m", p=128)
                loads = [(wap[:, ti:ti + 1, :],) for ti in range(16)]
            else:
                loads = [(dr[nm][ti * 128:(ti + 1) * 128,
                                 q4 * 1024:(q4 + 1) * 1024],)
                         for ti in range(4) for q4 in range(4)]
            for (src_ap,) in loads:
                ve.drain().then_inc(self.sV, 1)
                sy.wait_ge(self.sV, self.inc("sV"))
                self.dma(sy, self.tA[:, 0:1024], src_ap, "sL")
                self.wL(ve)
                ve.tensor_reduce(out=red, in_=self.tA[:, 0:1024], axis=AX,
                                 op=ALU.add, apply_absolute_value=True)
                ve.drain()
                ve.tensor_tensor(out=acc, in0=acc, in1=red, op=ALU.add)
                ve.drain()
            ve.drain().then_inc(self.sV, 1)
            tn.wait_ge(self.sV, self.inc("sV"))
            tn.wait_ge(self.sC, self.S["sC"])
            tn.matmul(out=self.psB[0:1, 0:1], lhsT=acc,
                      rhs=self.cstf[:, 128:129], start=True, stop=True)
            self.evac(self.srow[0:1, wi:wi + 1], self.psB[0:1, 0:1],
                      tcount=self.mm_done())
        sy.wait_ge(self.sC, self.S["sC"])
        self.sydma2(lambda h: dr["s_wsum"][h:h + 1, 0:3],
                    self.srow[0:1, 0:3], "sS")

    def phase_prep(self):
        dr = self.dr
        sy, g, tn, sc, ve = self.syp, self.g, self.tn, self.sc, self.ve
        self.dma(g, self.srow[0:1, 8:11], dr["s_wsum"][0:1, 0:3], "sL")
        self.dma(g, self.srow[0:1, 16:19], dr["s_wsum"][1:2, 0:3], "sL")
        self.wL(ve)
        ve.tensor_tensor(out=self.srow[0:1, 12:15], in0=self.srow[0:1, 8:11],
                         in1=self.srow[0:1, 16:19], op=ALU.add)
        ve.drain()
        ve.tensor_scalar(out=self.srow[0:1, 12:15], in0=self.srow[0:1, 12:15],
                         scalar1=1.0 / (4096.0 * 1024.0), scalar2=1e-5,
                         op0=ALU.mult, op1=ALU.max)
        ve.drain()
        ve.reciprocal(out=self.srow[0:1, 20:23], in_=self.srow[0:1, 12:15])
        ve.drain().then_inc(self.sV, 1)
        tn.wait_ge(self.sV, self.inc("sV"))
        for j in range(3):
            self.bcast_f32(self.srow[0:1, 12 + j:13 + j],
                           self.cols[:, 26 + j:27 + j], 1)
            self.bcast_f32(self.srow[0:1, 20 + j:21 + j],
                           self.cols[:, 29 + j:30 + j], 1)
        qt = [self.ARb[:, j * 4096:(j + 1) * 4096] for j in range(4)]
        ev = self.ARb[:, 16384:16896]
        scratch = self.ARf[:, 16384:18432]
        for (nm, dst, O, I, swc) in [("wg_half", "s_wgT", 2048, 1024, 29),
                                     ("wu_half", "s_wuT", 2048, 1024, 30),
                                     ("wd_half", "s_wdT", 512, 4096, 31)]:
            for og in range(O // 512):
                for j4 in range(4):
                    nh = 1 if I == 1024 else 2
                    for hh in range(nh):
                        w = I // nh
                        src = scratch[:, 0:w]
                        ve.drain().then_inc(self.sV, 1)
                        sy.wait_ge(self.sV, self.inc("sV"))
                        self.dma(sy, src,
                                 dr[nm][og * 512 + j4 * 128:
                                        og * 512 + (j4 + 1) * 128,
                                        hh * w:(hh + 1) * w], "sL")
                        self.wL(ve)
                        ve.wait_ge(self.sT, self.S["sT"])
                        self._quant3(src, qt[j4][:, hh * w:(hh + 1) * w], swc)
                ve.drain().then_inc(self.sV, 1)
                tn.wait_ge(self.sV, self.inc("sV"))
                for ic in range(I // 128):
                    ps = (self.psTa, self.psTb)[ic % 2][:, 0:512]
                    tn.wait_ge(self.sC, self.S["sC"])
                    for j4 in range(4):
                        tn.transpose(out=ps[:, j4 * 128:(j4 + 1) * 128],
                                     in_=qt[j4][:, ic * 128:(ic + 1) * 128],
                                     identity=self.cst[:, 0:128])
                    tcnt = self.mm_done()
                    sc.wait_ge(self.sT, tcnt)
                    self.wS(sc)
                    sc.copy(out=ev, in_=ps[:, 0:512])
                    sc.drain().then_inc(self.sC, 1)
                    self.inc("sC")
                    sy.wait_ge(self.sC, self.S["sC"])
                    self.sydma2(
                        lambda h, dst=dst, ic=ic, og=og, O=O:
                        dr[dst][ic * 128:(ic + 1) * 128,
                                ((h * 2048 + og * 512) if O == 2048
                                 else h * 512):
                                ((h * 2048 + og * 512) if O == 2048
                                 else h * 512) + 512],
                        ev, "sS")

    def _quant3(self, src, dst_bf, swc):
        ve = self.ve
        ve.tensor_scalar(out=src, in0=src, scalar1=self.cols[:, swc:swc + 1],
                         scalar2=MAGIC, op0=ALU.mult, op1=ALU.add)
        ve.drain()
        ve.tensor_scalar(out=src, in0=src, scalar1=MAGIC, scalar2=1.0,
                         op0=ALU.subtract, op1=ALU.min)
        ve.drain()
        ve.tensor_scalar(out=dst_bf, in0=src, scalar1=-1.0, scalar2=None,
                         op0=ALU.max)
        ve.drain()

    def phase_a2(self):
        dr = self.dr
        sy, g, tn, sc, ve = self.syp, self.g, self.tn, self.sc, self.ve
        xres = self.ARf[:, 0:16384]
        ogf = self.ARf[:, 16384:18432]
        xat = self.ARf[:, 18432:19456]
        wout0 = self.ARb[:, 0:16384]
        QT = self.ARb[:, 16384:18432]
        kTf = self.ARb[:, 18432:22528]
        vsl = [self.ARb[:, 22528 + i * 2048:22528 + (i + 1) * 2048]
               for i in range(2)]
        ogT = self.ARb[:, 26624:28672]
        attn = self.ARb[:, 28672:29696]
        og = self.bfB[:, 0:2048]
        gatet = self.bfA[:, 0:2048]

        self.dma(g, wout0.rearrange("p (c o) -> p c o", c=16),
                 dr["w_out0"].rearrange("(c p) o -> p c o", p=128), "sL")
        self.dma(g, kTf, dr["s_kT"], "sL")
        for e in [tn, sc, ve, sy]:
            self.wL(e)

        for h in self.role_region([tn, sc, ve]):
            for tt in range(NT):
                gb = QBLOCKS[h][tt // 4]
                nk = gb * 4 + (tt % 4) + 1
                sy.wait_ge(self.sV, self.S["sV"])
                self.dma(sy, gatet, dr["l_gate"][tt], "sL")
                self.dma(sy, xat, dr["l_xa"][tt], "sL")
                first_av = True
                for rnd in range((nk + 1) // 2):
                    ks = [k for k in (2 * rnd, 2 * rnd + 1) if k < nk]
                    for kj in ks:
                        sy.wait_ge(self.sT, self.S["sT"])
                        self.dma(sy, vsl[kj % 2],
                                 dr["s_v"][kj * 128:(kj + 1) * 128, :], "sL")
                    self.wL(tn)
                    for kj in ks:
                        ps = (self.psB, self.psC)[kj % 2]
                        bid = ("ps", kj % 2)
                        if bid in self.bank_ev:
                            tn.wait_ge(self.sC, self.bank_ev[bid])
                        tn.matmul(out=ps[:, 0:128],
                                  lhsT=kTf[:, kj * 128:(kj + 1) * 128],
                                  rhs=QT[:, tt * 128:(tt + 1) * 128],
                                  start=True, stop=True)
                        tcnt = self.mm_done()
                        sc.wait_ge(self.sT, tcnt)
                        sc.activation(out=self.tA[:, 0:128], in_=ps[:, 0:128],
                                      func=ACT.Relu)
                        sc.drain()
                        sl = attn[:, (kj % 8) * 128:(kj % 8 + 1) * 128]
                        sc.activation(out=sl, in_=self.tA[:, 0:128],
                                      func=ACT.Square)
                        sc.drain().then_inc(self.sC, 1)
                        self.bank_ev[bid] = self.inc("sC")
                        if gb * 4 <= kj < gb * 4 + 4:
                            ve.wait_ge(self.sC, self.S["sC"])
                            dj = kj - gb * 4
                            ve.tensor_tensor(
                                out=sl, in0=sl,
                                in1=self.dmsk[:, dj * 512 + (tt % 4) * 128:
                                              dj * 512 + (tt % 4 + 1) * 128],
                                op=ALU.mult)
                            ve.drain().then_inc(self.sV, 1)
                            self.inc("sV")
                            tn.wait_ge(self.sV, self.S["sV"])
                        else:
                            tn.wait_ge(self.sC, self.S["sC"])
                    if first_av:
                        for q in range(4):
                            bid = ("psA", q)
                            if bid in self.bank_ev:
                                tn.wait_ge(self.sC, self.bank_ev[bid])
                        first_av = False
                    for kj in ks:
                        for oc in range(4):
                            tn.matmul(out=self.psA[:, oc * 512:(oc + 1) * 512],
                                      lhsT=attn[:, (kj % 8) * 128:
                                                (kj % 8 + 1) * 128],
                                      rhs=vsl[kj % 2][:, oc * 512:
                                                      (oc + 1) * 512],
                                      start=(kj == 0), stop=(kj == nk - 1))
                    tcnt = self.mm_done()
                c = self.evac(ogf, self.psA[:, 0:2048], func=ACT.Copy,
                              scale=self.cols[:, 4 + tt:5 + tt], tcount=tcnt)
                for q in range(4):
                    self.bank_ev[("psA", q)] = c
                ve.wait_ge(self.sC, c)
                self.wL(ve)
                ve.tensor_tensor(out=og, in0=ogf, in1=gatet, op=ALU.mult)
                ve.drain().then_inc(self.sV, 1)
                tn.wait_ge(self.sV, self.inc("sV"))
                self.transpose_to(lambda i: og[:, i * 128:(i + 1) * 128], 16,
                                  lambda i: ogT[:, i * 128:(i + 1) * 128])
                tn.wait_ge(self.sC, self.S["sC"])
                for half in range(2):
                    ps = (self.psB, self.psC)[half]
                    bid = ("ps", half)
                    if bid in self.bank_ev:
                        tn.wait_ge(self.sC, self.bank_ev[bid])
                    tn.matmul(out=ps[:, 0:512], lhsT=self.onesbr[0:1, :],
                              rhs=self.rowb[0:1, 4224 + half * 512:
                                            4224 + (half + 1) * 512],
                              start=True, stop=False)
                    for c16 in range(16):
                        tn.matmul(out=ps[:, 0:512],
                                  lhsT=ogT[:, c16 * 128:(c16 + 1) * 128],
                                  rhs=wout0[:, c16 * 1024 + half * 512:
                                            c16 * 1024 + (half + 1) * 512],
                                  start=False, stop=(c16 == 15))
                tcnt = self.mm_done()
                self.bank_ev[("ps", 0)] = self.evac(
                    self.tA[:, 0:512], self.psB[:, 0:512], tcount=tcnt)
                self.bank_ev[("ps", 1)] = self.evac(
                    self.tA[:, 512:1024], self.psC[:, 0:512], tcount=tcnt)
                ve.wait_ge(self.sC, self.S["sC"])
                ve.tensor_tensor(out=self.tA[:, 0:1024], in0=self.tA[:, 0:1024],
                                 in1=xat, op=ALU.add)
                ve.drain().then_inc(self.sV, 1)
                self.inc("sV")
                mu, rc = self.ln_stats(self.tA[:, 0:1024], 1e-5)
                self.ln_norm(self.tA[:, 0:1024], self.tB[:, 0:1024], mu, rc)
                ve.tensor_tensor(out=xres[:, tt * 1024:(tt + 1) * 1024],
                                 in0=xres[:, tt * 1024:(tt + 1) * 1024],
                                 in1=self.tB[:, 0:1024], op=ALU.add)
                ve.drain().then_inc(self.sV, 1)
                self.inc("sV")

    def phase_a3(self):
        dr = self.dr
        sy, g, tn, sc, ve = self.syp, self.g, self.tn, self.sc, self.ve
        g.wait_ge(self.sT, self.S["sT"])
        g.wait_ge(self.sV, self.S["sV"])
        xres = self.ARf[:, 0:16384]
        xqT = self.ARb[:, 0:16384]
        wgc = self.ARb[:, 16384:20480]
        wuc = self.ARb[:, 20480:24576]
        xqbf = self.ARb[:, 24576:25600]
        evg = self.ARb[:, 25600:26112]
        evu = self.ARb[:, 26112:26624]
        for tt in range(NT):
            x_ap = xres[:, tt * 1024:(tt + 1) * 1024]
            mu, rc = self.ln_stats(x_ap, 1e-5)
            self.ln_norm(x_ap, self.tA[:, 0:1024], mu, rc)
            ve.reduce_max(out=self.cols[:, 23:24], in_=self.tA[:, 0:1024],
                          axis=AX, apply_absolute_value=True)
            ve.drain()
            ve.tensor_scalar(out=self.cols[:, 23:24], in0=self.cols[:, 23:24],
                             scalar1=1e-5, scalar2=None, op0=ALU.max)
            ve.drain()
            ve.reciprocal(out=self.cols[:, 40:41], in_=self.cols[:, 23:24])
            ve.drain()
            ve.tensor_scalar(out=self.cols[:, 40:41], in0=self.cols[:, 40:41],
                             scalar1=127.0, scalar2=None, op0=ALU.mult)
            ve.drain()
            ve.tensor_scalar(out=self.cols[:, 41:42], in0=self.cols[:, 23:24],
                             scalar1=1.0 / 127.0, scalar2=None, op0=ALU.mult)
            ve.drain()
            ve.tensor_tensor(out=self.carry[:, tt:tt + 1],
                             in0=self.cols[:, 41:42], in1=self.cols[:, 26:27],
                             op=ALU.mult)
            ve.drain()
            ve.tensor_tensor(out=self.carry[:, 16 + tt:17 + tt],
                             in0=self.cols[:, 41:42], in1=self.cols[:, 27:28],
                             op=ALU.mult)
            ve.drain()
            ve.tensor_scalar(out=self.tA[:, 0:1024], in0=self.tA[:, 0:1024],
                             scalar1=self.cols[:, 40:41], scalar2=MAGIC,
                             op0=ALU.mult, op1=ALU.add)
            ve.drain()
            ve.tensor_scalar(out=xqbf, in0=self.tA[:, 0:1024], scalar1=MAGIC,
                             scalar2=None, op0=ALU.subtract)
            ve.drain().then_inc(self.sV, 1)
            tn.wait_ge(self.sV, self.inc("sV"))
            self.transpose_to(lambda i: xqbf[:, i * 128:(i + 1) * 128], 8,
                              lambda i: xqT[:, tt * 1024 + i * 128:
                                            tt * 1024 + (i + 1) * 128])

        wgT = dr["s_wgT"].rearrange("(c p) o -> p c o", p=128)
        wuT = dr["s_wuT"].rearrange("(c p) o -> p c o", p=128)
        for oc in range(8):
            if oc >= 1:
                g.wait_ge(self.sT, self.S["sT"])
            self.dma(g, wgc.rearrange("p (c o) -> p c o", c=8),
                     wgT[:, :, oc * 512:(oc + 1) * 512], "sL")
            self.dma(g, wuc.rearrange("p (c o) -> p c o", c=8),
                     wuT[:, :, oc * 512:(oc + 1) * 512], "sL")
            self.wL(tn)
            tn.wait_ge(self.sC, self.S["sC"])
            for tt in range(NT):
                pg = 2 * (tt % 2)
                for w2 in range(2):
                    bk = pg + w2
                    ps = self.psA[:, bk * 512:(bk + 1) * 512]
                    bid = ("psA", bk)
                    if bid in self.bank_ev:
                        tn.wait_ge(self.sC, self.bank_ev[bid])
                    wbuf = (wgc, wuc)[w2]
                    for ic in range(8):
                        tn.matmul(out=ps,
                                  lhsT=xqT[:, tt * 1024 + ic * 128:
                                           tt * 1024 + (ic + 1) * 128],
                                  rhs=wbuf[:, ic * 512:(ic + 1) * 512],
                                  start=(ic == 0), stop=(ic == 7))
                tcnt = self.mm_done()
                self.wS(sc)
                evg2 = (evg, self.bfA[:, 0:512])[tt % 2]
                evu2 = (evu, self.bfA[:, 512:1024])[tt % 2]
                self.bank_ev[("psA", pg)] = self.evac(
                    evg2, self.psA[:, pg * 512:(pg + 1) * 512], func=ACT.Silu,
                    scale=self.carry[:, tt:tt + 1], tcount=tcnt)
                self.bank_ev[("psA", pg + 1)] = self.evac(
                    evu2, self.psA[:, (pg + 1) * 512:(pg + 2) * 512],
                    func=ACT.Copy,
                    scale=self.carry[:, 16 + tt:17 + tt], tcount=tcnt)
                sy.wait_ge(self.sC, self.S["sC"])
                self.dma(sy, dr["l_glin"][tt][:, oc * 512:(oc + 1) * 512],
                         evg2, "sS")
                self.dma(sy, dr["l_ulin"][tt][:, oc * 512:(oc + 1) * 512],
                         evu2, "sS")

        g.wait_ge(self.sT, self.S["sT"])
        glt = self.ARb[:, 16384:20480]
        ult = self.ARb[:, 20480:24576]
        h2 = self.ARb[:, 24576:28672]
        xq2 = self.ARb[:, 16384:20480]
        for tt in range(NT):
            ve.drain().then_inc(self.sV, 1)
            sy.wait_ge(self.sV, self.inc("sV"))
            sy.wait_ge(self.sS, self.S["sS"])
            self.dma(sy, glt, dr["l_glin"][tt], "sL")
            self.dma(sy, ult, dr["l_ulin"][tt], "sL")
            self.wL(ve)
            ve.tensor_tensor(out=h2, in0=glt, in1=ult, op=ALU.mult)
            ve.drain()
            ve.reduce_max(out=self.cols[:, 23:24], in_=h2, axis=AX,
                          apply_absolute_value=True)
            ve.drain()
            ve.tensor_scalar(out=self.cols[:, 23:24], in0=self.cols[:, 23:24],
                             scalar1=1e-5, scalar2=None, op0=ALU.max)
            ve.drain()
            ve.reciprocal(out=self.cols[:, 40:41], in_=self.cols[:, 23:24])
            ve.drain()
            ve.tensor_scalar(out=self.cols[:, 40:41], in0=self.cols[:, 40:41],
                             scalar1=127.0, scalar2=None, op0=ALU.mult)
            ve.drain()
            ve.tensor_scalar(out=self.cols[:, 41:42], in0=self.cols[:, 23:24],
                             scalar1=1.0 / 127.0, scalar2=None, op0=ALU.mult)
            ve.drain()
            ve.tensor_tensor(out=self.carry[:, 32 + tt:33 + tt],
                             in0=self.cols[:, 41:42], in1=self.cols[:, 28:29],
                             op=ALU.mult)
            ve.drain()
            qtmp = self.ARf[:, 16384:18432]
            for qh in range(2):
                ve.tensor_scalar(out=qtmp, in0=h2[:, qh * 2048:(qh + 1) * 2048],
                                 scalar1=self.cols[:, 40:41],
                                 scalar2=MAGIC, op0=ALU.mult, op1=ALU.add)
                ve.drain()
                ve.tensor_scalar(out=xq2[:, qh * 2048:(qh + 1) * 2048],
                                 in0=qtmp, scalar1=MAGIC, scalar2=None,
                                 op0=ALU.subtract)
                ve.drain()
            ve.drain().then_inc(self.sV, 1)
            tn.wait_ge(self.sV, self.inc("sV"))
            self.transpose_to(lambda i: xq2[:, i * 128:(i + 1) * 128], 32,
                              lambda i: ult[:, i * 128:(i + 1) * 128])
            sy.wait_ge(self.sC, self.S["sC"])
            self.dma(sy, dr["l_xq2T"][tt], ult, "sS")

        wd = self.ARb[:, 0:16384]
        xq2s = [self.ARb[:, 16384 + i * 4096:16384 + (i + 1) * 4096]
                for i in range(2)]
        wdT = dr["s_wdT"].rearrange("(c p) o -> p c o", p=128)
        sy.wait_ge(self.sS, self.S["sS"])
        for rnd in range(2):
            g.wait_ge(self.sT, self.S["sT"])
            self.dma(g, wd.rearrange("p (c o) -> p c o", c=16),
                     wdT[:, 16 * rnd:16 * rnd + 16, :], "sL")
            self.wL(tn)
            for tt in range(NT):
                sy.wait_ge(self.sT, self.S["sT"])
                self.dma(sy, xq2s[tt % 2], dr["l_xq2T"][tt], "sL")
                self.wL(tn)
                for half in range(2):
                    ps = (self.psB, self.psC)[half]
                    if ("ps", half) in self.bank_ev:
                        tn.wait_ge(self.sC, self.bank_ev[("ps", half)])
                    for c16 in range(16):
                        cx = 16 * rnd + c16
                        tn.matmul(out=ps[:, 0:512],
                                  lhsT=xq2s[tt % 2][:, cx * 128:(cx + 1) * 128],
                                  rhs=wd[:, c16 * 1024 + half * 512:
                                         c16 * 1024 + (half + 1) * 512],
                                  start=(c16 == 0), stop=(c16 == 15))
                tcnt = self.mm_done()
                sc.wait_ge(self.sV, self.S["sV"])
                self.wS(sc)
                part = self.ARf[:, 16384 + (tt % 2) * 1024:
                                16384 + (tt % 2 + 1) * 1024]
                if rnd == 0:
                    self.bank_ev[("ps", 0)] = self.evac(
                        self.tB[:, 0:512], self.psB[:, 0:512],
                        scale=self.carry[:, 32 + tt:33 + tt], tcount=tcnt)
                    self.bank_ev[("ps", 1)] = self.evac(
                        self.tB[:, 512:1024], self.psC[:, 0:512],
                        scale=self.carry[:, 32 + tt:33 + tt], tcount=tcnt)
                    sy.wait_ge(self.sC, self.S["sC"])
                    self.dma(sy, dr["l_part"][tt], self.tB[:, 0:1024], "sS")
                else:
                    self.bank_ev[("ps", 0)] = self.evac(
                        self.tB[:, 0:512], self.psB[:, 0:512],
                        scale=self.carry[:, 32 + tt:33 + tt], tcount=tcnt)
                    self.bank_ev[("ps", 1)] = self.evac(
                        self.tB[:, 512:1024], self.psC[:, 0:512],
                        scale=self.carry[:, 32 + tt:33 + tt], tcount=tcnt)
                    sy.wait_ge(self.sC, self.S["sC"])
                    self.dma(sy, part, dr["l_part"][tt], "sL")
                    ve.wait_ge(self.sC, self.S["sC"])
                    self.wL(ve)
                    ve.tensor_tensor(out=self.tB[:, 0:1024], in0=part,
                                     in1=self.tB[:, 0:1024], op=ALU.add)
                    ve.drain().then_inc(self.sV, 1)
                    self.inc("sV")
                    mu, rc = self.ln_stats(self.tB[:, 0:1024], 1e-5)
                    self.ln_norm(self.tB[:, 0:1024], self.tC[:, 0:1024], mu, rc)
                    ve.tensor_tensor(out=xres[:, tt * 1024:(tt + 1) * 1024],
                                     in0=xres[:, tt * 1024:(tt + 1) * 1024],
                                     in1=self.tC[:, 0:1024], op=ALU.add)
                    ve.drain().then_inc(self.sV, 1)
                    self.inc("sV")

    def phase_b(self):
        dr = self.dr
        sy, g, tn, sc, ve = self.syp, self.g, self.tn, self.sc, self.ve
        g.wait_ge(self.sT, self.S["sT"])
        g.wait_ge(self.sV, self.S["sV"])
        g.wait_ge(self.sC, self.S["sC"])
        xres = self.ARf[:, 0:16384]
        xnT = self.ARb[:, 0:16384]
        wic = [self.ARb[:, 16384 + i * 1024:16384 + (i + 1) * 1024]
               for i in range(2)]
        xnbf = self.ARb[:, 18432:19456]
        self.dma(g, self.rowf[0:1, 0:D], dr["ln1_g_r"], "sL")
        self.dma(g, self.rowf[0:1, D:2 * D], dr["ln1_b_r"], "sL")
        for e in [tn, sc, ve]:
            self.wL(e)
        self.load_gbc(self.rowf[0:1, 0:D], self.rowf[0:1, D:2 * D])
        ve.memset(self.tA[:, 0:3], 0.0)
        ve.drain().then_inc(self.sV, 1)
        sy.wait_ge(self.sV, self.inc("sV"))
        for dblk in range(16):
            self.sydma2(lambda h, dblk=dblk:
                        (dr["s_u"][dblk * 128:(dblk + 1) * 128, 0:3] if h == 0
                         else dr["l_part"][0][0:128, dblk * 3:dblk * 3 + 3]),
                        self.tA[:, 0:3], "sS")

        for tt in range(NT):
            x_ap = xres[:, tt * 1024:(tt + 1) * 1024]
            self.wS(ve)
            mu, rc = self.ln_stats(x_ap, 1e-5)
            self.ln_norm(x_ap, self.tA[:, 0:1024], mu, rc)
            ve.drain().then_inc(self.sV, 1)
            sy.wait_ge(self.sV, self.inc("sV"))
            self.dma(sy, dr["l_xa2"][tt], self.tA[:, 0:1024], "sS")
            self.dma(sy, dr["l_x2"][tt], x_ap, "sS")
            mu, rc = self.ln_stats(self.tA[:, 0:1024], 1e-5)
            self.ln_norm(self.tA[:, 0:1024], self.tB[:, 0:1024], mu, rc)
            ve.tensor_tensor(out=self.tB[:, 0:1024], in0=self.tB[:, 0:1024],
                             in1=self.gbc[:, 0:1024], op=ALU.mult)
            ve.drain()
            ve.tensor_tensor(out=xnbf, in0=self.tB[:, 0:1024],
                             in1=self.gbc[:, 1024:2048], op=ALU.add)
            ve.drain().then_inc(self.sV, 1)
            tn.wait_ge(self.sV, self.inc("sV"))
            self.transpose_to(lambda i: xnbf[:, i * 128:(i + 1) * 128], 8,
                              lambda i: xnT[:, i * 2048 + tt * 128:
                                            i * 2048 + (tt + 1) * 128])

        winv = dr["w_in"].rearrange("(c p) o -> p c o", p=128)
        for oc in range(32):
            buf = wic[oc % 2]
            if oc >= 2:
                g.wait_ge(self.sT, self.S["sT"])
            self.dma(g, buf.rearrange("p (c o) -> p c o", c=8),
                     winv[:, :, oc * 128:(oc + 1) * 128], "sL")
            self.wL(tn)
            tn.wait_ge(self.sC, self.S["sC"])
            for tc in range(4):
                ps = (self.psB, self.psC)[tc % 2]
                bid = ("ps", tc % 2)
                if bid in self.bank_ev:
                    tn.wait_ge(self.sC, self.bank_ev[bid])
                for ic in range(8):
                    tn.matmul(out=ps[:, 0:512],
                              lhsT=buf[:, ic * 128:(ic + 1) * 128],
                              rhs=xnT[:, ic * 2048 + tc * 512:
                                      ic * 2048 + (tc + 1) * 512],
                              start=(ic == 0), stop=(ic == 7))
                tcnt = self.mm_done()
                self.wS(sc)
                self.bank_ev[bid] = self.evac(
                    self.tA[:, 0:512] if tc % 2 == 0 else self.tB[:, 0:512],
                    ps[:, 0:512], tcount=tcnt)
                src = self.tA[:, 0:512] if tc % 2 == 0 else self.tB[:, 0:512]
                sy.wait_ge(self.sC, self.S["sC"])
                if oc < 16:
                    self.sydma2(lambda h, oc=oc, tc=tc:
                                dr["s_u"][oc * 128:(oc + 1) * 128,
                                          3 + QBLOCKS[h][tc] * 512:
                                          3 + QBLOCKS[h][tc] * 512 + 512],
                                src, "sS")
                else:
                    self.sydma2(lambda h, oc=oc, tc=tc:
                                dr["s_z"][(oc - 16) * 128:(oc - 15) * 128,
                                          QBLOCKS[h][tc] * 512:
                                          QBLOCKS[h][tc] * 512 + 512],
                                src, "sS")

    def phase_c(self):
        dr = self.dr
        sy, g, tn, sc, ve = self.syp, self.g, self.tn, self.sc, self.ve
        g.wait_ge(self.sT, self.S["sT"])
        g.wait_ge(self.sV, self.S["sV"])
        g.wait_ge(self.sC, self.S["sC"])
        wxp = self.ARb[:, 0:1536]
        wdt = self.ARb[0:DTR, 1536:3584]
        ucb = self.ARb[:, 4096:12288]
        dtb = self.ARb[0:DTR, 3584:4096]
        ublk = self.ARf[:, 0:8320]
        self.dma(g, self.tC[:, 0:64], dr["convw"], "sL")
        self.dma(g, self.tC[:, 64:80], dr["convb"], "sL")
        self.dma(g, self.tC[:, 80:96], dr["bdt_c"], "sL")
        self.dma(g, wxp.rearrange("p (c o) -> p c o", c=16),
                 dr["w_xp"].rearrange("(c p) o -> p c o", p=128), "sL")
        self.dma(g, wdt, dr["w_dt"], "sL")
        for e in [tn, sc, ve, sy]:
            self.wL(e)

        for tc in range(4):
            sy.wait_ge(self.sV, self.S["sV"])
            for dblk in range(16):
                self.sydma2(ublk[:, dblk * 520:dblk * 520 + 515],
                            lambda h, dblk=dblk, tc=tc:
                            dr["s_u"][dblk * 128:(dblk + 1) * 128,
                                      QBLOCKS[h][tc] * 512:
                                      QBLOCKS[h][tc] * 512 + 515], "sL")
            self.wL(ve)
            for dblk in range(16):
                u0 = ublk[:, dblk * 520:dblk * 520 + 515]
                self.wS(ve)
                ve.tensor_scalar(out=self.tA[:, 0:512], in0=u0[:, 0:512],
                                 scalar1=self.tC[:, dblk * 4:dblk * 4 + 1],
                                 scalar2=None, op0=ALU.mult)
                ve.drain()
                for k in range(1, 4):
                    ve.scalar_tensor_tensor(
                        out=self.tA[:, 0:512], in0=u0[:, k:k + 512],
                        scalar=self.tC[:, dblk * 4 + k:dblk * 4 + k + 1],
                        in1=self.tA[:, 0:512], op0=ALU.mult, op1=ALU.add)
                    ve.drain()
                ve.drain().then_inc(self.sV, 1)
                sc.wait_ge(self.sV, self.inc("sV"))
                self.wS(sc)
                sc.activation(out=self.tB[:, 0:512], in_=self.tA[:, 0:512],
                              func=ACT.Silu, bias=self.tC[:, 64 + dblk:65 + dblk],
                              scale=np.float32(1.0))
                sc.drain().then_inc(self.sC, 1)
                ve.wait_ge(self.sC, self.inc("sC"))
                ve.tensor_copy(out=ucb[:, dblk * 512:(dblk + 1) * 512],
                               in_=self.tB[:, 0:512])
                ve.drain().then_inc(self.sV, 1)
                self.inc("sV")
                sy.wait_ge(self.sC, self.S["sC"])
                self.sydma2(lambda h, dblk=dblk, tc=tc:
                            dr["s_uc"][dblk * 128:(dblk + 1) * 128,
                                       QBLOCKS[h][tc] * 512:
                                       QBLOCKS[h][tc] * 512 + 512],
                            self.tB[:, 0:512], "sS")
            ve.drain().then_inc(self.sV, 1)
            tn.wait_ge(self.sV, self.inc("sV"))
            bid = ("ps", 0)
            if bid in self.bank_ev:
                tn.wait_ge(self.sC, self.bank_ev[bid])
            for dblk in range(16):
                tn.matmul(out=self.psB[0:96, 0:512],
                          lhsT=wxp[:, dblk * 96:(dblk + 1) * 96],
                          rhs=ucb[:, dblk * 512:(dblk + 1) * 512],
                          start=(dblk == 0), stop=(dblk == 15))
            tcnt = self.mm_done()
            sc.wait_ge(self.sT, tcnt)
            sc.copy(out=dtb, in_=self.psB[0:DTR, 0:512])
            sc.drain()
            sc.copy(out=self.tA[0:32, 0:512], in_=self.psB[64:96, 0:512])
            sc.drain().then_inc(self.sC, 1)
            self.bank_ev[bid] = self.inc("sC")
            sy.wait_ge(self.sC, self.S["sC"])
            self.sydma2(lambda h, tc=tc:
                        dr["s_bc"][0:32, QBLOCKS[h][tc] * 512:
                                   QBLOCKS[h][tc] * 512 + 512],
                        self.tA[0:32, 0:512], "sS")
            for dblk in range(16):
                ps = (self.psB, self.psC)[dblk % 2]
                bid = ("ps", dblk % 2)
                if bid in self.bank_ev:
                    tn.wait_ge(self.sC, self.bank_ev[bid])
                tn.matmul(out=ps[:, 0:512],
                          lhsT=wdt[:, dblk * 128:(dblk + 1) * 128],
                          rhs=dtb, start=True, stop=True)
                tcnt = self.mm_done()
                sc.wait_ge(self.sT, tcnt)
                self.wS(sc)
                bcol = self.tC[:, 80 + dblk:81 + dblk]
                sc.activation(out=self.tA[:, 0:512], in_=ps[:, 0:512],
                              func=ACT.Relu, bias=bcol, scale=np.float32(1.0))
                sc.drain()
                sc.activation(out=self.tB[:, 0:512], in_=ps[:, 0:512],
                              func=ACT.Abs, bias=bcol, scale=np.float32(1.0))
                sc.drain().then_inc(self.sC, 1)
                self.bank_ev[bid] = self.inc("sC")
                sc.activation(out=self.tB[:, 0:512], in_=self.tB[:, 0:512],
                              func=ACT.Exp, scale=np.float32(-1.0))
                sc.drain()
                sc.activation(out=self.tB[:, 0:512], in_=self.tB[:, 0:512],
                              func=ACT.Ln, bias=np.float32(1.0), scale=np.float32(1.0))
                sc.drain().then_inc(self.sC, 1)
                ve.wait_ge(self.sC, self.inc("sC"))
                self.wS(ve)
                ve.tensor_tensor(out=self.tA[:, 0:512], in0=self.tA[:, 0:512],
                                 in1=self.tB[:, 0:512], op=ALU.add)
                ve.drain().then_inc(self.sV, 1)
                self.inc("sV")
                sy.wait_ge(self.sV, self.S["sV"])
                self.sydma2(lambda h, dblk=dblk, tc=tc:
                            dr["s_dlt"][dblk * 128:(dblk + 1) * 128,
                                        QBLOCKS[h][tc] * 512:
                                        QBLOCKS[h][tc] * 512 + 512],
                            self.tA[:, 0:512], "sS")

    def phase_d(self):
        dr = self.dr
        sy, g, tn, sc, ve = self.syp, self.g, self.tn, self.sc, self.ve
        g.wait_ge(self.sT, self.S["sT"])
        g.wait_ge(self.sV, self.S["sV"])
        g.wait_ge(self.sC, self.S["sC"])
        dl = self.ARf[:, 0:1024]; uc = self.ARf[:, 1024:2048]
        du = self.ARf[:, 2048:3072]
        aa = [self.ARf[:, 3072:4096], self.ARf[:, 4096:5120]]
        hh_ = self.ARf[:, 5120:6144]; bb = self.ARf[:, 6144:7168]
        zz = self.ARf[:, 7168:8192]
        yall = self.ARf[:, 8192:16384]
        bcB = self.ARb[:, 0:8192]
        bcC = self.ARb[:, 8192:16384]
        gbf = self.bfA[:, 0:1024]
        self.dma(g, self.tC[:, 0:128], dr["alog_c"], "sL")
        self.dma(g, self.tC[:, 128:136], dr["dskip_c"], "sL")
        for e in [tn, sc, ve, sy]:
            self.wL(e)
        sc.activation(out=self.tB[:, 0:128], in_=self.tC[:, 0:128],
                      func=ACT.Exp)
        sc.drain().then_inc(self.sC, 1)
        ve.wait_ge(self.sC, self.inc("sC"))
        ve.tensor_scalar(out=self.tC[:, 0:128], in0=self.tB[:, 0:128],
                         scalar1=-1.0, scalar2=None, op0=ALU.mult)
        ve.drain()
        ve.memset(self.carry[:, 0:128], 0.0)
        ve.drain().then_inc(self.sV, 1)
        self.inc("sV")

        for tc in range(4):
            for nh in range(2):
                sc.wait_ge(self.sV, self.S["sV"])
                for n8 in range(8):
                    n = nh * 8 + n8
                    g.wait_ge(self.sT, self.S["sT"])
                    self.dma(g, self.rowf[0:1, 0:1024],
                             dr["s_bc"][n:n + 1, tc * 1024:(tc + 1) * 1024],
                             "sL")
                    self.dma(g, self.rowf[0:1, 1024:2048],
                             dr["s_bc"][16 + n:17 + n,
                                        tc * 1024:(tc + 1) * 1024], "sL")
                    self.wL(tn)
                    for q in range(4):
                        bid = ("psA", q)
                        if bid in self.bank_ev:
                            tn.wait_ge(self.sC, self.bank_ev[bid])
                    for half in range(2):
                        tn.matmul(out=self.psA[:, half * 512:(half + 1) * 512],
                                  lhsT=self.onesfr[0:1, :],
                                  rhs=self.rowf[0:1, half * 512:
                                                (half + 1) * 512],
                                  start=True, stop=True)
                        tn.matmul(out=self.psA[:, 1024 + half * 512:
                                               1024 + (half + 1) * 512],
                                  lhsT=self.onesfr[0:1, :],
                                  rhs=self.rowf[0:1, 1024 + half * 512:
                                                1024 + (half + 1) * 512],
                                  start=True, stop=True)
                    tcnt = self.mm_done()
                    self.evac(bcB[:, n8 * 1024:(n8 + 1) * 1024],
                              self.psA[:, 0:1024], tcount=tcnt)
                    c = self.evac(bcC[:, n8 * 1024:(n8 + 1) * 1024],
                                  self.psA[:, 1024:2048], tcount=tcnt)
                    for q in range(4):
                        self.bank_ev[("psA", q)] = c
                for dblk in range(8):
                    sy.wait_ge(self.sV, self.S["sV"])
                    self.sydma2(dl, lambda h, dblk=dblk, tc=tc:
                                dr["s_dlt"][h * 1024 + dblk * 128:
                                            h * 1024 + dblk * 128 + 128,
                                            tc * 1024:(tc + 1) * 1024], "sL")
                    self.sydma2(uc, lambda h, dblk=dblk, tc=tc:
                                dr["s_uc"][h * 1024 + dblk * 128:
                                           h * 1024 + dblk * 128 + 128,
                                           tc * 1024:(tc + 1) * 1024], "sL")
                    if nh == 1:
                        self.sydma2(zz, lambda h, dblk=dblk, tc=tc:
                                    dr["s_z"][h * 1024 + dblk * 128:
                                              h * 1024 + dblk * 128 + 128,
                                              tc * 1024:(tc + 1) * 1024], "sL")
                    self.wL(ve)
                    ve.tensor_tensor(out=du, in0=dl, in1=uc, op=ALU.mult)
                    ve.drain()
                    if nh == 0:
                        ve.tensor_scalar(out=yall[:, dblk * 1024:
                                                  (dblk + 1) * 1024],
                                         in0=uc,
                                         scalar1=self.tC[:, 128 + dblk:
                                                         129 + dblk],
                                         scalar2=None, op0=ALU.mult)
                        ve.drain()
                    ve.drain().then_inc(self.sV, 1)
                    self.inc("sV")
                    vbase = self.S["sV"]
                    sc.wait_ge(self.sV, vbase)
                    for n8 in range(8):
                        n = nh * 8 + n8
                        if n8 >= 2:
                            sc.wait_ge(self.sV, vbase + n8 - 1)
                        sc.activation(out=aa[n8 % 2], in_=dl, func=ACT.Exp,
                                      scale=self.tC[:, dblk * 16 + n:
                                                    dblk * 16 + n + 1])
                        sc.drain().then_inc(self.sC, 1)
                        ve.wait_ge(self.sC, self.inc("sC"))
                        ve.tensor_tensor(out=bb, in0=du,
                                         in1=bcB[:, n8 * 1024:(n8 + 1) * 1024],
                                         op=ALU.mult)
                        ve.drain()
                        ve.tensor_tensor_scan(
                            out=hh_, data0=aa[n8 % 2], data1=bb,
                            initial=self.carry[:, dblk * 16 + n:
                                               dblk * 16 + n + 1],
                            op0=ALU.mult, op1=ALU.add)
                        ve.drain()
                        ve.tensor_copy(out=self.carry[:, dblk * 16 + n:
                                                      dblk * 16 + n + 1],
                                       in_=hh_[:, 1023:1024])
                        ve.drain()
                        ve.tensor_tensor(out=bb, in0=hh_,
                                         in1=bcC[:, n8 * 1024:(n8 + 1) * 1024],
                                         op=ALU.mult)
                        ve.drain()
                        ve.tensor_tensor(out=yall[:, dblk * 1024:
                                                  (dblk + 1) * 1024],
                                         in0=yall[:, dblk * 1024:
                                                  (dblk + 1) * 1024],
                                         in1=bb, op=ALU.add)
                        ve.drain().then_inc(self.sV, 1)
                        self.inc("sV")
                    if nh == 1:
                        sc.wait_ge(self.sV, self.S["sV"])
                        self.wL(sc)
                        sc.activation(out=zz, in_=zz, func=ACT.Silu)
                        sc.drain().then_inc(self.sC, 1)
                        ve.wait_ge(self.sC, self.inc("sC"))
                        self.wS(ve)
                        ve.tensor_tensor(out=gbf,
                                         in0=yall[:, dblk * 1024:
                                                  (dblk + 1) * 1024],
                                         in1=zz, op=ALU.mult)
                        ve.drain().then_inc(self.sV, 1)
                        self.inc("sV")
                        sy.wait_ge(self.sV, self.S["sV"])
                        self.sydma2(lambda h, dblk=dblk, tc=tc:
                                    dr["s_g"][h * 1024 + dblk * 128:
                                              h * 1024 + dblk * 128 + 128,
                                              tc * 1024:(tc + 1) * 1024],
                                    gbf, "sS")

    def phase_e(self, out_sh):
        dr = self.dr
        sy, g, tn, sc, ve = self.syp, self.g, self.tn, self.sc, self.ve
        g.wait_ge(self.sT, self.S["sT"])
        g.wait_ge(self.sV, self.S["sV"])
        g.wait_ge(self.sC, self.S["sC"])
        gT = self.ARb[:, 0:8192]
        wout1 = self.ARb[:, 8192:24576]
        hsT = self.ARb[:, 8192:24576]
        hsbf = self.bfA[:, 0:1024]
        wrt = self.ARf[:, 15360:15392]
        hsfT = self.ARf[:, 15488:16512]
        colsM = self.ARf[:, 16512:16576]
        xt2 = self.ARf[:, 16576:17600]
        self.dma(g, wout1.rearrange("p (c o) -> p c o", c=16),
                 dr["w_out1"].rearrange("(c p) o -> p c o", p=128), "sL")
        self.dma(g, wrt.rearrange("p (c o) -> p c o", c=8),
                 dr["w_rt"].rearrange("(c p) o -> p c o", p=128), "sL")
        for e in [tn, sc, ve, sy]:
            self.wL(e)

        for qg in range(4):
            sy.wait_ge(self.sT, self.S["sT"])
            self.sydma2(gT.rearrange("p (c t) -> p c t", c=16),
                        lambda h, qg=qg:
                        dr["s_g"].rearrange("(c p) t -> p c t", p=128)
                        [:, :, QBLOCKS[h][qg] * 512:QBLOCKS[h][qg] * 512 + 512],
                        "sL")
            self.wL(tn)
            for tq in range(4):
                tt = qg * 4 + tq
                for half in range(2):
                    ps = (self.psB, self.psC)[half]
                    bid = ("ps", half)
                    if bid in self.bank_ev:
                        tn.wait_ge(self.sC, self.bank_ev[bid])
                    for c16 in range(16):
                        tn.matmul(out=ps[:, 0:512],
                                  lhsT=gT[:, c16 * 512 + tq * 128:
                                          c16 * 512 + (tq + 1) * 128],
                                  rhs=wout1[:, c16 * 1024 + half * 512:
                                            c16 * 1024 + (half + 1) * 512],
                                  start=(c16 == 0), stop=(c16 == 15))
                tcnt = self.mm_done()
                sc.wait_ge(self.sV, self.S["sV"])
                self.bank_ev[("ps", 0)] = self.evac(
                    self.tA[:, 0:512], self.psB[:, 0:512], tcount=tcnt)
                self.bank_ev[("ps", 1)] = self.evac(
                    self.tA[:, 512:1024], self.psC[:, 0:512], tcount=tcnt)
                sy.wait_ge(self.sV, self.S["sV"])
                sy.wait_ge(self.sS, self.S["sS"])
                self.dma(sy, self.tB[:, 0:1024], dr["l_xa2"][tt], "sL")
                self.dma(sy, xt2, dr["l_x2"][tt], "sL")
                ve.wait_ge(self.sC, self.S["sC"])
                self.wL(ve)
                self.wS(ve)
                ve.tensor_tensor(out=self.tA[:, 0:1024], in0=self.tA[:, 0:1024],
                                 in1=self.tB[:, 0:1024], op=ALU.add)
                ve.drain().then_inc(self.sV, 1)
                self.inc("sV")
                mu, rc = self.ln_stats(self.tA[:, 0:1024], 1e-5)
                self.ln_norm(self.tA[:, 0:1024], self.tB[:, 0:1024], mu, rc)
                ve.tensor_tensor(out=xt2, in0=xt2, in1=self.tB[:, 0:1024],
                                 op=ALU.add)
                ve.drain().then_inc(self.sV, 1)
                sy.wait_ge(self.sV, self.inc("sV"))
                self.dma(sy, dr["l_x3"][tt], xt2, "sS")
                mu, rc = self.ln_stats(xt2, 1e-5)
                self.ln_norm(xt2, self.tB[:, 0:1024], mu, rc)
                ve.drain().then_inc(self.sV, 1)
                sy.wait_ge(self.sV, self.inc("sV"))
                self.dma(sy, dr["l_hs"][tt], self.tB[:, 0:1024], "sS")
                ve.drain().then_inc(self.sV, 1)
                tn.wait_ge(self.sV, self.inc("sV"))
                self.transpose_to(lambda i: self.tB[:, i * 128:(i + 1) * 128],
                                  8, lambda i: hsfT[:, i * 128:(i + 1) * 128],
                                  f32=True)
                tn.wait_ge(self.sC, self.S["sC"])
                for ic in range(8):
                    tn.matmul(out=self.psB[:, 0:NE],
                              lhsT=hsfT[:, ic * 128:(ic + 1) * 128],
                              rhs=wrt[:, ic * 4:(ic + 1) * 4],
                              start=(ic == 0), stop=(ic == 7))
                tcnt = self.mm_done()
                self.bank_ev[("ps", 0)] = self.evac(
                    self.tA[:, 0:NE], self.psB[:, 0:NE], tcount=tcnt)
                ve.wait_ge(self.sC, self.S["sC"])
                ve.tensor_reduce(out=self.cols[:, 23:24], in_=self.tA[:, 0:NE],
                                 axis=AX, op=ALU.max)
                ve.drain()
                ve.memset(self.cols[:, 24:25], 1.0)
                ve.drain()
                for ee in range(NE):
                    ve.tensor_scalar(out=self.cols[:, 25:26],
                                     in0=self.tA[:, ee:ee + 1],
                                     scalar1=self.cols[:, 23:24], scalar2=None,
                                     op0=ALU.is_ge)
                    ve.drain()
                    ve.tensor_tensor(out=colsM[:, tt * 4 + ee:tt * 4 + ee + 1],
                                     in0=self.cols[:, 25:26],
                                     in1=self.cols[:, 24:25], op=ALU.mult)
                    ve.drain()
                    if ee < NE - 1:
                        ve.tensor_scalar(out=self.cols[:, 25:26],
                                         in0=colsM[:, tt * 4 + ee:
                                                   tt * 4 + ee + 1],
                                         scalar1=-1.0, scalar2=1.0,
                                         op0=ALU.mult, op1=ALU.add)
                        ve.drain()
                        ve.tensor_tensor(out=self.cols[:, 24:25],
                                         in0=self.cols[:, 24:25],
                                         in1=self.cols[:, 25:26], op=ALU.mult)
                        ve.drain()
                ve.drain().then_inc(self.sV, 1)
                self.inc("sV")

        for tt in range(NT):
            ve.drain().then_inc(self.sV, 1)
            sy.wait_ge(self.sV, self.inc("sV"))
            sy.wait_ge(self.sS, self.S["sS"])
            self.dma(sy, self.tB[:, 0:1024], dr["l_hs"][tt], "sL")
            self.wL(ve)
            ve.wait_ge(self.sT, self.S["sT"])
            ve.tensor_copy(out=hsbf, in_=self.tB[:, 0:1024])
            ve.drain().then_inc(self.sV, 1)
            tn.wait_ge(self.sV, self.inc("sV"))
            self.transpose_to(lambda i: hsbf[:, i * 128:(i + 1) * 128], 8,
                              lambda i: hsT[:, tt * 1024 + i * 128:
                                            tt * 1024 + (i + 1) * 128])

        def hsT_ap(tt, ic):
            return hsT[:, tt * 1024 + ic * 128:tt * 1024 + (ic + 1) * 128]

        g.wait_ge(self.sT, self.S["sT"])
        g.wait_ge(self.sC, self.S["sC"])
        self._vq = []
        w1b = [self.ARb[:, i * 2048:(i + 1) * 2048] for i in range(2)]
        intr = self.ARb[:, 4096:8192]
        w1v = dr["w1"].rearrange("e (c p) o -> e p c o", p=128)
        for oc in range(16):
            self.wS(ve)
            for ee in range(NE):
                bslot = 6528 + ee * 256
                self.dma(g, self.rowb[0:1, bslot:bslot + 256],
                         dr["b1_r"][ee:ee + 1, oc * 256:(oc + 1) * 256], "sL")
                buf = w1b[(oc * NE + ee) % 2]
                if oc * NE + ee >= 2:
                    g.wait_ge(self.sT, self.S["sT"])
                self.dma(g, buf.rearrange("p (c o) -> p c o", c=8),
                         w1v[ee][:, :, oc * 256:(oc + 1) * 256], "sL")
                self.wL(tn)
                tn.wait_ge(self.sC, self.S["sC"])
                for tt in range(NT):
                    bk = tt % 4
                    ps = self.psA[:, bk * 512:bk * 512 + 256]
                    bid = ("psA", bk)
                    if bid in self.bank_ev:
                        tn.wait_ge(self.sC, self.bank_ev[bid])
                    tn.matmul(out=ps, lhsT=self.onesbr[0:1, :],
                              rhs=self.rowb[0:1, bslot:bslot + 256],
                              start=True, stop=False)
                    for ic in range(8):
                        tn.matmul(out=ps, lhsT=hsT_ap(tt, ic),
                                  rhs=buf[:, ic * 256:(ic + 1) * 256],
                                  start=False, stop=(ic == 7))
                    tcnt = self.mm_done()
                    dstf = (self.tA, self.tB, self.tC, self.tDmp)[bk][:, 0:256]
                    if len(self._vq) >= 4:
                        sc.wait_ge(self.sV, self._vq[-4])
                    self.bank_ev[bid] = self.evac(dstf, ps,
                                                  func=ACT.Gelu, tcount=tcnt)
                    ve.wait_ge(self.sC, self.S["sC"])
                    mc = colsM[:, tt * 4 + ee:tt * 4 + ee + 1]
                    if ee == 0:
                        ve.tensor_scalar(out=intr[:, tt * 256:(tt + 1) * 256],
                                         in0=dstf, scalar1=mc, scalar2=None,
                                         op0=ALU.mult)
                    else:
                        ve.scalar_tensor_tensor(
                            out=intr[:, tt * 256:(tt + 1) * 256], in0=dstf,
                            scalar=mc, in1=intr[:, tt * 256:(tt + 1) * 256],
                            op0=ALU.mult, op1=ALU.add)
                    ve.drain().then_inc(self.sV, 1)
                    self._vq.append(self.inc("sV"))
            sy.wait_ge(self.sV, self.S["sV"])
            for tt in range(NT):
                self.dma(sy, dr["l_inter"][oc][tt],
                         intr[:, tt * 256:(tt + 1) * 256], "sS")

        g.wait_ge(self.sT, self.S["sT"])
        g.wait_ge(self.sV, self.S["sV"])
        self.wS(g)
        wo = self.ARb[:, 0:16384]
        intT = self.ARb[:, 24576:28672]
        intf = self.bfB[:, 0:4096]
        part = self.ARf[:, 0:16384]
        self.dma(g, self.rowf[0:1, 0:D], dr["ln2_g_r"], "sL")
        self.dma(g, self.rowf[0:1, D:2 * D], dr["ln2_b_r"], "sL")
        for e in [tn, sc, ve]:
            self.wL(e)
        self.load_gbc(self.rowf[0:1, 0:D], self.rowf[0:1, D:2 * D])
        wov = dr["wo"].rearrange("(c p) o -> p c o", p=128)
        for rnd in range(2):
            g.wait_ge(self.sT, self.S["sT"])
            self.dma(g, wo.rearrange("p (c o) -> p c o", c=16),
                     wov[:, 16 * rnd:16 * rnd + 16, :], "sL")
            self.wL(tn)
            for tt in range(NT):
                ve.drain().then_inc(self.sV, 1)
                sy.wait_ge(self.sV, self.inc("sV"))
                sy.wait_ge(self.sS, self.S["sS"])
                for oc in range(16):
                    self.dma(sy, intf[:, oc * 256:(oc + 1) * 256],
                             dr["l_inter"][oc][tt], "sL")
                self.wL(ve)
                ve.drain().then_inc(self.sV, 1)
                tn.wait_ge(self.sV, self.inc("sV"))
                self.transpose_to(
                    lambda i: intf[:, (16 * rnd + i) * 128:
                                   (16 * rnd + i + 1) * 128], 16,
                    lambda i: intT[:, i * 128:(i + 1) * 128])
                tn.wait_ge(self.sC, self.S["sC"])
                for half in range(2):
                    ps = (self.psB, self.psC)[half]
                    bid = ("ps", half)
                    if bid in self.bank_ev:
                        tn.wait_ge(self.sC, self.bank_ev[bid])
                    if rnd == 0:
                        tn.matmul(out=ps[:, 0:512],
                                  lhsT=self.onesbr[0:1, :],
                                  rhs=self.rowb[0:1, 5504 + half * 512:
                                                5504 + (half + 1) * 512],
                                  start=True, stop=False)
                    for c16 in range(16):
                        tn.matmul(out=ps[:, 0:512],
                                  lhsT=intT[:, c16 * 128:(c16 + 1) * 128],
                                  rhs=wo[:, c16 * 1024 + half * 512:
                                         c16 * 1024 + (half + 1) * 512],
                                  start=(rnd == 1 and c16 == 0),
                                  stop=(c16 == 15))
                tcnt = self.mm_done()
                pslot = part[:, tt * 1024:(tt + 1) * 1024]
                self.bank_ev[("ps", 0)] = self.evac(
                    self.tA[:, 0:512], self.psB[:, 0:512], tcount=tcnt)
                self.bank_ev[("ps", 1)] = self.evac(
                    self.tA[:, 512:1024], self.psC[:, 0:512], tcount=tcnt)
                ve.wait_ge(self.sC, self.S["sC"])
                if rnd == 0:
                    ve.tensor_copy(out=pslot, in_=self.tA[:, 0:1024])
                    ve.drain().then_inc(self.sV, 1)
                    self.inc("sV")
                    continue
                ve.tensor_tensor(out=self.tA[:, 0:1024], in0=pslot,
                                 in1=self.tA[:, 0:1024], op=ALU.add)
                ve.drain().then_inc(self.sV, 1)
                sy.wait_ge(self.sV, self.inc("sV"))
                self.dma(sy, self.tB[:, 0:1024], dr["l_hs"][tt], "sL")
                self.wL(ve)
                ve.tensor_tensor(out=self.tA[:, 0:1024], in0=self.tA[:, 0:1024],
                                 in1=self.tB[:, 0:1024], op=ALU.add)
                ve.drain().then_inc(self.sV, 1)
                self.inc("sV")
                mu, rc = self.ln_stats(self.tA[:, 0:1024], 1e-12)
                self.ln_norm(self.tA[:, 0:1024], self.tB[:, 0:1024], mu, rc)
                ve.tensor_tensor(out=self.tB[:, 0:1024], in0=self.tB[:, 0:1024],
                                 in1=self.gbc[:, 0:1024], op=ALU.mult)
                ve.drain()
                ve.tensor_tensor(out=self.tB[:, 0:1024], in0=self.tB[:, 0:1024],
                                 in1=self.gbc[:, 1024:2048], op=ALU.add)
                ve.drain().then_inc(self.sV, 1)
                self.inc("sV")
                mu, rc = self.ln_stats(self.tB[:, 0:1024], 1e-5)
                self.ln_norm(self.tB[:, 0:1024], self.tC[:, 0:1024], mu, rc)
                ve.drain().then_inc(self.sV, 1)
                sy.wait_ge(self.sV, self.inc("sV"))
                self.dma(sy, self.tA[:, 0:1024], dr["l_x3"][tt], "sL")
                self.wL(ve)
                ve.tensor_tensor(out=self.tC[:, 0:1024], in0=self.tC[:, 0:1024],
                                 in1=self.tA[:, 0:1024], op=ALU.add)
                ve.drain().then_inc(self.sV, 1)
                sy.wait_ge(self.sV, self.inc("sV"))
                self.dma(sy, out_sh[tt * 128:(tt + 1) * 128, :],
                         self.tC[:, 0:1024], "sS")
        sy.wait_ge(self.sS, self.S["sS"])

_NC_CACHE = {}


def _get_nc(nphase=NPH):
    if nphase not in _NC_CACHE:
        _NC_CACHE[nphase] = Build(nphase).build()
    return _NC_CACHE[nphase]


def _mk_inputs(inputs, nphase=NPH):
    f32 = np.float32
    I = {k: np.asarray(v) for k, v in inputs.items()}
    x = np.ascontiguousarray(I["L0_attn_x"], dtype=f32)
    id_bf = np.eye(128, dtype=bf16)
    id_f32 = np.eye(128, dtype=f32)
    onesb_r = np.ones((1, 128), dtype=bf16)
    onesf_r = np.ones((1, 128), dtype=f32)
    onesf_c = np.ones((128, 1), dtype=f32)
    dm = np.zeros((128, 2048), dtype=bf16)
    for dj in range(4):
        for p in range(128):
            for qq in range(512):
                if dj * 128 + p <= qq:
                    dm[p, dj * 512 + qq] = 1.0
    alog = np.ascontiguousarray(I["L1_attn_A_log"], dtype=f32)
    dskip = np.ascontiguousarray(I["L1_attn_D_skip"], dtype=f32)
    convw = np.ascontiguousarray(I["L1_attn_conv_w"], dtype=f32)
    convb = np.ascontiguousarray(I["L1_attn_conv_b"], dtype=f32)
    bdt = np.ascontiguousarray(I["L1_attn_b_dt"], dtype=f32)
    maps = []
    for c in range(8):
        b, h = c // 2, c % 2
        rows = np.concatenate([x[b, gb * 512:(gb + 1) * 512, :]
                               for gb in QBLOCKS[h]], axis=0)
        invq2 = np.zeros((128, NT), dtype=f32)
        for tt in range(NT):
            tg = tglob(tt, h)
            q = np.arange(tg, tg + 128, dtype=np.float64) + 1.0
            invq2[:, tt] = (1.0 / (q * q)).astype(f32)
        al = np.zeros((128, 128), dtype=f32)
        for j in range(8):
            al[:, j * 16:(j + 1) * 16] = alog[h * 1024 + j * 128:
                                              h * 1024 + (j + 1) * 128, :]
        dsk = np.zeros((128, 8), dtype=f32)
        for j in range(8):
            dsk[:, j] = dskip[h * 1024 + j * 128:h * 1024 + (j + 1) * 128]
        cw = np.zeros((128, 64), dtype=f32)
        cb = np.zeros((128, 16), dtype=f32)
        bd = np.zeros((128, 16), dtype=f32)
        for j in range(16):
            cw[:, j * 4:(j + 1) * 4] = convw[j * 128:(j + 1) * 128, :]
            cb[:, j] = convb[j * 128:(j + 1) * 128]
            bd[:, j] = bdt[j * 128:(j + 1) * 128]
        m = {
            "x_sh": np.ascontiguousarray(rows),
            "w_hid": np.ascontiguousarray(I["L0_attn_W_hidden"], f32),
            "w_qk": np.ascontiguousarray(I["L0_attn_W_qk"], f32),
            "osg0": np.ascontiguousarray(I["L0_attn_os_gamma"][0].reshape(128, 1), f32),
            "osb0": np.ascontiguousarray(I["L0_attn_os_beta"][0].reshape(128, 1), f32),
            "osg1": np.ascontiguousarray(I["L0_attn_os_gamma"][1].reshape(128, 1), f32),
            "osb1": np.ascontiguousarray(I["L0_attn_os_beta"][1].reshape(128, 1), f32),
            "w_out0": np.ascontiguousarray(I["L0_attn_W_out"], f32),
            "wg_half": np.ascontiguousarray(I["L0_ffn_w_gate"][h * 2048:(h + 1) * 2048], f32),
            "wu_half": np.ascontiguousarray(I["L0_ffn_w_up"][h * 2048:(h + 1) * 2048], f32),
            "wd_half": np.ascontiguousarray(I["L0_ffn_w_down"][h * 512:(h + 1) * 512], f32),
            "w_in": np.ascontiguousarray(I["L1_attn_W_in"], f32),
            "convw": cw, "convb": cb, "bdt_c": bd,
            "w_xp": np.ascontiguousarray(I["L1_attn_W_xproj"], f32),
            "w_dt": np.ascontiguousarray(I["L1_attn_W_dt"], f32),
            "alog_c": al, "dskip_c": dsk,
            "w_out1": np.ascontiguousarray(I["L1_attn_W_out"], f32),
            "w_rt": np.ascontiguousarray(I["L1_ffn_W_router"], f32),
            "w1": np.ascontiguousarray(I["L1_ffn_W1"], f32),
            "wo": np.ascontiguousarray(I["L1_ffn_Wo"], f32),
            "b_hid_r": np.ascontiguousarray(I["L0_attn_b_hidden"].reshape(1, -1), f32),
            "b_qk_r": np.ascontiguousarray(I["L0_attn_b_qk"].reshape(1, -1), f32),
            "b_out0_r": np.ascontiguousarray(I["L0_attn_b_out"].reshape(1, -1), f32),
            "b1_r": np.ascontiguousarray(I["L1_ffn_b1"], f32),
            "bo_r": np.ascontiguousarray(I["L1_ffn_bo"].reshape(1, -1), f32),
            "ln_g_r": np.ascontiguousarray(I["L0_attn_ln_g"].reshape(1, -1), f32),
            "ln_b_r": np.ascontiguousarray(I["L0_attn_ln_b"].reshape(1, -1), f32),
            "ln1_g_r": np.ascontiguousarray(I["L1_attn_ln_gamma"].reshape(1, -1), f32),
            "ln1_b_r": np.ascontiguousarray(I["L1_attn_ln_beta"].reshape(1, -1), f32),
            "ln2_g_r": np.ascontiguousarray(I["L1_ffn_ln_gamma"].reshape(1, -1), f32),
            "ln2_b_r": np.ascontiguousarray(I["L1_ffn_ln_beta"].reshape(1, -1), f32),
            "id_bf": id_bf, "id_f32": id_f32,
            "onesb_r": onesb_r, "onesf_r": onesf_r, "onesf_c": onesf_c,
            "invq2_c": invq2, "dmask": dm,
        }
        maps.append(m)
    return maps


def kernel(**inputs):
    nphase = NPH
    nc = _get_nc(nphase)
    maps = _mk_inputs(inputs, nphase)
    res = run_bass_kernel_spmd(nc, maps, core_ids=list(range(8)))
    out = np.zeros((B, T, D), dtype=np.float32)
    for c in range(8):
        b, h = c // 2, c % 2
        o = res.results[c]["out_sh"]
        for bi, gb in enumerate(QBLOCKS[h]):
            out[b, gb * 512:(gb + 1) * 512, :] = o[bi * 512:(bi + 1) * 512, :]
    return out


# version 03
# speedup vs baseline: 1.0055x; 1.0055x over previous
import numpy as np
import ml_dtypes

import concourse.bass as bass
import concourse.mybir as mybir
from concourse.bass_utils import run_bass_kernel_spmd

F32 = mybir.dt.float32
BF16 = mybir.dt.bfloat16
ACT = mybir.ActivationFunctionType
ALU = mybir.AluOpType
AX = mybir.AxisListType.X

MAGIC = float(np.float32(12582912.0))
B, T, D = 4, 4096, 1024
HID, QK, INNER = 2048, 128, 4096
DI, DS, DTR, NE = 2048, 16, 64, 4
QBLOCKS = [[0, 3, 4, 7], [1, 2, 5, 6]]
NT, TS = 16, 2048
NPH = 99
bf16 = ml_dtypes.bfloat16


def tglob(tt, h):
    return QBLOCKS[h][tt // 4] * 512 + (tt % 4) * 128


class Build:
    def __init__(self, nphase=NPH):
        self.nphase = nphase
        self.nc = bass.Bass("TRN2", target_bir_lowering=False, debug=False)
        self.S = {}
        self._cm = []
        self._pid = {}
        self.bank_ev = {}
        self.syq = {0: [], 1: []}
        self._roleq = None

    def enter(self, cm):
        self._cm.append(cm)
        return cm.__enter__()

    def close(self):
        for cm in reversed(self._cm):
            cm.__exit__(None, None, None)

    def sem(self, name):
        setattr(self, name, self.enter(self.nc.semaphore(name)))
        self.S[name] = 0

    def wL(self, eng):
        if eng is not self.syp:
            self.syp.wait_ge(self.sL, self.S["sL"])
        eng.wait_ge(self.sL, self.S["sL"])
        eng.wait_ge(self.sLg, self.S["sLg"])

    def wS(self, eng):
        if eng is not self.syp:
            self.syp.wait_ge(self.sS, self.S["sS"])
        eng.wait_ge(self.sS, self.S["sS"])

    def inc(self, n, by=1):
        self.S[n] += by
        return self.S[n]

    def dma(self, eng, out, in_, sname):
        if eng is self.syp:
            self.sydma(out, in_, sname)
            return
        if eng is self.g and sname == "sL":
            sname = "sLg"
        eng.dma_start(out=out, in_=in_).then_inc(getattr(self, sname), 16)
        self.inc(sname, 16)
        if sname == "sLg":
            eng.wait_ge(self.sLg, self.S["sLg"])

    def pidmod2(self, eng):
        return self.nc.partition_id([eng.engine]) % 2

    class _RecInstr:
        def __init__(self, op):
            self.op = op

        def then_inc(self, sem, n):
            self.op[3].append((sem, n))
            return self

    class _Rec:
        def __init__(self, b, name):
            self._b = b
            self._name = name
            self.pre = []
            self.role = {0: [], 1: []}
            self.post = []
            self.mode = "pre"
            self.rolecur = 0

        def _q(self):
            if self.mode == "role":
                return self.role[self.rolecur]
            return self.pre if self.mode == "pre" else self.post

        def __getattr__(self, attr):
            if attr.startswith("_") or attr in ("pre", "role", "post", "mode",
                                                "rolecur"):
                raise AttributeError(attr)

            def call(*a, **k):
                op = [attr, a, k, []]
                self._q().append(op)
                return Build._RecInstr(op)
            return call

    @staticmethod
    def _replay_ops(eng, ops):
        for (name, a, k, incs) in ops:
            inst = getattr(eng, name)(*a, **k)
            for (sem, n) in incs:
                inst.then_inc(sem, n)

    def replay_all(self, blk):
        nc = self.nc
        for nm in ["g", "tn", "sc", "ve"]:
            rec = getattr(self, nm)
            dec = {"g": blk.gpsimd, "tn": blk.tensor, "sc": blk.scalar,
                   "ve": blk.vector}[nm]

            def body(eng, rec=rec):
                self._replay_ops(eng, rec.pre)
                if rec.role[0] or rec.role[1]:
                    idx = nc.partition_id([eng.engine]) % 2
                    for h in eng.Switch(index=idx, n=2):
                        self._replay_ops(eng, rec.role[h])
                self._replay_ops(eng, rec.post)
            dec(body)

        def sybody(sy):
            idx = nc.partition_id([sy.engine]) % 2
            for h in sy.Switch(index=idx, n=2):
                for op in self.syq[h]:
                    if op[0] == "w":
                        sy.wait_ge(op[1], op[2])
                    else:
                        sy.dma_start(out=op[1], in_=op[2]).then_inc(op[3], 16)
        blk.sync(sybody)

    def role_region(self, recorders):
        snap = end0 = None
        bsnap = bend0 = None
        for h in (0, 1):
            for r in recorders:
                r.mode = "role"
                r.rolecur = h
            self._roleq = h
            if h == 0:
                snap = dict(self.S)
                bsnap = dict(self.bank_ev)
            else:
                end0 = dict(self.S)
                bend0 = dict(self.bank_ev)
                self.S = dict(snap)
                self.bank_ev = dict(bsnap)
            yield h
            if h == 1:
                assert self.S == end0, f"role mismatch {self.S} vs {end0}"
                assert self.bank_ev == bend0, "bank_ev role mismatch"
        for r in recorders:
            r.mode = "post"
        self._roleq = None

    class _SyP:
        def __init__(self, b):
            self.b = b

        def wait_ge(self, sem, val):
            for q in self.b._syq_targets():
                q.append(("w", sem, val))

    def _syq_targets(self):
        if self._roleq is None:
            return [self.syq[0], self.syq[1]]
        return [self.syq[self._roleq]]

    def sydma(self, out, in_, sname):
        sem = getattr(self, sname)
        for q in self._syq_targets():
            q.append(("d", out, in_, sem))
        self.inc(sname, 16)

    def sydma2(self, out_fn, in_fn, sname):
        assert self._roleq is None
        sem = getattr(self, sname)
        for h in (0, 1):
            o = out_fn(h) if callable(out_fn) else out_fn
            i = in_fn(h) if callable(in_fn) else in_fn
            self.syq[h].append(("d", o, i, sem))
        self.inc(sname, 16)

    def flush_syq(self):
        sy = self.sy
        for h in self.rswitch_noassert([sy]):
            for op in self.syq[h]:
                if op[0] == "w":
                    sy.wait_ge(op[1], op[2])
                else:
                    sy.dma_start(out=op[1], in_=op[2]).then_inc(op[3], 16)

    def rswitch_noassert(self, engines):
        idx = [self.pidmod2(e) for e in engines]
        gen = self.nc.Switch(engines=engines, index=idx, n=2)
        for h in gen:
            yield h

    def rswitch(self, engines):
        idx = [self.pidmod2(e) for e in engines]
        gen = self.nc.Switch(engines=engines, index=idx, n=2)
        snap = end0 = None
        for h in gen:
            if h == 0:
                snap = dict(self.S)
            else:
                end0 = dict(self.S)
                self.S = dict(snap)
            yield h
        if end0 is not None:
            assert self.S == end0, f"role sem mismatch: {self.S} != {end0}"

    def build(self):
        nc = self.nc
        dr = {}

        def din(name, shape, dtype=F32):
            dr[name] = nc.dram_tensor(name, list(shape), dtype,
                                      kind="ExternalInput").ap()

        def dsh(name, shape, dtype=F32):
            dr[name] = nc.dram_tensor(name, list(shape), dtype, kind="Internal",
                                      addr_space="Shared").ap()

        def dlo(name, shape, dtype=F32):
            dr[name] = nc.dram_tensor(name, list(shape), dtype,
                                      kind="Internal").ap()

        din("x_sh", [TS, D])
        din("w_hid", [D, 2 * HID]); din("w_qk", [D, QK])
        din("osg0", [128, 1]); din("osb0", [128, 1])
        din("osg1", [128, 1]); din("osb1", [128, 1])
        din("w_out0", [HID, D])
        din("wg_half", [HID, D]); din("wu_half", [HID, D])
        din("wd_half", [512, INNER])
        din("w_in", [D, 2 * DI])
        din("convw", [128, 64]); din("convb", [128, 16])
        din("w_xp", [DI, 96]); din("w_dt", [DTR, DI]); din("bdt_c", [128, 16])
        din("alog_c", [128, 128]); din("dskip_c", [128, 8])
        din("w_out1", [DI, D]); din("w_rt", [D, NE])
        din("w1", [NE, D, INNER]); din("wo", [INNER, D])
        din("b_hid_r", [1, 2 * HID]); din("b_qk_r", [1, QK])
        din("b_out0_r", [1, D]); din("b1_r", [NE, INNER]); din("bo_r", [1, D])
        din("ln_g_r", [1, D]); din("ln_b_r", [1, D])
        din("ln1_g_r", [1, D]); din("ln1_b_r", [1, D])
        din("ln2_g_r", [1, D]); din("ln2_b_r", [1, D])
        din("id_bf", [128, 128], BF16); din("id_f32", [128, 128], F32)
        din("onesb_r", [1, 128], BF16); din("onesf_r", [1, 128], F32)
        din("onesf_c", [128, 1], F32)
        din("invq2_c", [128, NT]); din("dmask", [128, 2048], BF16)
        out_sh = nc.dram_tensor("out_sh", [TS, D], F32,
                                kind="ExternalOutput").ap()

        dsh("s_v", [T, HID], BF16); dsh("s_kT", [QK, T], BF16)
        dsh("s_wsum", [2, 16])
        dsh("s_wgT", [D, INNER], BF16); dsh("s_wuT", [D, INNER], BF16)
        dsh("s_wdT", [INNER, D], BF16)
        dsh("s_u", [DI, T + 3]); dsh("s_z", [DI, T])
        dsh("s_dlt", [DI, T]); dsh("s_uc", [DI, T])
        dsh("s_bc", [32, T]); dsh("s_g", [DI, T], BF16)
        dlo("l_gate", [NT, 128, HID], BF16)
        dlo("l_xa", [NT, 128, D]); dlo("l_x2", [NT, 128, D])
        dlo("l_xa2", [NT, 128, D])
        dlo("l_glin", [NT, 128, INNER], BF16)
        dlo("l_ulin", [NT, 128, INNER], BF16)
        dlo("l_xq2T", [NT, 128, INNER], BF16)
        dlo("l_x3", [NT, 128, D]); dlo("l_hs", [NT, 128, D])
        dlo("l_part", [NT, 128, D])
        dlo("l_inter", [16, NT, 128, 256], BF16)
        for i in range(8):
            dlo(f"cc_i{i}", [1, 8]); dlo(f"cc_o{i}", [1, 8])
        self.dr = dr

        for s in ["sL", "sLg", "sS", "sT", "sV", "sC", "sB"]:
            self.sem(s)

        t = lambda nm, sh, dt=F32: self.enter(nc.sbuf_tensor(nm, list(sh), dt))
        self.cst = t("cst", [128, 256], BF16)
        self.cstf = t("cstf", [128, 256])
        self.onesbr = t("onesbr", [1, 128], BF16)
        self.onesfr = t("onesfr", [1, 128])
        self.cols = t("cols", [128, 64])
        self.rowb = t("rowb", [1, 8192], BF16)
        self.rowf = t("rowf", [1, 2048])
        self.srow = t("srow", [1, 64])
        self.dmsk = t("dmsk", [128, 2048], BF16)
        self.carry = t("carry", [128, 130])
        self.gbc = t("gbc", [128, 2 * D])
        self.tA = t("tA", [128, 1024])
        self.tB = t("tB", [128, 1024])
        self.tC = t("tC", [128, 1024])
        self.tDmp = t("tDmp", [128, 1024])
        self.bfA = t("bfA", [128, 4096], BF16)
        self.bfB = t("bfB", [128, 4096], BF16)
        self.wsm = t("wsm", [128, 1024], BF16)
        self.ARf = t("ARf", [128, 19456])
        self.ARb = t("ARb", [128, 29696], BF16)
        p = lambda nm, sh, dt=F32: self.enter(nc.psum_tensor(nm, list(sh), dt))
        self.psA = p("psA", [128, 2048])
        self.psB = p("psB", [128, 512])
        self.psC = p("psC", [128, 512])
        self.psTa = p("psTa", [128, 1024], BF16)
        self.psTb = p("psTb", [128, 1024], BF16)

        self.g = self._Rec(self, "g")
        self.tn = self._Rec(self, "tn")
        self.sc = self._Rec(self, "sc")
        self.ve = self._Rec(self, "ve")
        self.syp = self._SyP(self)

        self.emit(out_sh)

        blk = self.enter(nc.Block())
        self.replay_all(blk)
        self.close()
        return nc

    def barrier(self, idx):
        g, dr = self.g, self.dr
        self.wS(g)
        g.memset(self.srow[0:1, 32:40], 1.0)
        g.drain()
        g.dma_start(out=dr[f"cc_i{idx}"][0:1, 0:8],
                    in_=self.srow[0:1, 32:40]).then_inc(self.sLg, 16)
        self.inc("sLg", 16)
        g.wait_ge(self.sLg, self.S["sLg"])
        g.collective_compute(
            "AllReduce", ALU.add,
            replica_groups=[[0, 1], [2, 3], [4, 5], [6, 7]],
            ins=[dr[f"cc_i{idx}"]], outs=[dr[f"cc_o{idx}"]],
        ).then_inc(self.sB, 1)
        self.inc("sB")
        for e in [self.syp, self.g, self.tn, self.sc, self.ve]:
            e.wait_ge(self.sB, self.S["sB"])

    def mm_done(self):
        q = self.tn._q()
        q[-1][3].append((self.sT, 1))
        return self.inc("sT")

    def evac(self, out_ap, psum_ap, func=ACT.Copy, scale=1.0, bias=0.0,
             tcount=None):
        sc = self.sc
        if func == ACT.Copy:
            bias = float(bias)
            if isinstance(scale, (float, np.floating)):
                scale = float(scale)
        else:
            if isinstance(scale, (float, np.floating)):
                scale = np.float32(scale)
            if isinstance(bias, (float, np.floating)):
                bias = np.float32(bias)
        sc.wait_ge(self.sT, tcount if tcount is not None else self.S["sT"])
        sc.activation(out=out_ap, in_=psum_ap, func=func, scale=scale,
                      bias=bias)
        sc.drain().then_inc(self.sC, 1)
        return self.inc("sC")

    def ln_stats(self, x_ap, eps, n=1024):
        sc, ve = self.sc, self.ve
        mu = self.cols[:, 20:21]
        rc = self.cols[:, 21:22]
        tc = self.cols[:, 22:23]
        sc.wait_ge(self.sV, self.S["sV"])
        sc.activation(out=self.tDmp[:, 0:n], in_=x_ap, func=ACT.Copy,
                      accum_out=mu)
        sc.drain()
        sc.activation(out=self.tDmp[:, 0:n], in_=x_ap, func=ACT.Square,
                      accum_out=tc)
        sc.drain().then_inc(self.sC, 1)
        ve.wait_ge(self.sC, self.inc("sC"))
        ve.tensor_scalar(out=mu, in0=mu, scalar1=1.0 / n, scalar2=None,
                         op0=ALU.mult)
        ve.drain()
        ve.tensor_scalar(out=tc, in0=tc, scalar1=1.0 / n, scalar2=None,
                         op0=ALU.mult)
        ve.drain()
        ve.scalar_tensor_tensor(out=rc, in0=mu, scalar=-1.0, in1=mu,
                                op0=ALU.mult, op1=ALU.mult)
        ve.drain()
        ve.tensor_tensor(out=tc, in0=tc, in1=rc, op=ALU.add)
        ve.drain().then_inc(self.sV, 1)
        sc.wait_ge(self.sV, self.inc("sV"))
        sc.activation(out=rc, in_=tc, func=ACT.Sqrt, bias=np.float32(eps), scale=np.float32(1.0))
        sc.drain().then_inc(self.sC, 1)
        ve.wait_ge(self.sC, self.inc("sC"))
        ve.reciprocal(out=rc, in_=rc)
        ve.drain()
        return mu, rc

    def ln_norm(self, x_ap, out_ap, mu, rc):
        self.ve.tensor_scalar(out=out_ap, in0=x_ap, scalar1=mu, scalar2=rc,
                              op0=ALU.subtract, op1=ALU.mult)
        self.ve.drain()

    def bcast_f32(self, row_ap, out_ap, n):
        tn = self.tn
        tn.wait_ge(self.sC, self.S["sC"])
        tn.matmul(out=self.psB[:, 0:n], lhsT=self.onesfr[0:1, :],
                  rhs=row_ap, start=True, stop=True)
        self.evac(out_ap, self.psB[:, 0:n], tcount=self.mm_done())

    def load_gbc(self, grow, brow):
        for hh in range(2):
            self.bcast_f32(grow[0:1, hh * 512:(hh + 1) * 512],
                           self.gbc[:, hh * 512:(hh + 1) * 512], 512)
            self.bcast_f32(brow[0:1, hh * 512:(hh + 1) * 512],
                           self.gbc[:, D + hh * 512:D + (hh + 1) * 512], 512)

    def transpose_to(self, src_fn, nblk, dst_fn, f32=False):
        tn, sc = self.tn, self.sc
        ident = self.cstf[:, 0:128] if f32 else self.cst[:, 0:128]
        pos, bank = 0, 0
        grp = 4 if f32 else 8
        while pos < nblk:
            cnt = min(grp, nblk - pos)
            if f32:
                ps = self.psA[:, bank * 512:(bank + 1) * 512]
            else:
                ps = (self.psTa, self.psTb)[bank][:, 0:1024]
            tn.wait_ge(self.sC, self.S["sC"])
            for i in range(cnt):
                tn.transpose(out=ps[:, i * 128:(i + 1) * 128],
                             in_=src_fn(pos + i), identity=ident)
            tcnt = self.mm_done()
            sc.wait_ge(self.sT, tcnt)
            for i in range(cnt):
                sc.copy(out=dst_fn(pos + i), in_=ps[:, i * 128:(i + 1) * 128])
            sc.drain().then_inc(self.sC, 1)
            self.inc("sC")
            pos += cnt
            bank ^= 1

    def emit(self, out_sh):
        dr = self.dr
        sy, g, tn, sc, ve = self.syp, self.g, self.tn, self.sc, self.ve
        xres = self.ARf[:, 0:16384]

        xsv = dr["x_sh"].rearrange("(a p) m -> p a m", p=128)
        for gg in range(4):
            self.dma(sy, xres[:, gg * 4096:(gg + 1) * 4096]
                     .rearrange("p (a m) -> p a m", a=4),
                     xsv[:, gg * 4:(gg + 1) * 4, :], "sL")
        self.dma(sy, self.cstf[:, 0:128], dr["id_f32"], "sL")
        self.dma(sy, self.cstf[:, 128:129], dr["onesf_c"], "sL")
        self.dma(sy, self.onesfr[:, :], dr["onesf_r"], "sL")
        self.dma(sy, self.dmsk[:, :], dr["dmask"], "sL")
        for i, nm in enumerate(["osg0", "osb0", "osg1", "osb1"]):
            self.dma(sy, self.cols[:, i:i + 1], dr[nm], "sL")
        self.dma(sy, self.cols[:, 4:4 + NT], dr["invq2_c"], "sL")
        self.dma(sy, self.rowf[0:1, 0:D], dr["ln_g_r"], "sL")
        self.dma(sy, self.rowf[0:1, D:2 * D], dr["ln_b_r"], "sL")
        self.dma(g, self.cst[:, 0:128], dr["id_bf"], "sL")
        self.dma(g, self.onesbr[:, :], dr["onesb_r"], "sL")
        self.dma(g, self.wsm[:, 0:8 * QK].rearrange("p (c o) -> p c o", c=8),
                 dr["w_qk"].rearrange("(c p) o -> p c o", p=128), "sL")
        self.dma(g, self.rowb[0:1, 0:2 * HID], dr["b_hid_r"], "sL")
        self.dma(g, self.rowb[0:1, 4096:4096 + QK], dr["b_qk_r"], "sL")
        self.dma(g, self.rowb[0:1, 4224:4224 + D], dr["b_out0_r"], "sL")
        self.dma(g, self.rowb[0:1, 5504:5504 + D], dr["bo_r"], "sL")
        for e in [tn, sc, ve]:
            self.wL(e)
        if self.nphase >= 0:
            self.load_gbc(self.rowf[0:1, 0:D], self.rowf[0:1, D:2 * D])

        if self.nphase <= 0:
            return self.finish(out_sh, xres)
        self.phase_a1()
        if self.nphase == 1:
            return self.finish(out_sh, xres)
        self.barrier(0)
        self.phase_prep()
        self.barrier(1)
        self.phase_a2()
        if self.nphase == 2:
            return self.finish(out_sh, xres)
        self.phase_a3()
        if self.nphase == 3:
            return self.finish(out_sh, xres)
        self.phase_b()
        self.barrier(2)
        self.phase_c()
        self.barrier(3)
        if self.nphase == 4:
            return self.finish(out_sh, xres, "l_x2")
        self.phase_d()
        self.barrier(4)
        if self.nphase == 5:
            return self.finish(out_sh, xres, "l_x2")
        self.phase_e(out_sh)

    def finish(self, out_sh, xres, reload=None):
        sy = self.syp
        sy.wait_ge(self.sV, self.S["sV"])
        sy.wait_ge(self.sC, self.S["sC"])
        sy.wait_ge(self.sS, self.S["sS"])
        if reload:
            xi = self.dr[reload].rearrange("a p m -> p a m")
            for gg in range(4):
                self.dma(sy, xres[:, gg * 4096:(gg + 1) * 4096]
                         .rearrange("p (a m) -> p a m", a=4),
                         xi[:, gg * 4:(gg + 1) * 4, :], "sL")
            self.wL(sy)
        xo = out_sh.rearrange("(a p) m -> p a m", p=128)
        for gg in range(4):
            self.dma(sy, xo[:, gg * 4:(gg + 1) * 4, :],
                     xres[:, gg * 4096:(gg + 1) * 4096]
                     .rearrange("p (a m) -> p a m", a=4), "sS")
        sy.wait_ge(self.sS, self.S["sS"])

    def phase_a1(self):
        dr = self.dr
        sy, g, tn, sc, ve = self.syp, self.g, self.tn, self.sc, self.ve
        xres = self.ARf[:, 0:16384]
        naT = self.ARb[:, 0:16384]
        QT = self.ARb[:, 16384:18432]
        kTl = self.ARb[:, 18432:20480]
        wch = [self.ARb[:, 20480:24576], self.ARb[:, 24576:28672]]
        nabf = self.bfA[:, 1024:2048]
        qkbf = self.bfB[:, 2048:2176]
        qkT = self.bfB[:, 2176:2304]
        evb = self.bfB[:, 2304:2816]
        psB, psA = self.psB, self.psA

        for tt in range(NT):
            x_ap = xres[:, tt * 1024:(tt + 1) * 1024]
            xa = self.tA[:, 0:1024]
            self.wS(ve)
            mu, rc = self.ln_stats(x_ap, 1e-5)
            self.ln_norm(x_ap, xa, mu, rc)
            ve.drain().then_inc(self.sV, 1)
            sy.wait_ge(self.sV, self.inc("sV"))
            self.dma(sy, dr["l_xa"][tt], xa, "sS")
            mu, rc = self.ln_stats(xa, 1e-5)
            self.ln_norm(xa, self.tB[:, 0:1024], mu, rc)
            ve.tensor_tensor(out=self.tB[:, 0:1024], in0=self.tB[:, 0:1024],
                             in1=self.gbc[:, 0:1024], op=ALU.mult)
            ve.drain()
            ve.tensor_tensor(out=nabf, in0=self.tB[:, 0:1024],
                             in1=self.gbc[:, 1024:2048], op=ALU.add)
            ve.drain().then_inc(self.sV, 1)
            tn.wait_ge(self.sV, self.inc("sV"))
            self.transpose_to(lambda i: nabf[:, i * 128:(i + 1) * 128], 8,
                              lambda i: naT[:, tt * 1024 + i * 128:
                                            tt * 1024 + (i + 1) * 128])
            tn.wait_ge(self.sC, self.S["sC"])
            tn.matmul(out=psB[:, 0:QK], lhsT=self.onesbr[0:1, :],
                      rhs=self.rowb[0:1, 4096:4096 + QK], start=True, stop=False)
            for ic in range(8):
                tn.matmul(out=psB[:, 0:QK],
                          lhsT=naT[:, tt * 1024 + ic * 128:
                                    tt * 1024 + (ic + 1) * 128],
                          rhs=self.wsm[:, ic * QK:(ic + 1) * QK],
                          start=False, stop=(ic == 7))
            self.evac(qkbf, psB[:, 0:QK], func=ACT.Silu, tcount=self.mm_done())
            self.transpose_to(lambda i: qkbf, 1, lambda i: qkT)
            ve.wait_ge(self.sC, self.S["sC"])
            ve.tensor_scalar(out=QT[:, tt * 128:(tt + 1) * 128], in0=qkT,
                             scalar1=self.cols[:, 0:1], scalar2=self.cols[:, 1:2],
                             op0=ALU.mult, op1=ALU.add)
            ve.drain()
            ve.tensor_scalar(out=kTl[:, tt * 128:(tt + 1) * 128], in0=qkT,
                             scalar1=self.cols[:, 2:3], scalar2=self.cols[:, 3:4],
                             op0=ALU.mult, op1=ALU.add)
            ve.drain().then_inc(self.sV, 1)
            self.inc("sV")

        sy.wait_ge(self.sV, self.S["sV"])
        for bi in range(4):
            self.sydma2(lambda h, bi=bi: dr["s_kT"][:, QBLOCKS[h][bi] * 512:
                                                    QBLOCKS[h][bi] * 512 + 512],
                        kTl[:, bi * 512:(bi + 1) * 512], "sS")

        whid = dr["w_hid"].rearrange("(c p) o -> p c o", p=128)
        for oc in range(8):
            buf = wch[oc % 2]
            if oc >= 2:
                g.wait_ge(self.sT, self.S["sT"])
            self.dma(g, buf.rearrange("p (c o) -> p c o", c=8),
                     whid[:, :, oc * 512:(oc + 1) * 512], "sL")
            self.wL(tn)
            for tt in range(NT):
                bid = ("psA", tt % 4)
                pslot = psA[:, (tt % 4) * 512:((tt % 4) + 1) * 512]
                if bid in self.bank_ev:
                    tn.wait_ge(self.sC, self.bank_ev[bid])
                tn.matmul(out=pslot, lhsT=self.onesbr[0:1, :],
                          rhs=self.rowb[0:1, oc * 512:(oc + 1) * 512],
                          start=True, stop=False)
                for ic in range(8):
                    tn.matmul(out=pslot,
                              lhsT=naT[:, tt * 1024 + ic * 128:
                                        tt * 1024 + (ic + 1) * 128],
                              rhs=buf[:, ic * 512:(ic + 1) * 512],
                              start=False, stop=(ic == 7))
                tcnt = self.mm_done()
                self.wS(sc)
                self.bank_ev[bid] = self.evac(evb, pslot, func=ACT.Silu,
                                              tcount=tcnt)
                sy.wait_ge(self.sC, self.S["sC"])
                if oc < 4:
                    self.sydma2(lambda h, tt=tt, oc=oc:
                                dr["s_v"][tglob(tt, h):tglob(tt, h) + 128,
                                          oc * 512:(oc + 1) * 512],
                                evb, "sS")
                else:
                    self.dma(sy, dr["l_gate"][tt][:, (oc - 4) * 512:
                                                  (oc - 3) * 512], evb, "sS")

        acc, red = self.cols[:, 24:25], self.cols[:, 25:26]
        for wi, nm in enumerate(["wg_half", "wu_half", "wd_half"]):
            ve.wait_ge(self.sT, self.S["sT"])
            ve.memset(acc, 0.0)
            ve.drain()
            if wi < 2:
                wap = dr[nm].rearrange("(a p) m -> p a m", p=128)
                loads = [(wap[:, ti:ti + 1, :],) for ti in range(16)]
            else:
                loads = [(dr[nm][ti * 128:(ti + 1) * 128,
                                 q4 * 1024:(q4 + 1) * 1024],)
                         for ti in range(4) for q4 in range(4)]
            for (src_ap,) in loads:
                ve.drain().then_inc(self.sV, 1)
                sy.wait_ge(self.sV, self.inc("sV"))
                self.dma(sy, self.tA[:, 0:1024], src_ap, "sL")
                self.wL(ve)
                ve.tensor_reduce(out=red, in_=self.tA[:, 0:1024], axis=AX,
                                 op=ALU.add, apply_absolute_value=True)
                ve.drain()
                ve.tensor_tensor(out=acc, in0=acc, in1=red, op=ALU.add)
                ve.drain()
            ve.drain().then_inc(self.sV, 1)
            tn.wait_ge(self.sV, self.inc("sV"))
            tn.wait_ge(self.sC, self.S["sC"])
            tn.matmul(out=self.psB[0:1, 0:1], lhsT=acc,
                      rhs=self.cstf[:, 128:129], start=True, stop=True)
            self.evac(self.srow[0:1, wi:wi + 1], self.psB[0:1, 0:1],
                      tcount=self.mm_done())
        sy.wait_ge(self.sC, self.S["sC"])
        self.sydma2(lambda h: dr["s_wsum"][h:h + 1, 0:3],
                    self.srow[0:1, 0:3], "sS")

    def phase_prep(self):
        dr = self.dr
        sy, g, tn, sc, ve = self.syp, self.g, self.tn, self.sc, self.ve
        self.dma(g, self.srow[0:1, 8:11], dr["s_wsum"][0:1, 0:3], "sL")
        self.dma(g, self.srow[0:1, 16:19], dr["s_wsum"][1:2, 0:3], "sL")
        self.wL(ve)
        ve.tensor_tensor(out=self.srow[0:1, 12:15], in0=self.srow[0:1, 8:11],
                         in1=self.srow[0:1, 16:19], op=ALU.add)
        ve.drain()
        ve.tensor_scalar(out=self.srow[0:1, 12:15], in0=self.srow[0:1, 12:15],
                         scalar1=1.0 / (4096.0 * 1024.0), scalar2=1e-5,
                         op0=ALU.mult, op1=ALU.max)
        ve.drain()
        ve.reciprocal(out=self.srow[0:1, 20:23], in_=self.srow[0:1, 12:15])
        ve.drain().then_inc(self.sV, 1)
        tn.wait_ge(self.sV, self.inc("sV"))
        for j in range(3):
            self.bcast_f32(self.srow[0:1, 12 + j:13 + j],
                           self.cols[:, 26 + j:27 + j], 1)
            self.bcast_f32(self.srow[0:1, 20 + j:21 + j],
                           self.cols[:, 29 + j:30 + j], 1)
        qt = [self.ARb[:, j * 4096:(j + 1) * 4096] for j in range(4)]
        ev = self.ARb[:, 16384:16896]
        scratch = self.ARf[:, 16384:18432]
        for (nm, dst, O, I, swc) in [("wg_half", "s_wgT", 2048, 1024, 29),
                                     ("wu_half", "s_wuT", 2048, 1024, 30),
                                     ("wd_half", "s_wdT", 512, 4096, 31)]:
            for og in range(O // 512):
                for j4 in range(4):
                    nh = 1 if I == 1024 else 2
                    for hh in range(nh):
                        w = I // nh
                        src = scratch[:, 0:w]
                        ve.drain().then_inc(self.sV, 1)
                        sy.wait_ge(self.sV, self.inc("sV"))
                        self.dma(sy, src,
                                 dr[nm][og * 512 + j4 * 128:
                                        og * 512 + (j4 + 1) * 128,
                                        hh * w:(hh + 1) * w], "sL")
                        self.wL(ve)
                        ve.wait_ge(self.sT, self.S["sT"])
                        self._quant3(src, qt[j4][:, hh * w:(hh + 1) * w], swc)
                ve.drain().then_inc(self.sV, 1)
                tn.wait_ge(self.sV, self.inc("sV"))
                for ic in range(I // 128):
                    ps = (self.psTa, self.psTb)[ic % 2][:, 0:512]
                    tn.wait_ge(self.sC, self.S["sC"])
                    for j4 in range(4):
                        tn.transpose(out=ps[:, j4 * 128:(j4 + 1) * 128],
                                     in_=qt[j4][:, ic * 128:(ic + 1) * 128],
                                     identity=self.cst[:, 0:128])
                    tcnt = self.mm_done()
                    sc.wait_ge(self.sT, tcnt)
                    self.wS(sc)
                    sc.copy(out=ev, in_=ps[:, 0:512])
                    sc.drain().then_inc(self.sC, 1)
                    self.inc("sC")
                    sy.wait_ge(self.sC, self.S["sC"])
                    self.sydma2(
                        lambda h, dst=dst, ic=ic, og=og, O=O:
                        dr[dst][ic * 128:(ic + 1) * 128,
                                ((h * 2048 + og * 512) if O == 2048
                                 else h * 512):
                                ((h * 2048 + og * 512) if O == 2048
                                 else h * 512) + 512],
                        ev, "sS")

    def _quant3(self, src, dst_bf, swc):
        ve = self.ve
        ve.tensor_scalar(out=src, in0=src, scalar1=self.cols[:, swc:swc + 1],
                         scalar2=MAGIC, op0=ALU.mult, op1=ALU.add)
        ve.drain()
        ve.tensor_scalar(out=src, in0=src, scalar1=MAGIC, scalar2=1.0,
                         op0=ALU.subtract, op1=ALU.min)
        ve.drain()
        ve.tensor_scalar(out=dst_bf, in0=src, scalar1=-1.0, scalar2=None,
                         op0=ALU.max)
        ve.drain()

    def phase_a2(self):
        dr = self.dr
        sy, g, tn, sc, ve = self.syp, self.g, self.tn, self.sc, self.ve
        xres = self.ARf[:, 0:16384]
        ogf = self.ARf[:, 16384:18432]
        xat = self.ARf[:, 18432:19456]
        wout0 = self.ARb[:, 0:16384]
        QT = self.ARb[:, 16384:18432]
        kTf = self.ARb[:, 18432:22528]
        vsl = [self.ARb[:, 22528 + i * 2048:22528 + (i + 1) * 2048]
               for i in range(2)]
        ogT = self.ARb[:, 26624:28672]
        attn = self.ARb[:, 28672:29696]
        og = self.bfB[:, 0:2048]
        gatet = self.bfA[:, 0:2048]

        self.dma(g, wout0.rearrange("p (c o) -> p c o", c=16),
                 dr["w_out0"].rearrange("(c p) o -> p c o", p=128), "sL")
        self.dma(g, kTf, dr["s_kT"], "sL")
        for e in [tn, sc, ve, sy]:
            self.wL(e)

        for h in self.role_region([tn, sc, ve]):
            for tt in range(NT):
                gb = QBLOCKS[h][tt // 4]
                nk = gb * 4 + (tt % 4) + 1
                sy.wait_ge(self.sV, self.S["sV"])
                self.dma(sy, gatet, dr["l_gate"][tt], "sL")
                self.dma(sy, xat, dr["l_xa"][tt], "sL")
                first_av = True
                for rnd in range((nk + 1) // 2):
                    ks = [k for k in (2 * rnd, 2 * rnd + 1) if k < nk]
                    for kj in ks:
                        sy.wait_ge(self.sT, self.S["sT"])
                        self.dma(sy, vsl[kj % 2],
                                 dr["s_v"][kj * 128:(kj + 1) * 128, :], "sL")
                    self.wL(tn)
                    for kj in ks:
                        ps = (self.psB, self.psC)[kj % 2]
                        bid = ("ps", kj % 2)
                        if bid in self.bank_ev:
                            tn.wait_ge(self.sC, self.bank_ev[bid])
                        tn.matmul(out=ps[:, 0:128],
                                  lhsT=kTf[:, kj * 128:(kj + 1) * 128],
                                  rhs=QT[:, tt * 128:(tt + 1) * 128],
                                  start=True, stop=True)
                        tcnt = self.mm_done()
                        sc.wait_ge(self.sT, tcnt)
                        sc.activation(out=self.tA[:, 0:128], in_=ps[:, 0:128],
                                      func=ACT.Relu)
                        sc.drain()
                        sl = attn[:, (kj % 8) * 128:(kj % 8 + 1) * 128]
                        sc.activation(out=sl, in_=self.tA[:, 0:128],
                                      func=ACT.Square)
                        sc.drain().then_inc(self.sC, 1)
                        self.bank_ev[bid] = self.inc("sC")
                        if gb * 4 <= kj < gb * 4 + 4:
                            ve.wait_ge(self.sC, self.S["sC"])
                            dj = kj - gb * 4
                            ve.tensor_tensor(
                                out=sl, in0=sl,
                                in1=self.dmsk[:, dj * 512 + (tt % 4) * 128:
                                              dj * 512 + (tt % 4 + 1) * 128],
                                op=ALU.mult)
                            ve.drain().then_inc(self.sV, 1)
                            self.inc("sV")
                            tn.wait_ge(self.sV, self.S["sV"])
                        else:
                            tn.wait_ge(self.sC, self.S["sC"])
                    if first_av:
                        for q in range(4):
                            bid = ("psA", q)
                            if bid in self.bank_ev:
                                tn.wait_ge(self.sC, self.bank_ev[bid])
                        first_av = False
                    for kj in ks:
                        for oc in range(4):
                            tn.matmul(out=self.psA[:, oc * 512:(oc + 1) * 512],
                                      lhsT=attn[:, (kj % 8) * 128:
                                                (kj % 8 + 1) * 128],
                                      rhs=vsl[kj % 2][:, oc * 512:
                                                      (oc + 1) * 512],
                                      start=(kj == 0), stop=(kj == nk - 1))
                    tcnt = self.mm_done()
                c = self.evac(ogf, self.psA[:, 0:2048], func=ACT.Copy,
                              scale=self.cols[:, 4 + tt:5 + tt], tcount=tcnt)
                for q in range(4):
                    self.bank_ev[("psA", q)] = c
                ve.wait_ge(self.sC, c)
                self.wL(ve)
                ve.tensor_tensor(out=og, in0=ogf, in1=gatet, op=ALU.mult)
                ve.drain().then_inc(self.sV, 1)
                tn.wait_ge(self.sV, self.inc("sV"))
                self.transpose_to(lambda i: og[:, i * 128:(i + 1) * 128], 16,
                                  lambda i: ogT[:, i * 128:(i + 1) * 128])
                tn.wait_ge(self.sC, self.S["sC"])
                for half in range(2):
                    ps = (self.psB, self.psC)[half]
                    bid = ("ps", half)
                    if bid in self.bank_ev:
                        tn.wait_ge(self.sC, self.bank_ev[bid])
                    tn.matmul(out=ps[:, 0:512], lhsT=self.onesbr[0:1, :],
                              rhs=self.rowb[0:1, 4224 + half * 512:
                                            4224 + (half + 1) * 512],
                              start=True, stop=False)
                    for c16 in range(16):
                        tn.matmul(out=ps[:, 0:512],
                                  lhsT=ogT[:, c16 * 128:(c16 + 1) * 128],
                                  rhs=wout0[:, c16 * 1024 + half * 512:
                                            c16 * 1024 + (half + 1) * 512],
                                  start=False, stop=(c16 == 15))
                tcnt = self.mm_done()
                self.bank_ev[("ps", 0)] = self.evac(
                    self.tA[:, 0:512], self.psB[:, 0:512], tcount=tcnt)
                self.bank_ev[("ps", 1)] = self.evac(
                    self.tA[:, 512:1024], self.psC[:, 0:512], tcount=tcnt)
                ve.wait_ge(self.sC, self.S["sC"])
                ve.tensor_tensor(out=self.tA[:, 0:1024], in0=self.tA[:, 0:1024],
                                 in1=xat, op=ALU.add)
                ve.drain().then_inc(self.sV, 1)
                self.inc("sV")
                mu, rc = self.ln_stats(self.tA[:, 0:1024], 1e-5)
                self.ln_norm(self.tA[:, 0:1024], self.tB[:, 0:1024], mu, rc)
                ve.tensor_tensor(out=xres[:, tt * 1024:(tt + 1) * 1024],
                                 in0=xres[:, tt * 1024:(tt + 1) * 1024],
                                 in1=self.tB[:, 0:1024], op=ALU.add)
                ve.drain().then_inc(self.sV, 1)
                self.inc("sV")

    def phase_a3(self):
        dr = self.dr
        sy, g, tn, sc, ve = self.syp, self.g, self.tn, self.sc, self.ve
        g.wait_ge(self.sT, self.S["sT"])
        g.wait_ge(self.sV, self.S["sV"])
        xres = self.ARf[:, 0:16384]
        xqT = self.ARb[:, 0:16384]
        wgc = self.ARb[:, 16384:20480]
        wuc = self.ARb[:, 20480:24576]
        xqbf = self.ARb[:, 24576:25600]
        evg = self.ARb[:, 25600:26112]
        evu = self.ARb[:, 26112:26624]
        for tt in range(NT):
            x_ap = xres[:, tt * 1024:(tt + 1) * 1024]
            mu, rc = self.ln_stats(x_ap, 1e-5)
            self.ln_norm(x_ap, self.tA[:, 0:1024], mu, rc)
            ve.reduce_max(out=self.cols[:, 23:24], in_=self.tA[:, 0:1024],
                          axis=AX, apply_absolute_value=True)
            ve.drain()
            ve.tensor_scalar(out=self.cols[:, 23:24], in0=self.cols[:, 23:24],
                             scalar1=1e-5, scalar2=None, op0=ALU.max)
            ve.drain()
            ve.reciprocal(out=self.cols[:, 40:41], in_=self.cols[:, 23:24])
            ve.drain()
            ve.tensor_scalar(out=self.cols[:, 40:41], in0=self.cols[:, 40:41],
                             scalar1=127.0, scalar2=None, op0=ALU.mult)
            ve.drain()
            ve.tensor_scalar(out=self.cols[:, 41:42], in0=self.cols[:, 23:24],
                             scalar1=1.0 / 127.0, scalar2=None, op0=ALU.mult)
            ve.drain()
            ve.tensor_tensor(out=self.carry[:, tt:tt + 1],
                             in0=self.cols[:, 41:42], in1=self.cols[:, 26:27],
                             op=ALU.mult)
            ve.drain()
            ve.tensor_tensor(out=self.carry[:, 16 + tt:17 + tt],
                             in0=self.cols[:, 41:42], in1=self.cols[:, 27:28],
                             op=ALU.mult)
            ve.drain()
            ve.tensor_scalar(out=self.tA[:, 0:1024], in0=self.tA[:, 0:1024],
                             scalar1=self.cols[:, 40:41], scalar2=MAGIC,
                             op0=ALU.mult, op1=ALU.add)
            ve.drain()
            ve.tensor_scalar(out=xqbf, in0=self.tA[:, 0:1024], scalar1=MAGIC,
                             scalar2=None, op0=ALU.subtract)
            ve.drain().then_inc(self.sV, 1)
            tn.wait_ge(self.sV, self.inc("sV"))
            self.transpose_to(lambda i: xqbf[:, i * 128:(i + 1) * 128], 8,
                              lambda i: xqT[:, tt * 1024 + i * 128:
                                            tt * 1024 + (i + 1) * 128])

        wgT = dr["s_wgT"].rearrange("(c p) o -> p c o", p=128)
        wuT = dr["s_wuT"].rearrange("(c p) o -> p c o", p=128)
        for oc in range(8):
            if oc >= 1:
                g.wait_ge(self.sT, self.S["sT"])
            self.dma(g, wgc.rearrange("p (c o) -> p c o", c=8),
                     wgT[:, :, oc * 512:(oc + 1) * 512], "sL")
            self.dma(g, wuc.rearrange("p (c o) -> p c o", c=8),
                     wuT[:, :, oc * 512:(oc + 1) * 512], "sL")
            self.wL(tn)
            tn.wait_ge(self.sC, self.S["sC"])
            for tt in range(NT):
                pg = 2 * (tt % 2)
                for w2 in range(2):
                    bk = pg + w2
                    ps = self.psA[:, bk * 512:(bk + 1) * 512]
                    bid = ("psA", bk)
                    if bid in self.bank_ev:
                        tn.wait_ge(self.sC, self.bank_ev[bid])
                    wbuf = (wgc, wuc)[w2]
                    for ic in range(8):
                        tn.matmul(out=ps,
                                  lhsT=xqT[:, tt * 1024 + ic * 128:
                                           tt * 1024 + (ic + 1) * 128],
                                  rhs=wbuf[:, ic * 512:(ic + 1) * 512],
                                  start=(ic == 0), stop=(ic == 7))
                tcnt = self.mm_done()
                self.wS(sc)
                evg2 = (evg, self.bfA[:, 0:512])[tt % 2]
                evu2 = (evu, self.bfA[:, 512:1024])[tt % 2]
                self.bank_ev[("psA", pg)] = self.evac(
                    evg2, self.psA[:, pg * 512:(pg + 1) * 512], func=ACT.Silu,
                    scale=self.carry[:, tt:tt + 1], tcount=tcnt)
                self.bank_ev[("psA", pg + 1)] = self.evac(
                    evu2, self.psA[:, (pg + 1) * 512:(pg + 2) * 512],
                    func=ACT.Copy,
                    scale=self.carry[:, 16 + tt:17 + tt], tcount=tcnt)
                sy.wait_ge(self.sC, self.S["sC"])
                self.dma(sy, dr["l_glin"][tt][:, oc * 512:(oc + 1) * 512],
                         evg2, "sS")
                self.dma(sy, dr["l_ulin"][tt][:, oc * 512:(oc + 1) * 512],
                         evu2, "sS")

        g.wait_ge(self.sT, self.S["sT"])
        glt = self.ARb[:, 16384:20480]
        ult = self.ARb[:, 20480:24576]
        h2 = self.ARb[:, 24576:28672]
        xq2 = self.ARb[:, 16384:20480]
        for tt in range(NT):
            ve.drain().then_inc(self.sV, 1)
            sy.wait_ge(self.sV, self.inc("sV"))
            sy.wait_ge(self.sS, self.S["sS"])
            self.dma(sy, glt, dr["l_glin"][tt], "sL")
            self.dma(sy, ult, dr["l_ulin"][tt], "sL")
            self.wL(ve)
            ve.tensor_tensor(out=h2, in0=glt, in1=ult, op=ALU.mult)
            ve.drain()
            ve.reduce_max(out=self.cols[:, 23:24], in_=h2, axis=AX,
                          apply_absolute_value=True)
            ve.drain()
            ve.tensor_scalar(out=self.cols[:, 23:24], in0=self.cols[:, 23:24],
                             scalar1=1e-5, scalar2=None, op0=ALU.max)
            ve.drain()
            ve.reciprocal(out=self.cols[:, 40:41], in_=self.cols[:, 23:24])
            ve.drain()
            ve.tensor_scalar(out=self.cols[:, 40:41], in0=self.cols[:, 40:41],
                             scalar1=127.0, scalar2=None, op0=ALU.mult)
            ve.drain()
            ve.tensor_scalar(out=self.cols[:, 41:42], in0=self.cols[:, 23:24],
                             scalar1=1.0 / 127.0, scalar2=None, op0=ALU.mult)
            ve.drain()
            ve.tensor_tensor(out=self.carry[:, 32 + tt:33 + tt],
                             in0=self.cols[:, 41:42], in1=self.cols[:, 28:29],
                             op=ALU.mult)
            ve.drain()
            qtmp = self.ARf[:, 16384:18432]
            for qh in range(2):
                ve.tensor_scalar(out=qtmp, in0=h2[:, qh * 2048:(qh + 1) * 2048],
                                 scalar1=self.cols[:, 40:41],
                                 scalar2=MAGIC, op0=ALU.mult, op1=ALU.add)
                ve.drain()
                ve.tensor_scalar(out=xq2[:, qh * 2048:(qh + 1) * 2048],
                                 in0=qtmp, scalar1=MAGIC, scalar2=None,
                                 op0=ALU.subtract)
                ve.drain()
            ve.drain().then_inc(self.sV, 1)
            tn.wait_ge(self.sV, self.inc("sV"))
            self.transpose_to(lambda i: xq2[:, i * 128:(i + 1) * 128], 32,
                              lambda i: ult[:, i * 128:(i + 1) * 128])
            sy.wait_ge(self.sC, self.S["sC"])
            self.dma(sy, dr["l_xq2T"][tt], ult, "sS")

        wd = self.ARb[:, 0:16384]
        xq2s = [self.ARb[:, 16384 + i * 4096:16384 + (i + 1) * 4096]
                for i in range(2)]
        wdT = dr["s_wdT"].rearrange("(c p) o -> p c o", p=128)
        sy.wait_ge(self.sS, self.S["sS"])
        for rnd in range(2):
            g.wait_ge(self.sT, self.S["sT"])
            self.dma(g, wd.rearrange("p (c o) -> p c o", c=16),
                     wdT[:, 16 * rnd:16 * rnd + 16, :], "sL")
            self.wL(tn)
            for tt in range(NT):
                sy.wait_ge(self.sT, self.S["sT"])
                self.dma(sy, xq2s[tt % 2], dr["l_xq2T"][tt], "sL")
                self.wL(tn)
                for half in range(2):
                    ps = (self.psB, self.psC)[half]
                    if ("ps", half) in self.bank_ev:
                        tn.wait_ge(self.sC, self.bank_ev[("ps", half)])
                    for c16 in range(16):
                        cx = 16 * rnd + c16
                        tn.matmul(out=ps[:, 0:512],
                                  lhsT=xq2s[tt % 2][:, cx * 128:(cx + 1) * 128],
                                  rhs=wd[:, c16 * 1024 + half * 512:
                                         c16 * 1024 + (half + 1) * 512],
                                  start=(c16 == 0), stop=(c16 == 15))
                tcnt = self.mm_done()
                sc.wait_ge(self.sV, self.S["sV"])
                self.wS(sc)
                part = self.ARf[:, 16384 + (tt % 2) * 1024:
                                16384 + (tt % 2 + 1) * 1024]
                if rnd == 0:
                    self.bank_ev[("ps", 0)] = self.evac(
                        self.tB[:, 0:512], self.psB[:, 0:512],
                        scale=self.carry[:, 32 + tt:33 + tt], tcount=tcnt)
                    self.bank_ev[("ps", 1)] = self.evac(
                        self.tB[:, 512:1024], self.psC[:, 0:512],
                        scale=self.carry[:, 32 + tt:33 + tt], tcount=tcnt)
                    sy.wait_ge(self.sC, self.S["sC"])
                    self.dma(sy, dr["l_part"][tt], self.tB[:, 0:1024], "sS")
                else:
                    self.bank_ev[("ps", 0)] = self.evac(
                        self.tB[:, 0:512], self.psB[:, 0:512],
                        scale=self.carry[:, 32 + tt:33 + tt], tcount=tcnt)
                    self.bank_ev[("ps", 1)] = self.evac(
                        self.tB[:, 512:1024], self.psC[:, 0:512],
                        scale=self.carry[:, 32 + tt:33 + tt], tcount=tcnt)
                    sy.wait_ge(self.sC, self.S["sC"])
                    self.dma(sy, part, dr["l_part"][tt], "sL")
                    ve.wait_ge(self.sC, self.S["sC"])
                    self.wL(ve)
                    ve.tensor_tensor(out=self.tB[:, 0:1024], in0=part,
                                     in1=self.tB[:, 0:1024], op=ALU.add)
                    ve.drain().then_inc(self.sV, 1)
                    self.inc("sV")
                    mu, rc = self.ln_stats(self.tB[:, 0:1024], 1e-5)
                    self.ln_norm(self.tB[:, 0:1024], self.tC[:, 0:1024], mu, rc)
                    ve.tensor_tensor(out=xres[:, tt * 1024:(tt + 1) * 1024],
                                     in0=xres[:, tt * 1024:(tt + 1) * 1024],
                                     in1=self.tC[:, 0:1024], op=ALU.add)
                    ve.drain().then_inc(self.sV, 1)
                    self.inc("sV")

    def phase_b(self):
        dr = self.dr
        sy, g, tn, sc, ve = self.syp, self.g, self.tn, self.sc, self.ve
        g.wait_ge(self.sT, self.S["sT"])
        g.wait_ge(self.sV, self.S["sV"])
        g.wait_ge(self.sC, self.S["sC"])
        xres = self.ARf[:, 0:16384]
        xnT = self.ARb[:, 0:16384]
        wic = [self.ARb[:, 16384 + i * 1024:16384 + (i + 1) * 1024]
               for i in range(2)]
        xnbf = self.ARb[:, 18432:19456]
        self.dma(g, self.rowf[0:1, 0:D], dr["ln1_g_r"], "sL")
        self.dma(g, self.rowf[0:1, D:2 * D], dr["ln1_b_r"], "sL")
        for e in [tn, sc, ve]:
            self.wL(e)
        self.load_gbc(self.rowf[0:1, 0:D], self.rowf[0:1, D:2 * D])
        ve.memset(self.tA[:, 0:3], 0.0)
        ve.drain().then_inc(self.sV, 1)
        sy.wait_ge(self.sV, self.inc("sV"))
        for dblk in range(16):
            self.sydma2(lambda h, dblk=dblk:
                        (dr["s_u"][dblk * 128:(dblk + 1) * 128, 0:3] if h == 0
                         else dr["l_part"][0][0:128, dblk * 3:dblk * 3 + 3]),
                        self.tA[:, 0:3], "sS")

        for tt in range(NT):
            x_ap = xres[:, tt * 1024:(tt + 1) * 1024]
            self.wS(ve)
            mu, rc = self.ln_stats(x_ap, 1e-5)
            self.ln_norm(x_ap, self.tA[:, 0:1024], mu, rc)
            ve.drain().then_inc(self.sV, 1)
            sy.wait_ge(self.sV, self.inc("sV"))
            self.dma(sy, dr["l_xa2"][tt], self.tA[:, 0:1024], "sS")
            self.dma(sy, dr["l_x2"][tt], x_ap, "sS")
            mu, rc = self.ln_stats(self.tA[:, 0:1024], 1e-5)
            self.ln_norm(self.tA[:, 0:1024], self.tB[:, 0:1024], mu, rc)
            ve.tensor_tensor(out=self.tB[:, 0:1024], in0=self.tB[:, 0:1024],
                             in1=self.gbc[:, 0:1024], op=ALU.mult)
            ve.drain()
            ve.tensor_tensor(out=xnbf, in0=self.tB[:, 0:1024],
                             in1=self.gbc[:, 1024:2048], op=ALU.add)
            ve.drain().then_inc(self.sV, 1)
            tn.wait_ge(self.sV, self.inc("sV"))
            self.transpose_to(lambda i: xnbf[:, i * 128:(i + 1) * 128], 8,
                              lambda i: xnT[:, i * 2048 + tt * 128:
                                            i * 2048 + (tt + 1) * 128])

        winv = dr["w_in"].rearrange("(c p) o -> p c o", p=128)
        for oc in range(32):
            buf = wic[oc % 2]
            if oc >= 2:
                g.wait_ge(self.sT, self.S["sT"])
            self.dma(g, buf.rearrange("p (c o) -> p c o", c=8),
                     winv[:, :, oc * 128:(oc + 1) * 128], "sL")
            self.wL(tn)
            tn.wait_ge(self.sC, self.S["sC"])
            for tc in range(4):
                ps = (self.psB, self.psC)[tc % 2]
                bid = ("ps", tc % 2)
                if bid in self.bank_ev:
                    tn.wait_ge(self.sC, self.bank_ev[bid])
                for ic in range(8):
                    tn.matmul(out=ps[:, 0:512],
                              lhsT=buf[:, ic * 128:(ic + 1) * 128],
                              rhs=xnT[:, ic * 2048 + tc * 512:
                                      ic * 2048 + (tc + 1) * 512],
                              start=(ic == 0), stop=(ic == 7))
                tcnt = self.mm_done()
                self.wS(sc)
                self.bank_ev[bid] = self.evac(
                    self.tA[:, 0:512] if tc % 2 == 0 else self.tB[:, 0:512],
                    ps[:, 0:512], tcount=tcnt)
                src = self.tA[:, 0:512] if tc % 2 == 0 else self.tB[:, 0:512]
                sy.wait_ge(self.sC, self.S["sC"])
                if oc < 16:
                    self.sydma2(lambda h, oc=oc, tc=tc:
                                dr["s_u"][oc * 128:(oc + 1) * 128,
                                          3 + QBLOCKS[h][tc] * 512:
                                          3 + QBLOCKS[h][tc] * 512 + 512],
                                src, "sS")
                else:
                    self.sydma2(lambda h, oc=oc, tc=tc:
                                dr["s_z"][(oc - 16) * 128:(oc - 15) * 128,
                                          QBLOCKS[h][tc] * 512:
                                          QBLOCKS[h][tc] * 512 + 512],
                                src, "sS")

    def phase_c(self):
        dr = self.dr
        sy, g, tn, sc, ve = self.syp, self.g, self.tn, self.sc, self.ve
        g.wait_ge(self.sT, self.S["sT"])
        g.wait_ge(self.sV, self.S["sV"])
        g.wait_ge(self.sC, self.S["sC"])
        wxp = self.ARb[:, 0:1536]
        wdt = self.ARb[0:DTR, 1536:3584]
        ucb = self.ARb[:, 4096:12288]
        dtb = self.ARb[0:DTR, 3584:4096]
        ublk = self.ARf[:, 0:8320]
        self.dma(g, self.tC[:, 0:64], dr["convw"], "sL")
        self.dma(g, self.tC[:, 64:80], dr["convb"], "sL")
        self.dma(g, self.tC[:, 80:96], dr["bdt_c"], "sL")
        self.dma(g, wxp.rearrange("p (c o) -> p c o", c=16),
                 dr["w_xp"].rearrange("(c p) o -> p c o", p=128), "sL")
        self.dma(g, wdt, dr["w_dt"], "sL")
        for e in [tn, sc, ve, sy]:
            self.wL(e)

        for tc in range(4):
            sy.wait_ge(self.sV, self.S["sV"])
            for dblk in range(16):
                self.sydma2(ublk[:, dblk * 520:dblk * 520 + 515],
                            lambda h, dblk=dblk, tc=tc:
                            dr["s_u"][dblk * 128:(dblk + 1) * 128,
                                      QBLOCKS[h][tc] * 512:
                                      QBLOCKS[h][tc] * 512 + 515], "sL")
            self.wL(ve)
            for dblk in range(16):
                u0 = ublk[:, dblk * 520:dblk * 520 + 515]
                self.wS(ve)
                ve.tensor_scalar(out=self.tA[:, 0:512], in0=u0[:, 0:512],
                                 scalar1=self.tC[:, dblk * 4:dblk * 4 + 1],
                                 scalar2=None, op0=ALU.mult)
                ve.drain()
                for k in range(1, 4):
                    ve.scalar_tensor_tensor(
                        out=self.tA[:, 0:512], in0=u0[:, k:k + 512],
                        scalar=self.tC[:, dblk * 4 + k:dblk * 4 + k + 1],
                        in1=self.tA[:, 0:512], op0=ALU.mult, op1=ALU.add)
                    ve.drain()
                ve.drain().then_inc(self.sV, 1)
                sc.wait_ge(self.sV, self.inc("sV"))
                self.wS(sc)
                sc.activation(out=self.tB[:, 0:512], in_=self.tA[:, 0:512],
                              func=ACT.Silu, bias=self.tC[:, 64 + dblk:65 + dblk],
                              scale=np.float32(1.0))
                sc.drain().then_inc(self.sC, 1)
                ve.wait_ge(self.sC, self.inc("sC"))
                ve.tensor_copy(out=ucb[:, dblk * 512:(dblk + 1) * 512],
                               in_=self.tB[:, 0:512])
                ve.drain().then_inc(self.sV, 1)
                self.inc("sV")
                sy.wait_ge(self.sC, self.S["sC"])
                self.sydma2(lambda h, dblk=dblk, tc=tc:
                            dr["s_uc"][dblk * 128:(dblk + 1) * 128,
                                       QBLOCKS[h][tc] * 512:
                                       QBLOCKS[h][tc] * 512 + 512],
                            self.tB[:, 0:512], "sS")
            ve.drain().then_inc(self.sV, 1)
            tn.wait_ge(self.sV, self.inc("sV"))
            bid = ("ps", 0)
            if bid in self.bank_ev:
                tn.wait_ge(self.sC, self.bank_ev[bid])
            for dblk in range(16):
                tn.matmul(out=self.psB[0:96, 0:512],
                          lhsT=wxp[:, dblk * 96:(dblk + 1) * 96],
                          rhs=ucb[:, dblk * 512:(dblk + 1) * 512],
                          start=(dblk == 0), stop=(dblk == 15))
            tcnt = self.mm_done()
            sc.wait_ge(self.sT, tcnt)
            sc.copy(out=dtb, in_=self.psB[0:DTR, 0:512])
            sc.drain()
            sc.copy(out=self.tA[0:32, 0:512], in_=self.psB[64:96, 0:512])
            sc.drain().then_inc(self.sC, 1)
            self.bank_ev[bid] = self.inc("sC")
            sy.wait_ge(self.sC, self.S["sC"])
            self.sydma2(lambda h, tc=tc:
                        dr["s_bc"][0:32, QBLOCKS[h][tc] * 512:
                                   QBLOCKS[h][tc] * 512 + 512],
                        self.tA[0:32, 0:512], "sS")
            for dblk in range(16):
                ps = (self.psB, self.psC)[dblk % 2]
                bid = ("ps", dblk % 2)
                if bid in self.bank_ev:
                    tn.wait_ge(self.sC, self.bank_ev[bid])
                tn.matmul(out=ps[:, 0:512],
                          lhsT=wdt[:, dblk * 128:(dblk + 1) * 128],
                          rhs=dtb, start=True, stop=True)
                tcnt = self.mm_done()
                sc.wait_ge(self.sT, tcnt)
                self.wS(sc)
                bcol = self.tC[:, 80 + dblk:81 + dblk]
                sc.activation(out=self.tA[:, 0:512], in_=ps[:, 0:512],
                              func=ACT.Relu, bias=bcol, scale=np.float32(1.0))
                sc.drain()
                sc.activation(out=self.tB[:, 0:512], in_=ps[:, 0:512],
                              func=ACT.Abs, bias=bcol, scale=np.float32(1.0))
                sc.drain().then_inc(self.sC, 1)
                self.bank_ev[bid] = self.inc("sC")
                sc.activation(out=self.tB[:, 0:512], in_=self.tB[:, 0:512],
                              func=ACT.Exp, scale=np.float32(-1.0))
                sc.drain()
                sc.activation(out=self.tB[:, 0:512], in_=self.tB[:, 0:512],
                              func=ACT.Ln, bias=np.float32(1.0), scale=np.float32(1.0))
                sc.drain().then_inc(self.sC, 1)
                ve.wait_ge(self.sC, self.inc("sC"))
                self.wS(ve)
                ve.tensor_tensor(out=self.tA[:, 0:512], in0=self.tA[:, 0:512],
                                 in1=self.tB[:, 0:512], op=ALU.add)
                ve.drain().then_inc(self.sV, 1)
                self.inc("sV")
                sy.wait_ge(self.sV, self.S["sV"])
                self.sydma2(lambda h, dblk=dblk, tc=tc:
                            dr["s_dlt"][dblk * 128:(dblk + 1) * 128,
                                        QBLOCKS[h][tc] * 512:
                                        QBLOCKS[h][tc] * 512 + 512],
                            self.tA[:, 0:512], "sS")

    def phase_d(self):
        dr = self.dr
        sy, g, tn, sc, ve = self.syp, self.g, self.tn, self.sc, self.ve
        g.wait_ge(self.sT, self.S["sT"])
        g.wait_ge(self.sV, self.S["sV"])
        g.wait_ge(self.sC, self.S["sC"])
        dl = self.ARf[:, 0:1024]; uc = self.ARf[:, 1024:2048]
        du = self.ARf[:, 2048:3072]
        aa = [self.ARf[:, 3072:4096], self.ARf[:, 4096:5120]]
        hh_ = self.ARf[:, 5120:6144]; bb = self.ARf[:, 6144:7168]
        zz = self.ARf[:, 7168:8192]
        yall = self.ARf[:, 8192:16384]
        bcB = self.ARb[:, 0:8192]
        bcC = self.ARb[:, 8192:16384]
        gbf = self.bfA[:, 0:1024]
        self.dma(g, self.tC[:, 0:128], dr["alog_c"], "sL")
        self.dma(g, self.tC[:, 128:136], dr["dskip_c"], "sL")
        for e in [tn, sc, ve, sy]:
            self.wL(e)
        sc.activation(out=self.tB[:, 0:128], in_=self.tC[:, 0:128],
                      func=ACT.Exp)
        sc.drain().then_inc(self.sC, 1)
        ve.wait_ge(self.sC, self.inc("sC"))
        ve.tensor_scalar(out=self.tC[:, 0:128], in0=self.tB[:, 0:128],
                         scalar1=-1.0, scalar2=None, op0=ALU.mult)
        ve.drain()
        ve.memset(self.carry[:, 0:128], 0.0)
        ve.drain().then_inc(self.sV, 1)
        self.inc("sV")

        for tc in range(4):
            for nh in range(2):
                sc.wait_ge(self.sV, self.S["sV"])
                for n8 in range(8):
                    n = nh * 8 + n8
                    g.wait_ge(self.sT, self.S["sT"])
                    self.dma(g, self.rowf[0:1, 0:1024],
                             dr["s_bc"][n:n + 1, tc * 1024:(tc + 1) * 1024],
                             "sL")
                    self.dma(g, self.rowf[0:1, 1024:2048],
                             dr["s_bc"][16 + n:17 + n,
                                        tc * 1024:(tc + 1) * 1024], "sL")
                    self.wL(tn)
                    for q in range(4):
                        bid = ("psA", q)
                        if bid in self.bank_ev:
                            tn.wait_ge(self.sC, self.bank_ev[bid])
                    for half in range(2):
                        tn.matmul(out=self.psA[:, half * 512:(half + 1) * 512],
                                  lhsT=self.onesfr[0:1, :],
                                  rhs=self.rowf[0:1, half * 512:
                                                (half + 1) * 512],
                                  start=True, stop=True)
                        tn.matmul(out=self.psA[:, 1024 + half * 512:
                                               1024 + (half + 1) * 512],
                                  lhsT=self.onesfr[0:1, :],
                                  rhs=self.rowf[0:1, 1024 + half * 512:
                                                1024 + (half + 1) * 512],
                                  start=True, stop=True)
                    tcnt = self.mm_done()
                    self.evac(bcB[:, n8 * 1024:(n8 + 1) * 1024],
                              self.psA[:, 0:1024], tcount=tcnt)
                    c = self.evac(bcC[:, n8 * 1024:(n8 + 1) * 1024],
                                  self.psA[:, 1024:2048], tcount=tcnt)
                    for q in range(4):
                        self.bank_ev[("psA", q)] = c
                for dblk in range(8):
                    sy.wait_ge(self.sV, self.S["sV"])
                    self.sydma2(dl, lambda h, dblk=dblk, tc=tc:
                                dr["s_dlt"][h * 1024 + dblk * 128:
                                            h * 1024 + dblk * 128 + 128,
                                            tc * 1024:(tc + 1) * 1024], "sL")
                    self.sydma2(uc, lambda h, dblk=dblk, tc=tc:
                                dr["s_uc"][h * 1024 + dblk * 128:
                                           h * 1024 + dblk * 128 + 128,
                                           tc * 1024:(tc + 1) * 1024], "sL")
                    if nh == 1:
                        self.sydma2(zz, lambda h, dblk=dblk, tc=tc:
                                    dr["s_z"][h * 1024 + dblk * 128:
                                              h * 1024 + dblk * 128 + 128,
                                              tc * 1024:(tc + 1) * 1024], "sL")
                    self.wL(ve)
                    ve.tensor_tensor(out=du, in0=dl, in1=uc, op=ALU.mult)
                    ve.drain()
                    if nh == 0:
                        ve.tensor_scalar(out=yall[:, dblk * 1024:
                                                  (dblk + 1) * 1024],
                                         in0=uc,
                                         scalar1=self.tC[:, 128 + dblk:
                                                         129 + dblk],
                                         scalar2=None, op0=ALU.mult)
                        ve.drain()
                    ve.drain().then_inc(self.sV, 1)
                    self.inc("sV")
                    vbase = self.S["sV"]
                    sc.wait_ge(self.sV, vbase)
                    for n8 in range(8):
                        n = nh * 8 + n8
                        if n8 >= 2:
                            sc.wait_ge(self.sV, vbase + n8 - 1)
                        sc.activation(out=aa[n8 % 2], in_=dl, func=ACT.Exp,
                                      scale=self.tC[:, dblk * 16 + n:
                                                    dblk * 16 + n + 1])
                        sc.drain().then_inc(self.sC, 1)
                        ve.wait_ge(self.sC, self.inc("sC"))
                        ve.tensor_tensor(out=bb, in0=du,
                                         in1=bcB[:, n8 * 1024:(n8 + 1) * 1024],
                                         op=ALU.mult)
                        ve.drain()
                        ve.tensor_tensor_scan(
                            out=hh_, data0=aa[n8 % 2], data1=bb,
                            initial=self.carry[:, dblk * 16 + n:
                                               dblk * 16 + n + 1],
                            op0=ALU.mult, op1=ALU.add)
                        ve.drain()
                        ve.tensor_copy(out=self.carry[:, dblk * 16 + n:
                                                      dblk * 16 + n + 1],
                                       in_=hh_[:, 1023:1024])
                        ve.tensor_tensor(out=bb, in0=hh_,
                                         in1=bcC[:, n8 * 1024:(n8 + 1) * 1024],
                                         op=ALU.mult)
                        ve.drain()
                        ve.tensor_tensor(out=yall[:, dblk * 1024:
                                                  (dblk + 1) * 1024],
                                         in0=yall[:, dblk * 1024:
                                                  (dblk + 1) * 1024],
                                         in1=bb, op=ALU.add)
                        ve.drain().then_inc(self.sV, 1)
                        self.inc("sV")
                    if nh == 1:
                        sc.wait_ge(self.sV, self.S["sV"])
                        self.wL(sc)
                        sc.activation(out=zz, in_=zz, func=ACT.Silu)
                        sc.drain().then_inc(self.sC, 1)
                        ve.wait_ge(self.sC, self.inc("sC"))
                        self.wS(ve)
                        ve.tensor_tensor(out=gbf,
                                         in0=yall[:, dblk * 1024:
                                                  (dblk + 1) * 1024],
                                         in1=zz, op=ALU.mult)
                        ve.drain().then_inc(self.sV, 1)
                        self.inc("sV")
                        sy.wait_ge(self.sV, self.S["sV"])
                        self.sydma2(lambda h, dblk=dblk, tc=tc:
                                    dr["s_g"][h * 1024 + dblk * 128:
                                              h * 1024 + dblk * 128 + 128,
                                              tc * 1024:(tc + 1) * 1024],
                                    gbf, "sS")

    def phase_e(self, out_sh):
        dr = self.dr
        sy, g, tn, sc, ve = self.syp, self.g, self.tn, self.sc, self.ve
        g.wait_ge(self.sT, self.S["sT"])
        g.wait_ge(self.sV, self.S["sV"])
        g.wait_ge(self.sC, self.S["sC"])
        gT = self.ARb[:, 0:8192]
        wout1 = self.ARb[:, 8192:24576]
        hsT = self.ARb[:, 8192:24576]
        hsbf = self.bfA[:, 0:1024]
        wrt = self.ARf[:, 15360:15392]
        hsfT = self.ARf[:, 15488:16512]
        colsM = self.ARf[:, 16512:16576]
        xt2 = self.ARf[:, 16576:17600]
        self.dma(g, wout1.rearrange("p (c o) -> p c o", c=16),
                 dr["w_out1"].rearrange("(c p) o -> p c o", p=128), "sL")
        self.dma(g, wrt.rearrange("p (c o) -> p c o", c=8),
                 dr["w_rt"].rearrange("(c p) o -> p c o", p=128), "sL")
        for e in [tn, sc, ve, sy]:
            self.wL(e)

        for qg in range(4):
            sy.wait_ge(self.sT, self.S["sT"])
            self.sydma2(gT.rearrange("p (c t) -> p c t", c=16),
                        lambda h, qg=qg:
                        dr["s_g"].rearrange("(c p) t -> p c t", p=128)
                        [:, :, QBLOCKS[h][qg] * 512:QBLOCKS[h][qg] * 512 + 512],
                        "sL")
            self.wL(tn)
            for tq in range(4):
                tt = qg * 4 + tq
                for half in range(2):
                    ps = (self.psB, self.psC)[half]
                    bid = ("ps", half)
                    if bid in self.bank_ev:
                        tn.wait_ge(self.sC, self.bank_ev[bid])
                    for c16 in range(16):
                        tn.matmul(out=ps[:, 0:512],
                                  lhsT=gT[:, c16 * 512 + tq * 128:
                                          c16 * 512 + (tq + 1) * 128],
                                  rhs=wout1[:, c16 * 1024 + half * 512:
                                            c16 * 1024 + (half + 1) * 512],
                                  start=(c16 == 0), stop=(c16 == 15))
                tcnt = self.mm_done()
                sc.wait_ge(self.sV, self.S["sV"])
                self.bank_ev[("ps", 0)] = self.evac(
                    self.tA[:, 0:512], self.psB[:, 0:512], tcount=tcnt)
                self.bank_ev[("ps", 1)] = self.evac(
                    self.tA[:, 512:1024], self.psC[:, 0:512], tcount=tcnt)
                sy.wait_ge(self.sV, self.S["sV"])
                sy.wait_ge(self.sS, self.S["sS"])
                self.dma(sy, self.tB[:, 0:1024], dr["l_xa2"][tt], "sL")
                self.dma(sy, xt2, dr["l_x2"][tt], "sL")
                ve.wait_ge(self.sC, self.S["sC"])
                self.wL(ve)
                self.wS(ve)
                ve.tensor_tensor(out=self.tA[:, 0:1024], in0=self.tA[:, 0:1024],
                                 in1=self.tB[:, 0:1024], op=ALU.add)
                ve.drain().then_inc(self.sV, 1)
                self.inc("sV")
                mu, rc = self.ln_stats(self.tA[:, 0:1024], 1e-5)
                self.ln_norm(self.tA[:, 0:1024], self.tB[:, 0:1024], mu, rc)
                ve.tensor_tensor(out=xt2, in0=xt2, in1=self.tB[:, 0:1024],
                                 op=ALU.add)
                ve.drain().then_inc(self.sV, 1)
                sy.wait_ge(self.sV, self.inc("sV"))
                self.dma(sy, dr["l_x3"][tt], xt2, "sS")
                mu, rc = self.ln_stats(xt2, 1e-5)
                self.ln_norm(xt2, self.tB[:, 0:1024], mu, rc)
                ve.drain().then_inc(self.sV, 1)
                sy.wait_ge(self.sV, self.inc("sV"))
                self.dma(sy, dr["l_hs"][tt], self.tB[:, 0:1024], "sS")
                ve.drain().then_inc(self.sV, 1)
                tn.wait_ge(self.sV, self.inc("sV"))
                self.transpose_to(lambda i: self.tB[:, i * 128:(i + 1) * 128],
                                  8, lambda i: hsfT[:, i * 128:(i + 1) * 128],
                                  f32=True)
                tn.wait_ge(self.sC, self.S["sC"])
                for ic in range(8):
                    tn.matmul(out=self.psB[:, 0:NE],
                              lhsT=hsfT[:, ic * 128:(ic + 1) * 128],
                              rhs=wrt[:, ic * 4:(ic + 1) * 4],
                              start=(ic == 0), stop=(ic == 7))
                tcnt = self.mm_done()
                self.bank_ev[("ps", 0)] = self.evac(
                    self.tA[:, 0:NE], self.psB[:, 0:NE], tcount=tcnt)
                ve.wait_ge(self.sC, self.S["sC"])
                ve.tensor_reduce(out=self.cols[:, 23:24], in_=self.tA[:, 0:NE],
                                 axis=AX, op=ALU.max)
                ve.drain()
                ve.memset(self.cols[:, 24:25], 1.0)
                ve.drain()
                for ee in range(NE):
                    ve.tensor_scalar(out=self.cols[:, 25:26],
                                     in0=self.tA[:, ee:ee + 1],
                                     scalar1=self.cols[:, 23:24], scalar2=None,
                                     op0=ALU.is_ge)
                    ve.drain()
                    ve.tensor_tensor(out=colsM[:, tt * 4 + ee:tt * 4 + ee + 1],
                                     in0=self.cols[:, 25:26],
                                     in1=self.cols[:, 24:25], op=ALU.mult)
                    ve.drain()
                    if ee < NE - 1:
                        ve.tensor_scalar(out=self.cols[:, 25:26],
                                         in0=colsM[:, tt * 4 + ee:
                                                   tt * 4 + ee + 1],
                                         scalar1=-1.0, scalar2=1.0,
                                         op0=ALU.mult, op1=ALU.add)
                        ve.drain()
                        ve.tensor_tensor(out=self.cols[:, 24:25],
                                         in0=self.cols[:, 24:25],
                                         in1=self.cols[:, 25:26], op=ALU.mult)
                        ve.drain()
                ve.drain().then_inc(self.sV, 1)
                self.inc("sV")

        for tt in range(NT):
            ve.drain().then_inc(self.sV, 1)
            sy.wait_ge(self.sV, self.inc("sV"))
            sy.wait_ge(self.sS, self.S["sS"])
            self.dma(sy, self.tB[:, 0:1024], dr["l_hs"][tt], "sL")
            self.wL(ve)
            ve.wait_ge(self.sT, self.S["sT"])
            ve.tensor_copy(out=hsbf, in_=self.tB[:, 0:1024])
            ve.drain().then_inc(self.sV, 1)
            tn.wait_ge(self.sV, self.inc("sV"))
            self.transpose_to(lambda i: hsbf[:, i * 128:(i + 1) * 128], 8,
                              lambda i: hsT[:, tt * 1024 + i * 128:
                                            tt * 1024 + (i + 1) * 128])

        def hsT_ap(tt, ic):
            return hsT[:, tt * 1024 + ic * 128:tt * 1024 + (ic + 1) * 128]

        g.wait_ge(self.sT, self.S["sT"])
        g.wait_ge(self.sC, self.S["sC"])
        self._vq = []
        w1b = [self.ARb[:, i * 2048:(i + 1) * 2048] for i in range(2)]
        intr = self.ARb[:, 4096:8192]
        w1v = dr["w1"].rearrange("e (c p) o -> e p c o", p=128)
        for oc in range(16):
            self.wS(ve)
            for ee in range(NE):
                bslot = 6528 + ee * 256
                self.dma(g, self.rowb[0:1, bslot:bslot + 256],
                         dr["b1_r"][ee:ee + 1, oc * 256:(oc + 1) * 256], "sL")
                buf = w1b[(oc * NE + ee) % 2]
                if oc * NE + ee >= 2:
                    g.wait_ge(self.sT, self.S["sT"])
                self.dma(g, buf.rearrange("p (c o) -> p c o", c=8),
                         w1v[ee][:, :, oc * 256:(oc + 1) * 256], "sL")
                self.wL(tn)
                tn.wait_ge(self.sC, self.S["sC"])
                for tt in range(NT):
                    bk = tt % 4
                    ps = self.psA[:, bk * 512:bk * 512 + 256]
                    bid = ("psA", bk)
                    if bid in self.bank_ev:
                        tn.wait_ge(self.sC, self.bank_ev[bid])
                    tn.matmul(out=ps, lhsT=self.onesbr[0:1, :],
                              rhs=self.rowb[0:1, bslot:bslot + 256],
                              start=True, stop=False)
                    for ic in range(8):
                        tn.matmul(out=ps, lhsT=hsT_ap(tt, ic),
                                  rhs=buf[:, ic * 256:(ic + 1) * 256],
                                  start=False, stop=(ic == 7))
                    tcnt = self.mm_done()
                    dstf = (self.tA, self.tB, self.tC, self.tDmp)[bk][:, 0:256]
                    if len(self._vq) >= 4:
                        sc.wait_ge(self.sV, self._vq[-4])
                    self.bank_ev[bid] = self.evac(dstf, ps,
                                                  func=ACT.Gelu, tcount=tcnt)
                    ve.wait_ge(self.sC, self.S["sC"])
                    mc = colsM[:, tt * 4 + ee:tt * 4 + ee + 1]
                    if ee == 0:
                        ve.tensor_scalar(out=intr[:, tt * 256:(tt + 1) * 256],
                                         in0=dstf, scalar1=mc, scalar2=None,
                                         op0=ALU.mult)
                    else:
                        ve.scalar_tensor_tensor(
                            out=intr[:, tt * 256:(tt + 1) * 256], in0=dstf,
                            scalar=mc, in1=intr[:, tt * 256:(tt + 1) * 256],
                            op0=ALU.mult, op1=ALU.add)
                    ve.drain().then_inc(self.sV, 1)
                    self._vq.append(self.inc("sV"))
            sy.wait_ge(self.sV, self.S["sV"])
            for tt in range(NT):
                self.dma(sy, dr["l_inter"][oc][tt],
                         intr[:, tt * 256:(tt + 1) * 256], "sS")

        g.wait_ge(self.sT, self.S["sT"])
        g.wait_ge(self.sV, self.S["sV"])
        self.wS(g)
        wo = self.ARb[:, 0:16384]
        intT = self.ARb[:, 24576:28672]
        intf = self.bfB[:, 0:4096]
        part = self.ARf[:, 0:16384]
        self.dma(g, self.rowf[0:1, 0:D], dr["ln2_g_r"], "sL")
        self.dma(g, self.rowf[0:1, D:2 * D], dr["ln2_b_r"], "sL")
        for e in [tn, sc, ve]:
            self.wL(e)
        self.load_gbc(self.rowf[0:1, 0:D], self.rowf[0:1, D:2 * D])
        wov = dr["wo"].rearrange("(c p) o -> p c o", p=128)
        for rnd in range(2):
            g.wait_ge(self.sT, self.S["sT"])
            self.dma(g, wo.rearrange("p (c o) -> p c o", c=16),
                     wov[:, 16 * rnd:16 * rnd + 16, :], "sL")
            self.wL(tn)
            for tt in range(NT):
                ve.drain().then_inc(self.sV, 1)
                sy.wait_ge(self.sV, self.inc("sV"))
                sy.wait_ge(self.sS, self.S["sS"])
                for oc in range(16):
                    self.dma(sy, intf[:, oc * 256:(oc + 1) * 256],
                             dr["l_inter"][oc][tt], "sL")
                self.wL(ve)
                ve.drain().then_inc(self.sV, 1)
                tn.wait_ge(self.sV, self.inc("sV"))
                self.transpose_to(
                    lambda i: intf[:, (16 * rnd + i) * 128:
                                   (16 * rnd + i + 1) * 128], 16,
                    lambda i: intT[:, i * 128:(i + 1) * 128])
                tn.wait_ge(self.sC, self.S["sC"])
                for half in range(2):
                    ps = (self.psB, self.psC)[half]
                    bid = ("ps", half)
                    if bid in self.bank_ev:
                        tn.wait_ge(self.sC, self.bank_ev[bid])
                    if rnd == 0:
                        tn.matmul(out=ps[:, 0:512],
                                  lhsT=self.onesbr[0:1, :],
                                  rhs=self.rowb[0:1, 5504 + half * 512:
                                                5504 + (half + 1) * 512],
                                  start=True, stop=False)
                    for c16 in range(16):
                        tn.matmul(out=ps[:, 0:512],
                                  lhsT=intT[:, c16 * 128:(c16 + 1) * 128],
                                  rhs=wo[:, c16 * 1024 + half * 512:
                                         c16 * 1024 + (half + 1) * 512],
                                  start=(rnd == 1 and c16 == 0),
                                  stop=(c16 == 15))
                tcnt = self.mm_done()
                pslot = part[:, tt * 1024:(tt + 1) * 1024]
                self.bank_ev[("ps", 0)] = self.evac(
                    self.tA[:, 0:512], self.psB[:, 0:512], tcount=tcnt)
                self.bank_ev[("ps", 1)] = self.evac(
                    self.tA[:, 512:1024], self.psC[:, 0:512], tcount=tcnt)
                ve.wait_ge(self.sC, self.S["sC"])
                if rnd == 0:
                    ve.tensor_copy(out=pslot, in_=self.tA[:, 0:1024])
                    ve.drain().then_inc(self.sV, 1)
                    self.inc("sV")
                    continue
                ve.tensor_tensor(out=self.tA[:, 0:1024], in0=pslot,
                                 in1=self.tA[:, 0:1024], op=ALU.add)
                ve.drain().then_inc(self.sV, 1)
                sy.wait_ge(self.sV, self.inc("sV"))
                self.dma(sy, self.tB[:, 0:1024], dr["l_hs"][tt], "sL")
                self.wL(ve)
                ve.tensor_tensor(out=self.tA[:, 0:1024], in0=self.tA[:, 0:1024],
                                 in1=self.tB[:, 0:1024], op=ALU.add)
                ve.drain().then_inc(self.sV, 1)
                self.inc("sV")
                mu, rc = self.ln_stats(self.tA[:, 0:1024], 1e-12)
                self.ln_norm(self.tA[:, 0:1024], self.tB[:, 0:1024], mu, rc)
                ve.tensor_tensor(out=self.tB[:, 0:1024], in0=self.tB[:, 0:1024],
                                 in1=self.gbc[:, 0:1024], op=ALU.mult)
                ve.drain()
                ve.tensor_tensor(out=self.tB[:, 0:1024], in0=self.tB[:, 0:1024],
                                 in1=self.gbc[:, 1024:2048], op=ALU.add)
                ve.drain().then_inc(self.sV, 1)
                self.inc("sV")
                mu, rc = self.ln_stats(self.tB[:, 0:1024], 1e-5)
                self.ln_norm(self.tB[:, 0:1024], self.tC[:, 0:1024], mu, rc)
                ve.drain().then_inc(self.sV, 1)
                sy.wait_ge(self.sV, self.inc("sV"))
                self.dma(sy, self.tA[:, 0:1024], dr["l_x3"][tt], "sL")
                self.wL(ve)
                ve.tensor_tensor(out=self.tC[:, 0:1024], in0=self.tC[:, 0:1024],
                                 in1=self.tA[:, 0:1024], op=ALU.add)
                ve.drain().then_inc(self.sV, 1)
                sy.wait_ge(self.sV, self.inc("sV"))
                self.dma(sy, out_sh[tt * 128:(tt + 1) * 128, :],
                         self.tC[:, 0:1024], "sS")
        sy.wait_ge(self.sS, self.S["sS"])

_NC_CACHE = {}


def _get_nc(nphase=NPH):
    if nphase not in _NC_CACHE:
        _NC_CACHE[nphase] = Build(nphase).build()
    return _NC_CACHE[nphase]


def _mk_inputs(inputs, nphase=NPH):
    f32 = np.float32
    I = {k: np.asarray(v) for k, v in inputs.items()}
    x = np.ascontiguousarray(I["L0_attn_x"], dtype=f32)
    id_bf = np.eye(128, dtype=bf16)
    id_f32 = np.eye(128, dtype=f32)
    onesb_r = np.ones((1, 128), dtype=bf16)
    onesf_r = np.ones((1, 128), dtype=f32)
    onesf_c = np.ones((128, 1), dtype=f32)
    dm = np.zeros((128, 2048), dtype=bf16)
    for dj in range(4):
        for p in range(128):
            for qq in range(512):
                if dj * 128 + p <= qq:
                    dm[p, dj * 512 + qq] = 1.0
    alog = np.ascontiguousarray(I["L1_attn_A_log"], dtype=f32)
    dskip = np.ascontiguousarray(I["L1_attn_D_skip"], dtype=f32)
    convw = np.ascontiguousarray(I["L1_attn_conv_w"], dtype=f32)
    convb = np.ascontiguousarray(I["L1_attn_conv_b"], dtype=f32)
    bdt = np.ascontiguousarray(I["L1_attn_b_dt"], dtype=f32)
    maps = []
    for c in range(8):
        b, h = c // 2, c % 2
        rows = np.concatenate([x[b, gb * 512:(gb + 1) * 512, :]
                               for gb in QBLOCKS[h]], axis=0)
        invq2 = np.zeros((128, NT), dtype=f32)
        for tt in range(NT):
            tg = tglob(tt, h)
            q = np.arange(tg, tg + 128, dtype=np.float64) + 1.0
            invq2[:, tt] = (1.0 / (q * q)).astype(f32)
        al = np.zeros((128, 128), dtype=f32)
        for j in range(8):
            al[:, j * 16:(j + 1) * 16] = alog[h * 1024 + j * 128:
                                              h * 1024 + (j + 1) * 128, :]
        dsk = np.zeros((128, 8), dtype=f32)
        for j in range(8):
            dsk[:, j] = dskip[h * 1024 + j * 128:h * 1024 + (j + 1) * 128]
        cw = np.zeros((128, 64), dtype=f32)
        cb = np.zeros((128, 16), dtype=f32)
        bd = np.zeros((128, 16), dtype=f32)
        for j in range(16):
            cw[:, j * 4:(j + 1) * 4] = convw[j * 128:(j + 1) * 128, :]
            cb[:, j] = convb[j * 128:(j + 1) * 128]
            bd[:, j] = bdt[j * 128:(j + 1) * 128]
        m = {
            "x_sh": np.ascontiguousarray(rows),
            "w_hid": np.ascontiguousarray(I["L0_attn_W_hidden"], f32),
            "w_qk": np.ascontiguousarray(I["L0_attn_W_qk"], f32),
            "osg0": np.ascontiguousarray(I["L0_attn_os_gamma"][0].reshape(128, 1), f32),
            "osb0": np.ascontiguousarray(I["L0_attn_os_beta"][0].reshape(128, 1), f32),
            "osg1": np.ascontiguousarray(I["L0_attn_os_gamma"][1].reshape(128, 1), f32),
            "osb1": np.ascontiguousarray(I["L0_attn_os_beta"][1].reshape(128, 1), f32),
            "w_out0": np.ascontiguousarray(I["L0_attn_W_out"], f32),
            "wg_half": np.ascontiguousarray(I["L0_ffn_w_gate"][h * 2048:(h + 1) * 2048], f32),
            "wu_half": np.ascontiguousarray(I["L0_ffn_w_up"][h * 2048:(h + 1) * 2048], f32),
            "wd_half": np.ascontiguousarray(I["L0_ffn_w_down"][h * 512:(h + 1) * 512], f32),
            "w_in": np.ascontiguousarray(I["L1_attn_W_in"], f32),
            "convw": cw, "convb": cb, "bdt_c": bd,
            "w_xp": np.ascontiguousarray(I["L1_attn_W_xproj"], f32),
            "w_dt": np.ascontiguousarray(I["L1_attn_W_dt"], f32),
            "alog_c": al, "dskip_c": dsk,
            "w_out1": np.ascontiguousarray(I["L1_attn_W_out"], f32),
            "w_rt": np.ascontiguousarray(I["L1_ffn_W_router"], f32),
            "w1": np.ascontiguousarray(I["L1_ffn_W1"], f32),
            "wo": np.ascontiguousarray(I["L1_ffn_Wo"], f32),
            "b_hid_r": np.ascontiguousarray(I["L0_attn_b_hidden"].reshape(1, -1), f32),
            "b_qk_r": np.ascontiguousarray(I["L0_attn_b_qk"].reshape(1, -1), f32),
            "b_out0_r": np.ascontiguousarray(I["L0_attn_b_out"].reshape(1, -1), f32),
            "b1_r": np.ascontiguousarray(I["L1_ffn_b1"], f32),
            "bo_r": np.ascontiguousarray(I["L1_ffn_bo"].reshape(1, -1), f32),
            "ln_g_r": np.ascontiguousarray(I["L0_attn_ln_g"].reshape(1, -1), f32),
            "ln_b_r": np.ascontiguousarray(I["L0_attn_ln_b"].reshape(1, -1), f32),
            "ln1_g_r": np.ascontiguousarray(I["L1_attn_ln_gamma"].reshape(1, -1), f32),
            "ln1_b_r": np.ascontiguousarray(I["L1_attn_ln_beta"].reshape(1, -1), f32),
            "ln2_g_r": np.ascontiguousarray(I["L1_ffn_ln_gamma"].reshape(1, -1), f32),
            "ln2_b_r": np.ascontiguousarray(I["L1_ffn_ln_beta"].reshape(1, -1), f32),
            "id_bf": id_bf, "id_f32": id_f32,
            "onesb_r": onesb_r, "onesf_r": onesf_r, "onesf_c": onesf_c,
            "invq2_c": invq2, "dmask": dm,
        }
        maps.append(m)
    return maps


def kernel(**inputs):
    nphase = NPH
    nc = _get_nc(nphase)
    maps = _mk_inputs(inputs, nphase)
    res = run_bass_kernel_spmd(nc, maps, core_ids=list(range(8)))
    out = np.zeros((B, T, D), dtype=np.float32)
    for c in range(8):
        b, h = c // 2, c % 2
        o = res.results[c]["out_sh"]
        for bi, gb in enumerate(QBLOCKS[h]):
            out[b, gb * 512:(gb + 1) * 512, :] = o[bi * 512:(bi + 1) * 512, :]
    return out


# version 04
# speedup vs baseline: 1.0527x; 1.0469x over previous
import numpy as np
import ml_dtypes

import concourse.bass as bass
import concourse.mybir as mybir
from concourse.bass_utils import run_bass_kernel_spmd

F32 = mybir.dt.float32
BF16 = mybir.dt.bfloat16
ACT = mybir.ActivationFunctionType
ALU = mybir.AluOpType
AX = mybir.AxisListType.X

MAGIC = float(np.float32(12582912.0))
B, T, D = 4, 4096, 1024
HID, QK, INNER = 2048, 128, 4096
DI, DS, DTR, NE = 2048, 16, 64, 4
QBLOCKS = [[0, 3, 4, 7], [1, 2, 5, 6]]
NT, TS = 16, 2048
NPH = 99
bf16 = ml_dtypes.bfloat16


def tglob(tt, h):
    return QBLOCKS[h][tt // 4] * 512 + (tt % 4) * 128


class Build:
    def __init__(self, nphase=NPH):
        self.nphase = nphase
        self.nc = bass.Bass("TRN2", target_bir_lowering=False, debug=False)
        self.S = {}
        self._cm = []
        self._pid = {}
        self.bank_ev = {}
        self.syq = {0: [], 1: []}
        self._roleq = None

    def enter(self, cm):
        self._cm.append(cm)
        return cm.__enter__()

    def close(self):
        for cm in reversed(self._cm):
            cm.__exit__(None, None, None)

    def sem(self, name):
        setattr(self, name, self.enter(self.nc.semaphore(name)))
        self.S[name] = 0

    def wL(self, eng):
        if eng is not self.syp:
            self.syp.wait_ge(self.sL, self.S["sL"])
        eng.wait_ge(self.sL, self.S["sL"])
        eng.wait_ge(self.sLg, self.S["sLg"])

    def wS(self, eng):
        if eng is not self.syp:
            self.syp.wait_ge(self.sS, self.S["sS"])
        eng.wait_ge(self.sS, self.S["sS"])

    def inc(self, n, by=1):
        self.S[n] += by
        return self.S[n]

    def dma(self, eng, out, in_, sname):
        if eng is self.syp:
            self.sydma(out, in_, sname)
            return
        if eng is self.g and sname == "sL":
            sname = "sLg"
        eng.dma_start(out=out, in_=in_).then_inc(getattr(self, sname), 16)
        self.inc(sname, 16)
        if sname == "sLg":
            eng.wait_ge(self.sLg, self.S["sLg"])

    def pidmod2(self, eng):
        return self.nc.partition_id([eng.engine]) % 2

    class _RecInstr:
        def __init__(self, op):
            self.op = op

        def then_inc(self, sem, n):
            self.op[3].append((sem, n))
            return self

    class _Rec:
        def __init__(self, b, name):
            self._b = b
            self._name = name
            self.pre = []
            self.role = {0: [], 1: []}
            self.post = []
            self.mode = "pre"
            self.rolecur = 0

        def _q(self):
            if self.mode == "role":
                return self.role[self.rolecur]
            return self.pre if self.mode == "pre" else self.post

        def __getattr__(self, attr):
            if attr.startswith("_") or attr in ("pre", "role", "post", "mode",
                                                "rolecur"):
                raise AttributeError(attr)

            def call(*a, **k):
                op = [attr, a, k, []]
                self._q().append(op)
                return Build._RecInstr(op)
            return call

    @staticmethod
    def _replay_ops(eng, ops):
        for (name, a, k, incs) in ops:
            inst = getattr(eng, name)(*a, **k)
            for (sem, n) in incs:
                inst.then_inc(sem, n)

    def replay_all(self, blk):
        nc = self.nc
        for nm in ["g", "tn", "sc", "ve"]:
            rec = getattr(self, nm)
            dec = {"g": blk.gpsimd, "tn": blk.tensor, "sc": blk.scalar,
                   "ve": blk.vector}[nm]

            def body(eng, rec=rec):
                self._replay_ops(eng, rec.pre)
                if rec.role[0] or rec.role[1]:
                    idx = nc.partition_id([eng.engine]) % 2
                    for h in eng.Switch(index=idx, n=2):
                        self._replay_ops(eng, rec.role[h])
                self._replay_ops(eng, rec.post)
            dec(body)

        def sybody(sy):
            idx = nc.partition_id([sy.engine]) % 2
            for h in sy.Switch(index=idx, n=2):
                for op in self.syq[h]:
                    if op[0] == "w":
                        sy.wait_ge(op[1], op[2])
                    else:
                        sy.dma_start(out=op[1], in_=op[2]).then_inc(op[3], 16)
        blk.sync(sybody)

    def role_region(self, recorders):
        snap = end0 = None
        bsnap = bend0 = None
        for h in (0, 1):
            for r in recorders:
                r.mode = "role"
                r.rolecur = h
            self._roleq = h
            if h == 0:
                snap = dict(self.S)
                bsnap = dict(self.bank_ev)
            else:
                end0 = dict(self.S)
                bend0 = dict(self.bank_ev)
                self.S = dict(snap)
                self.bank_ev = dict(bsnap)
            yield h
            if h == 1:
                assert self.S == end0, f"role mismatch {self.S} vs {end0}"
                assert self.bank_ev == bend0, "bank_ev role mismatch"
        for r in recorders:
            r.mode = "post"
        self._roleq = None

    class _SyP:
        def __init__(self, b):
            self.b = b

        def wait_ge(self, sem, val):
            for q in self.b._syq_targets():
                q.append(("w", sem, val))

    def _syq_targets(self):
        if self._roleq is None:
            return [self.syq[0], self.syq[1]]
        return [self.syq[self._roleq]]

    def sydma(self, out, in_, sname):
        sem = getattr(self, sname)
        for q in self._syq_targets():
            q.append(("d", out, in_, sem))
        self.inc(sname, 16)

    def sydma2(self, out_fn, in_fn, sname):
        assert self._roleq is None
        sem = getattr(self, sname)
        for h in (0, 1):
            o = out_fn(h) if callable(out_fn) else out_fn
            i = in_fn(h) if callable(in_fn) else in_fn
            self.syq[h].append(("d", o, i, sem))
        self.inc(sname, 16)

    def flush_syq(self):
        sy = self.sy
        for h in self.rswitch_noassert([sy]):
            for op in self.syq[h]:
                if op[0] == "w":
                    sy.wait_ge(op[1], op[2])
                else:
                    sy.dma_start(out=op[1], in_=op[2]).then_inc(op[3], 16)

    def rswitch_noassert(self, engines):
        idx = [self.pidmod2(e) for e in engines]
        gen = self.nc.Switch(engines=engines, index=idx, n=2)
        for h in gen:
            yield h

    def rswitch(self, engines):
        idx = [self.pidmod2(e) for e in engines]
        gen = self.nc.Switch(engines=engines, index=idx, n=2)
        snap = end0 = None
        for h in gen:
            if h == 0:
                snap = dict(self.S)
            else:
                end0 = dict(self.S)
                self.S = dict(snap)
            yield h
        if end0 is not None:
            assert self.S == end0, f"role sem mismatch: {self.S} != {end0}"

    def build(self):
        nc = self.nc
        dr = {}

        def din(name, shape, dtype=F32):
            dr[name] = nc.dram_tensor(name, list(shape), dtype,
                                      kind="ExternalInput").ap()

        def dsh(name, shape, dtype=F32):
            dr[name] = nc.dram_tensor(name, list(shape), dtype, kind="Internal",
                                      addr_space="Shared").ap()

        def dlo(name, shape, dtype=F32):
            dr[name] = nc.dram_tensor(name, list(shape), dtype,
                                      kind="Internal").ap()

        din("x_sh", [TS, D])
        din("w_hid", [D, 2 * HID]); din("w_qk", [D, QK])
        din("osg0", [128, 1]); din("osb0", [128, 1])
        din("osg1", [128, 1]); din("osb1", [128, 1])
        din("w_out0", [HID, D])
        din("wg_half", [HID, D]); din("wu_half", [HID, D])
        din("wd_half", [512, INNER])
        din("w_in", [D, 2 * DI])
        din("convw", [128, 64]); din("convb", [128, 16])
        din("w_xp", [DI, 96]); din("w_dt", [DTR, DI]); din("bdt_c", [128, 16])
        din("alog_c", [128, 128]); din("dskip_c", [128, 8])
        din("w_out1", [DI, D]); din("w_rt", [D, NE])
        din("w1", [NE, D, INNER]); din("wo", [INNER, D])
        din("b_hid_r", [1, 2 * HID]); din("b_qk_r", [1, QK])
        din("b_out0_r", [1, D]); din("b1_r", [NE, INNER]); din("bo_r", [1, D])
        din("ln_g_r", [1, D]); din("ln_b_r", [1, D])
        din("ln1_g_r", [1, D]); din("ln1_b_r", [1, D])
        din("ln2_g_r", [1, D]); din("ln2_b_r", [1, D])
        din("id_bf", [128, 128], BF16); din("id_f32", [128, 128], F32)
        din("onesb_r", [1, 128], BF16); din("onesf_r", [1, 128], F32)
        din("onesf_c", [128, 1], F32)
        din("invq2_c", [128, NT]); din("dmask", [128, 2048], BF16)
        out_sh = nc.dram_tensor("out_sh", [TS, D], F32,
                                kind="ExternalOutput").ap()

        dsh("s_v", [T, HID], BF16); dsh("s_kT", [QK, T], BF16)
        dsh("s_wsum", [2, 16])
        dsh("s_wgT", [D, INNER], BF16); dsh("s_wuT", [D, INNER], BF16)
        dsh("s_wdT", [INNER, D], BF16)
        dsh("s_u", [DI, T + 3]); dsh("s_z", [DI, T])
        dsh("s_dlt", [DI, T]); dsh("s_uc", [DI, T])
        dsh("s_bc", [32, T]); dsh("s_g", [DI, T], BF16)
        dlo("l_gate", [NT, 128, HID], BF16)
        dlo("l_xa", [NT, 128, D]); dlo("l_x2", [NT, 128, D])
        dlo("l_xa2", [NT, 128, D])
        dlo("l_glin", [NT, 128, INNER], BF16)
        dlo("l_ulin", [NT, 128, INNER], BF16)
        dlo("l_xq2T", [NT, 128, INNER], BF16)
        dlo("l_x3", [NT, 128, D]); dlo("l_hs", [NT, 128, D])
        dlo("l_part", [NT, 128, D])
        dlo("l_inter", [16, NT, 128, 256], BF16)
        for i in range(8):
            dlo(f"cc_i{i}", [1, 8]); dlo(f"cc_o{i}", [1, 8])
        self.dr = dr

        for s in ["sL", "sLg", "sS", "sT", "sV", "sC", "sB"]:
            self.sem(s)

        t = lambda nm, sh, dt=F32: self.enter(nc.sbuf_tensor(nm, list(sh), dt))
        self.cst = t("cst", [128, 256], BF16)
        self.cstf = t("cstf", [128, 256])
        self.onesbr = t("onesbr", [1, 128], BF16)
        self.onesfr = t("onesfr", [1, 128])
        self.cols = t("cols", [128, 64])
        self.rowb = t("rowb", [1, 8192], BF16)
        self.rowf = t("rowf", [1, 2048])
        self.srow = t("srow", [1, 64])
        self.dmsk = t("dmsk", [128, 2048], BF16)
        self.carry = t("carry", [128, 130])
        self.gbc = t("gbc", [128, 2 * D])
        self.tA = t("tA", [128, 1024])
        self.tB = t("tB", [128, 1024])
        self.tC = t("tC", [128, 1024])
        self.tDmp = t("tDmp", [128, 1024])
        self.bfA = t("bfA", [128, 4096], BF16)
        self.bfB = t("bfB", [128, 4096], BF16)
        self.wsm = t("wsm", [128, 1024], BF16)
        self.ARf = t("ARf", [128, 19456])
        self.ARb = t("ARb", [128, 29696], BF16)
        p = lambda nm, sh, dt=F32: self.enter(nc.psum_tensor(nm, list(sh), dt))
        self.psA = p("psA", [128, 2048])
        self.psB = p("psB", [128, 512])
        self.psC = p("psC", [128, 512])
        self.psTa = p("psTa", [128, 1024], BF16)
        self.psTb = p("psTb", [128, 1024], BF16)

        self.g = self._Rec(self, "g")
        self.tn = self._Rec(self, "tn")
        self.sc = self._Rec(self, "sc")
        self.ve = self._Rec(self, "ve")
        self.syp = self._SyP(self)

        self.emit(out_sh)

        blk = self.enter(nc.Block())
        self.replay_all(blk)
        self.close()
        return nc

    def barrier(self, idx):
        g, dr = self.g, self.dr
        self.wS(g)
        g.memset(self.srow[0:1, 32:40], 1.0)
        g.drain()
        g.dma_start(out=dr[f"cc_i{idx}"][0:1, 0:8],
                    in_=self.srow[0:1, 32:40]).then_inc(self.sLg, 16)
        self.inc("sLg", 16)
        g.wait_ge(self.sLg, self.S["sLg"])
        g.collective_compute(
            "AllReduce", ALU.add,
            replica_groups=[[0, 1], [2, 3], [4, 5], [6, 7]],
            ins=[dr[f"cc_i{idx}"]], outs=[dr[f"cc_o{idx}"]],
        ).then_inc(self.sB, 1)
        self.inc("sB")
        for e in [self.syp, self.g, self.tn, self.sc, self.ve]:
            e.wait_ge(self.sB, self.S["sB"])

    def mm_done(self):
        q = self.tn._q()
        q[-1][3].append((self.sT, 1))
        return self.inc("sT")

    def evac(self, out_ap, psum_ap, func=ACT.Copy, scale=1.0, bias=0.0,
             tcount=None):
        sc = self.sc
        if func == ACT.Copy:
            bias = float(bias)
            if isinstance(scale, (float, np.floating)):
                scale = float(scale)
        else:
            if isinstance(scale, (float, np.floating)):
                scale = np.float32(scale)
            if isinstance(bias, (float, np.floating)):
                bias = np.float32(bias)
        sc.wait_ge(self.sT, tcount if tcount is not None else self.S["sT"])
        sc.activation(out=out_ap, in_=psum_ap, func=func, scale=scale,
                      bias=bias)
        sc.drain().then_inc(self.sC, 1)
        return self.inc("sC")

    def ln_stats(self, x_ap, eps, n=1024):
        sc, ve = self.sc, self.ve
        mu = self.cols[:, 20:21]
        rc = self.cols[:, 21:22]
        tc = self.cols[:, 22:23]
        sc.wait_ge(self.sV, self.S["sV"])
        sc.activation(out=self.tDmp[:, 0:n], in_=x_ap, func=ACT.Copy,
                      accum_out=mu)
        sc.drain()
        sc.activation(out=self.tDmp[:, 0:n], in_=x_ap, func=ACT.Square,
                      accum_out=tc)
        sc.drain().then_inc(self.sC, 1)
        ve.wait_ge(self.sC, self.inc("sC"))
        ve.tensor_scalar(out=mu, in0=mu, scalar1=1.0 / n, scalar2=None,
                         op0=ALU.mult)
        ve.drain()
        ve.tensor_scalar(out=tc, in0=tc, scalar1=1.0 / n, scalar2=None,
                         op0=ALU.mult)
        ve.drain()
        ve.scalar_tensor_tensor(out=rc, in0=mu, scalar=-1.0, in1=mu,
                                op0=ALU.mult, op1=ALU.mult)
        ve.drain()
        ve.tensor_tensor(out=tc, in0=tc, in1=rc, op=ALU.add)
        ve.drain().then_inc(self.sV, 1)
        sc.wait_ge(self.sV, self.inc("sV"))
        sc.activation(out=rc, in_=tc, func=ACT.Sqrt, bias=np.float32(eps), scale=np.float32(1.0))
        sc.drain().then_inc(self.sC, 1)
        ve.wait_ge(self.sC, self.inc("sC"))
        ve.reciprocal(out=rc, in_=rc)
        ve.drain()
        return mu, rc

    def ln_norm(self, x_ap, out_ap, mu, rc):
        self.ve.tensor_scalar(out=out_ap, in0=x_ap, scalar1=mu, scalar2=rc,
                              op0=ALU.subtract, op1=ALU.mult)
        self.ve.drain()

    def bcast_f32(self, row_ap, out_ap, n):
        tn = self.tn
        tn.wait_ge(self.sC, self.S["sC"])
        tn.matmul(out=self.psB[:, 0:n], lhsT=self.onesfr[0:1, :],
                  rhs=row_ap, start=True, stop=True)
        self.evac(out_ap, self.psB[:, 0:n], tcount=self.mm_done())

    def load_gbc(self, grow, brow):
        for hh in range(2):
            self.bcast_f32(grow[0:1, hh * 512:(hh + 1) * 512],
                           self.gbc[:, hh * 512:(hh + 1) * 512], 512)
            self.bcast_f32(brow[0:1, hh * 512:(hh + 1) * 512],
                           self.gbc[:, D + hh * 512:D + (hh + 1) * 512], 512)

    def transpose_to(self, src_fn, nblk, dst_fn, f32=False):
        tn, sc = self.tn, self.sc
        ident = self.cstf[:, 0:128] if f32 else self.cst[:, 0:128]
        pos, bank = 0, 0
        grp = 4 if f32 else 8
        while pos < nblk:
            cnt = min(grp, nblk - pos)
            if f32:
                ps = self.psA[:, bank * 512:(bank + 1) * 512]
            else:
                ps = (self.psTa, self.psTb)[bank][:, 0:1024]
            tn.wait_ge(self.sC, self.S["sC"])
            for i in range(cnt):
                tn.transpose(out=ps[:, i * 128:(i + 1) * 128],
                             in_=src_fn(pos + i), identity=ident)
            tcnt = self.mm_done()
            sc.wait_ge(self.sT, tcnt)
            for i in range(cnt):
                sc.copy(out=dst_fn(pos + i), in_=ps[:, i * 128:(i + 1) * 128])
            sc.drain().then_inc(self.sC, 1)
            self.inc("sC")
            pos += cnt
            bank ^= 1

    def emit(self, out_sh):
        dr = self.dr
        sy, g, tn, sc, ve = self.syp, self.g, self.tn, self.sc, self.ve
        xres = self.ARf[:, 0:16384]

        xsv = dr["x_sh"].rearrange("(a p) m -> p a m", p=128)
        for gg in range(4):
            self.dma(sy, xres[:, gg * 4096:(gg + 1) * 4096]
                     .rearrange("p (a m) -> p a m", a=4),
                     xsv[:, gg * 4:(gg + 1) * 4, :], "sL")
        self.dma(sy, self.cstf[:, 0:128], dr["id_f32"], "sL")
        self.dma(sy, self.cstf[:, 128:129], dr["onesf_c"], "sL")
        self.dma(sy, self.onesfr[:, :], dr["onesf_r"], "sL")
        self.dma(sy, self.dmsk[:, :], dr["dmask"], "sL")
        for i, nm in enumerate(["osg0", "osb0", "osg1", "osb1"]):
            self.dma(sy, self.cols[:, i:i + 1], dr[nm], "sL")
        self.dma(sy, self.cols[:, 4:4 + NT], dr["invq2_c"], "sL")
        self.dma(sy, self.rowf[0:1, 0:D], dr["ln_g_r"], "sL")
        self.dma(sy, self.rowf[0:1, D:2 * D], dr["ln_b_r"], "sL")
        self.dma(g, self.cst[:, 0:128], dr["id_bf"], "sL")
        self.dma(g, self.onesbr[:, :], dr["onesb_r"], "sL")
        self.dma(g, self.wsm[:, 0:8 * QK].rearrange("p (c o) -> p c o", c=8),
                 dr["w_qk"].rearrange("(c p) o -> p c o", p=128), "sL")
        self.dma(g, self.rowb[0:1, 0:2 * HID], dr["b_hid_r"], "sL")
        self.dma(g, self.rowb[0:1, 4096:4096 + QK], dr["b_qk_r"], "sL")
        self.dma(g, self.rowb[0:1, 4224:4224 + D], dr["b_out0_r"], "sL")
        self.dma(g, self.rowb[0:1, 5504:5504 + D], dr["bo_r"], "sL")
        for e in [tn, sc, ve]:
            self.wL(e)
        if self.nphase >= 0:
            self.load_gbc(self.rowf[0:1, 0:D], self.rowf[0:1, D:2 * D])

        if self.nphase <= 0:
            return self.finish(out_sh, xres)
        self.phase_a1()
        if self.nphase == 1:
            return self.finish(out_sh, xres)
        self.barrier(0)
        self.phase_prep()
        self.barrier(1)
        self.phase_a2()
        if self.nphase == 2:
            return self.finish(out_sh, xres)
        self.phase_a3()
        if self.nphase == 3:
            return self.finish(out_sh, xres)
        self.phase_b()
        self.barrier(2)
        self.phase_c()
        self.barrier(3)
        if self.nphase == 4:
            return self.finish(out_sh, xres, "l_x2")
        self.phase_d()
        self.barrier(4)
        if self.nphase == 5:
            return self.finish(out_sh, xres, "l_x2")
        self.phase_e(out_sh)

    def finish(self, out_sh, xres, reload=None):
        sy = self.syp
        sy.wait_ge(self.sV, self.S["sV"])
        sy.wait_ge(self.sC, self.S["sC"])
        sy.wait_ge(self.sS, self.S["sS"])
        if reload:
            xi = self.dr[reload].rearrange("a p m -> p a m")
            for gg in range(4):
                self.dma(sy, xres[:, gg * 4096:(gg + 1) * 4096]
                         .rearrange("p (a m) -> p a m", a=4),
                         xi[:, gg * 4:(gg + 1) * 4, :], "sL")
            self.wL(sy)
        xo = out_sh.rearrange("(a p) m -> p a m", p=128)
        for gg in range(4):
            self.dma(sy, xo[:, gg * 4:(gg + 1) * 4, :],
                     xres[:, gg * 4096:(gg + 1) * 4096]
                     .rearrange("p (a m) -> p a m", a=4), "sS")
        sy.wait_ge(self.sS, self.S["sS"])

    def phase_a1(self):
        dr = self.dr
        sy, g, tn, sc, ve = self.syp, self.g, self.tn, self.sc, self.ve
        xres = self.ARf[:, 0:16384]
        naT = self.ARb[:, 0:16384]
        QT = self.ARb[:, 16384:18432]
        kTl = self.ARb[:, 18432:20480]
        wch = [self.ARb[:, 20480:24576], self.ARb[:, 24576:28672]]
        nabf = self.bfA[:, 1024:2048]
        qkbf = self.bfB[:, 2048:2176]
        qkT = self.bfB[:, 2176:2304]
        evb = self.bfB[:, 2304:2816]
        psB, psA = self.psB, self.psA

        for tt in range(NT):
            x_ap = xres[:, tt * 1024:(tt + 1) * 1024]
            xa = self.tA[:, 0:1024]
            self.wS(ve)
            mu, rc = self.ln_stats(x_ap, 1e-5)
            self.ln_norm(x_ap, xa, mu, rc)
            ve.drain().then_inc(self.sV, 1)
            sy.wait_ge(self.sV, self.inc("sV"))
            self.dma(sy, dr["l_xa"][tt], xa, "sS")
            mu, rc = self.ln_stats(xa, 1e-5)
            self.ln_norm(xa, self.tB[:, 0:1024], mu, rc)
            ve.tensor_tensor(out=self.tB[:, 0:1024], in0=self.tB[:, 0:1024],
                             in1=self.gbc[:, 0:1024], op=ALU.mult)
            ve.drain()
            ve.tensor_tensor(out=nabf, in0=self.tB[:, 0:1024],
                             in1=self.gbc[:, 1024:2048], op=ALU.add)
            ve.drain().then_inc(self.sV, 1)
            tn.wait_ge(self.sV, self.inc("sV"))
            self.transpose_to(lambda i: nabf[:, i * 128:(i + 1) * 128], 8,
                              lambda i: naT[:, tt * 1024 + i * 128:
                                            tt * 1024 + (i + 1) * 128])
            tn.wait_ge(self.sC, self.S["sC"])
            tn.matmul(out=psB[:, 0:QK], lhsT=self.onesbr[0:1, :],
                      rhs=self.rowb[0:1, 4096:4096 + QK], start=True, stop=False)
            for ic in range(8):
                tn.matmul(out=psB[:, 0:QK],
                          lhsT=naT[:, tt * 1024 + ic * 128:
                                    tt * 1024 + (ic + 1) * 128],
                          rhs=self.wsm[:, ic * QK:(ic + 1) * QK],
                          start=False, stop=(ic == 7))
            self.evac(qkbf, psB[:, 0:QK], func=ACT.Silu, tcount=self.mm_done())
            self.transpose_to(lambda i: qkbf, 1, lambda i: qkT)
            ve.wait_ge(self.sC, self.S["sC"])
            ve.tensor_scalar(out=QT[:, tt * 128:(tt + 1) * 128], in0=qkT,
                             scalar1=self.cols[:, 0:1], scalar2=self.cols[:, 1:2],
                             op0=ALU.mult, op1=ALU.add)
            ve.drain()
            ve.tensor_scalar(out=kTl[:, tt * 128:(tt + 1) * 128], in0=qkT,
                             scalar1=self.cols[:, 2:3], scalar2=self.cols[:, 3:4],
                             op0=ALU.mult, op1=ALU.add)
            ve.drain().then_inc(self.sV, 1)
            self.inc("sV")

        sy.wait_ge(self.sV, self.S["sV"])
        for bi in range(4):
            self.sydma2(lambda h, bi=bi: dr["s_kT"][:, QBLOCKS[h][bi] * 512:
                                                    QBLOCKS[h][bi] * 512 + 512],
                        kTl[:, bi * 512:(bi + 1) * 512], "sS")

        whid = dr["w_hid"].rearrange("(c p) o -> p c o", p=128)
        for oc in range(8):
            buf = wch[oc % 2]
            if oc >= 2:
                g.wait_ge(self.sT, self.S["sT"])
            self.dma(g, buf.rearrange("p (c o) -> p c o", c=8),
                     whid[:, :, oc * 512:(oc + 1) * 512], "sL")
            self.wL(tn)
            for tt in range(NT):
                bid = ("psA", tt % 4)
                pslot = psA[:, (tt % 4) * 512:((tt % 4) + 1) * 512]
                if bid in self.bank_ev:
                    tn.wait_ge(self.sC, self.bank_ev[bid])
                tn.matmul(out=pslot, lhsT=self.onesbr[0:1, :],
                          rhs=self.rowb[0:1, oc * 512:(oc + 1) * 512],
                          start=True, stop=False)
                for ic in range(8):
                    tn.matmul(out=pslot,
                              lhsT=naT[:, tt * 1024 + ic * 128:
                                        tt * 1024 + (ic + 1) * 128],
                              rhs=buf[:, ic * 512:(ic + 1) * 512],
                              start=False, stop=(ic == 7))
                tcnt = self.mm_done()
                self.wS(sc)
                self.bank_ev[bid] = self.evac(evb, pslot, func=ACT.Silu,
                                              tcount=tcnt)
                sy.wait_ge(self.sC, self.S["sC"])
                if oc < 4:
                    self.sydma2(lambda h, tt=tt, oc=oc:
                                dr["s_v"][tglob(tt, h):tglob(tt, h) + 128,
                                          oc * 512:(oc + 1) * 512],
                                evb, "sS")
                else:
                    self.dma(sy, dr["l_gate"][tt][:, (oc - 4) * 512:
                                                  (oc - 3) * 512], evb, "sS")

        acc, red = self.cols[:, 24:25], self.cols[:, 25:26]
        for wi, nm in enumerate(["wg_half", "wu_half", "wd_half"]):
            ve.wait_ge(self.sT, self.S["sT"])
            ve.memset(acc, 0.0)
            ve.drain()
            if wi < 2:
                wap = dr[nm].rearrange("(a p) m -> p a m", p=128)
                loads = [(wap[:, ti:ti + 1, :],) for ti in range(16)]
            else:
                loads = [(dr[nm][ti * 128:(ti + 1) * 128,
                                 q4 * 1024:(q4 + 1) * 1024],)
                         for ti in range(4) for q4 in range(4)]
            for (src_ap,) in loads:
                ve.drain().then_inc(self.sV, 1)
                sy.wait_ge(self.sV, self.inc("sV"))
                self.dma(sy, self.tA[:, 0:1024], src_ap, "sL")
                self.wL(ve)
                ve.tensor_reduce(out=red, in_=self.tA[:, 0:1024], axis=AX,
                                 op=ALU.add, apply_absolute_value=True)
                ve.drain()
                ve.tensor_tensor(out=acc, in0=acc, in1=red, op=ALU.add)
                ve.drain()
            ve.drain().then_inc(self.sV, 1)
            tn.wait_ge(self.sV, self.inc("sV"))
            tn.wait_ge(self.sC, self.S["sC"])
            tn.matmul(out=self.psB[0:1, 0:1], lhsT=acc,
                      rhs=self.cstf[:, 128:129], start=True, stop=True)
            self.evac(self.srow[0:1, wi:wi + 1], self.psB[0:1, 0:1],
                      tcount=self.mm_done())
        sy.wait_ge(self.sC, self.S["sC"])
        self.sydma2(lambda h: dr["s_wsum"][h:h + 1, 0:3],
                    self.srow[0:1, 0:3], "sS")

    def phase_prep(self):
        dr = self.dr
        sy, g, tn, sc, ve = self.syp, self.g, self.tn, self.sc, self.ve
        self.dma(g, self.srow[0:1, 8:11], dr["s_wsum"][0:1, 0:3], "sL")
        self.dma(g, self.srow[0:1, 16:19], dr["s_wsum"][1:2, 0:3], "sL")
        self.wL(ve)
        ve.tensor_tensor(out=self.srow[0:1, 12:15], in0=self.srow[0:1, 8:11],
                         in1=self.srow[0:1, 16:19], op=ALU.add)
        ve.drain()
        ve.tensor_scalar(out=self.srow[0:1, 12:15], in0=self.srow[0:1, 12:15],
                         scalar1=1.0 / (4096.0 * 1024.0), scalar2=1e-5,
                         op0=ALU.mult, op1=ALU.max)
        ve.drain()
        ve.reciprocal(out=self.srow[0:1, 20:23], in_=self.srow[0:1, 12:15])
        ve.drain().then_inc(self.sV, 1)
        tn.wait_ge(self.sV, self.inc("sV"))
        for j in range(3):
            self.bcast_f32(self.srow[0:1, 12 + j:13 + j],
                           self.cols[:, 26 + j:27 + j], 1)
            self.bcast_f32(self.srow[0:1, 20 + j:21 + j],
                           self.cols[:, 29 + j:30 + j], 1)
        qt = [self.ARb[:, j * 4096:(j + 1) * 4096] for j in range(4)]
        ev = self.ARb[:, 16384:16896]
        scratch = self.ARf[:, 16384:18432]
        for (nm, dst, O, I, swc) in [("wg_half", "s_wgT", 2048, 1024, 29),
                                     ("wu_half", "s_wuT", 2048, 1024, 30),
                                     ("wd_half", "s_wdT", 512, 4096, 31)]:
            for og in range(O // 512):
                for j4 in range(4):
                    nh = 1 if I == 1024 else 2
                    for hh in range(nh):
                        w = I // nh
                        src = scratch[:, 0:w]
                        ve.drain().then_inc(self.sV, 1)
                        sy.wait_ge(self.sV, self.inc("sV"))
                        self.dma(sy, src,
                                 dr[nm][og * 512 + j4 * 128:
                                        og * 512 + (j4 + 1) * 128,
                                        hh * w:(hh + 1) * w], "sL")
                        self.wL(ve)
                        ve.wait_ge(self.sT, self.S["sT"])
                        self._quant3(src, qt[j4][:, hh * w:(hh + 1) * w], swc)
                ve.drain().then_inc(self.sV, 1)
                tn.wait_ge(self.sV, self.inc("sV"))
                for ic in range(I // 128):
                    ps = (self.psTa, self.psTb)[ic % 2][:, 0:512]
                    tn.wait_ge(self.sC, self.S["sC"])
                    for j4 in range(4):
                        tn.transpose(out=ps[:, j4 * 128:(j4 + 1) * 128],
                                     in_=qt[j4][:, ic * 128:(ic + 1) * 128],
                                     identity=self.cst[:, 0:128])
                    tcnt = self.mm_done()
                    sc.wait_ge(self.sT, tcnt)
                    self.wS(sc)
                    sc.copy(out=ev, in_=ps[:, 0:512])
                    sc.drain().then_inc(self.sC, 1)
                    self.inc("sC")
                    sy.wait_ge(self.sC, self.S["sC"])
                    self.sydma2(
                        lambda h, dst=dst, ic=ic, og=og, O=O:
                        dr[dst][ic * 128:(ic + 1) * 128,
                                ((h * 2048 + og * 512) if O == 2048
                                 else h * 512):
                                ((h * 2048 + og * 512) if O == 2048
                                 else h * 512) + 512],
                        ev, "sS")

    def _quant3(self, src, dst_bf, swc):
        ve = self.ve
        ve.tensor_scalar(out=src, in0=src, scalar1=self.cols[:, swc:swc + 1],
                         scalar2=MAGIC, op0=ALU.mult, op1=ALU.add)
        ve.drain()
        ve.tensor_scalar(out=src, in0=src, scalar1=MAGIC, scalar2=1.0,
                         op0=ALU.subtract, op1=ALU.min)
        ve.drain()
        ve.tensor_scalar(out=dst_bf, in0=src, scalar1=-1.0, scalar2=None,
                         op0=ALU.max)
        ve.drain()

    def phase_a2(self):
        dr = self.dr
        sy, g, tn, sc, ve = self.syp, self.g, self.tn, self.sc, self.ve
        xres = self.ARf[:, 0:16384]
        ogf = self.ARf[:, 16384:18432]
        xat = self.ARf[:, 18432:19456]
        wout0 = self.ARb[:, 0:16384]
        QT = self.ARb[:, 16384:18432]
        kTf = self.ARb[:, 18432:22528]
        vsl = [self.ARb[:, 22528:24576], self.ARb[:, 24576:26624],
               self.bfB[:, 2048:4096], self.bfA[:, 2048:4096]]
        ogT = self.ARb[:, 26624:28672]
        attn = self.ARb[:, 28672:29696]
        og = self.bfB[:, 0:2048]
        gatet = self.bfA[:, 0:2048]

        self.dma(g, wout0.rearrange("p (c o) -> p c o", c=16),
                 dr["w_out0"].rearrange("(c p) o -> p c o", p=128), "sL")
        self.dma(g, kTf, dr["s_kT"], "sL")
        for e in [tn, sc, ve, sy]:
            self.wL(e)

        for h in self.role_region([tn, sc, ve]):
            self._rmq = []
            for tt in range(NT):
                gb = QBLOCKS[h][tt // 4]
                nk = gb * 4 + (tt % 4) + 1
                sy.wait_ge(self.sV, self.S["sV"])
                self.dma(sy, gatet, dr["l_gate"][tt], "sL")
                self.dma(sy, xat, dr["l_xa"][tt], "sL")
                first_av = True
                for rnd in range((nk + 1) // 2):
                    ks = [k for k in (2 * rnd, 2 * rnd + 1) if k < nk]
                    if len(self._rmq) >= 2:
                        sy.wait_ge(self.sT, self._rmq[-2])
                    for kj in ks:
                        self.dma(sy, vsl[kj % 4],
                                 dr["s_v"][kj * 128:(kj + 1) * 128, :], "sL")
                    self.wL(tn)
                    for kj in ks:
                        ps = (self.psB, self.psC)[kj % 2]
                        bid = ("ps", kj % 2)
                        if bid in self.bank_ev:
                            tn.wait_ge(self.sC, self.bank_ev[bid])
                        tn.matmul(out=ps[:, 0:128],
                                  lhsT=kTf[:, kj * 128:(kj + 1) * 128],
                                  rhs=QT[:, tt * 128:(tt + 1) * 128],
                                  start=True, stop=True)
                        tcnt = self.mm_done()
                        sc.wait_ge(self.sT, tcnt)
                        sc.activation(out=self.tA[:, 0:128], in_=ps[:, 0:128],
                                      func=ACT.Relu)
                        sc.drain()
                        sl = attn[:, (kj % 8) * 128:(kj % 8 + 1) * 128]
                        sc.activation(out=sl, in_=self.tA[:, 0:128],
                                      func=ACT.Square)
                        sc.drain().then_inc(self.sC, 1)
                        self.bank_ev[bid] = self.inc("sC")
                        if gb * 4 <= kj < gb * 4 + 4:
                            ve.wait_ge(self.sC, self.S["sC"])
                            dj = kj - gb * 4
                            ve.tensor_tensor(
                                out=sl, in0=sl,
                                in1=self.dmsk[:, dj * 512 + (tt % 4) * 128:
                                              dj * 512 + (tt % 4 + 1) * 128],
                                op=ALU.mult)
                            ve.drain().then_inc(self.sV, 1)
                            self.inc("sV")
                            tn.wait_ge(self.sV, self.S["sV"])
                        else:
                            tn.wait_ge(self.sC, self.S["sC"])
                    if first_av:
                        for q in range(4):
                            bid = ("psA", q)
                            if bid in self.bank_ev:
                                tn.wait_ge(self.sC, self.bank_ev[bid])
                        first_av = False
                    for kj in ks:
                        for oc in range(4):
                            tn.matmul(out=self.psA[:, oc * 512:(oc + 1) * 512],
                                      lhsT=attn[:, (kj % 8) * 128:
                                                (kj % 8 + 1) * 128],
                                      rhs=vsl[kj % 4][:, oc * 512:
                                                      (oc + 1) * 512],
                                      start=(kj == 0), stop=(kj == nk - 1))
                    tcnt = self.mm_done()
                    self._rmq.append(tcnt)
                c = self.evac(ogf, self.psA[:, 0:2048], func=ACT.Copy,
                              scale=self.cols[:, 4 + tt:5 + tt], tcount=tcnt)
                for q in range(4):
                    self.bank_ev[("psA", q)] = c
                ve.wait_ge(self.sC, c)
                self.wL(ve)
                ve.tensor_tensor(out=og, in0=ogf, in1=gatet, op=ALU.mult)
                ve.drain().then_inc(self.sV, 1)
                tn.wait_ge(self.sV, self.inc("sV"))
                self.transpose_to(lambda i: og[:, i * 128:(i + 1) * 128], 16,
                                  lambda i: ogT[:, i * 128:(i + 1) * 128])
                tn.wait_ge(self.sC, self.S["sC"])
                for half in range(2):
                    ps = (self.psB, self.psC)[half]
                    bid = ("ps", half)
                    if bid in self.bank_ev:
                        tn.wait_ge(self.sC, self.bank_ev[bid])
                    tn.matmul(out=ps[:, 0:512], lhsT=self.onesbr[0:1, :],
                              rhs=self.rowb[0:1, 4224 + half * 512:
                                            4224 + (half + 1) * 512],
                              start=True, stop=False)
                    for c16 in range(16):
                        tn.matmul(out=ps[:, 0:512],
                                  lhsT=ogT[:, c16 * 128:(c16 + 1) * 128],
                                  rhs=wout0[:, c16 * 1024 + half * 512:
                                            c16 * 1024 + (half + 1) * 512],
                                  start=False, stop=(c16 == 15))
                tcnt = self.mm_done()
                self.bank_ev[("ps", 0)] = self.evac(
                    self.tA[:, 0:512], self.psB[:, 0:512], tcount=tcnt)
                self.bank_ev[("ps", 1)] = self.evac(
                    self.tA[:, 512:1024], self.psC[:, 0:512], tcount=tcnt)
                ve.wait_ge(self.sC, self.S["sC"])
                ve.tensor_tensor(out=self.tA[:, 0:1024], in0=self.tA[:, 0:1024],
                                 in1=xat, op=ALU.add)
                ve.drain().then_inc(self.sV, 1)
                self.inc("sV")
                mu, rc = self.ln_stats(self.tA[:, 0:1024], 1e-5)
                self.ln_norm(self.tA[:, 0:1024], self.tB[:, 0:1024], mu, rc)
                ve.tensor_tensor(out=xres[:, tt * 1024:(tt + 1) * 1024],
                                 in0=xres[:, tt * 1024:(tt + 1) * 1024],
                                 in1=self.tB[:, 0:1024], op=ALU.add)
                ve.drain().then_inc(self.sV, 1)
                self.inc("sV")

    def phase_a3(self):
        dr = self.dr
        sy, g, tn, sc, ve = self.syp, self.g, self.tn, self.sc, self.ve
        g.wait_ge(self.sT, self.S["sT"])
        g.wait_ge(self.sV, self.S["sV"])
        xres = self.ARf[:, 0:16384]
        xqT = self.ARb[:, 0:16384]
        wgc = self.ARb[:, 16384:20480]
        wuc = self.ARb[:, 20480:24576]
        xqbf = self.ARb[:, 24576:25600]
        evg = self.ARb[:, 25600:26112]
        evu = self.ARb[:, 26112:26624]
        for tt in range(NT):
            x_ap = xres[:, tt * 1024:(tt + 1) * 1024]
            mu, rc = self.ln_stats(x_ap, 1e-5)
            self.ln_norm(x_ap, self.tA[:, 0:1024], mu, rc)
            ve.reduce_max(out=self.cols[:, 23:24], in_=self.tA[:, 0:1024],
                          axis=AX, apply_absolute_value=True)
            ve.drain()
            ve.tensor_scalar(out=self.cols[:, 23:24], in0=self.cols[:, 23:24],
                             scalar1=1e-5, scalar2=None, op0=ALU.max)
            ve.drain()
            ve.reciprocal(out=self.cols[:, 40:41], in_=self.cols[:, 23:24])
            ve.drain()
            ve.tensor_scalar(out=self.cols[:, 40:41], in0=self.cols[:, 40:41],
                             scalar1=127.0, scalar2=None, op0=ALU.mult)
            ve.drain()
            ve.tensor_scalar(out=self.cols[:, 41:42], in0=self.cols[:, 23:24],
                             scalar1=1.0 / 127.0, scalar2=None, op0=ALU.mult)
            ve.drain()
            ve.tensor_tensor(out=self.carry[:, tt:tt + 1],
                             in0=self.cols[:, 41:42], in1=self.cols[:, 26:27],
                             op=ALU.mult)
            ve.drain()
            ve.tensor_tensor(out=self.carry[:, 16 + tt:17 + tt],
                             in0=self.cols[:, 41:42], in1=self.cols[:, 27:28],
                             op=ALU.mult)
            ve.drain()
            ve.tensor_scalar(out=self.tA[:, 0:1024], in0=self.tA[:, 0:1024],
                             scalar1=self.cols[:, 40:41], scalar2=MAGIC,
                             op0=ALU.mult, op1=ALU.add)
            ve.drain()
            ve.tensor_scalar(out=xqbf, in0=self.tA[:, 0:1024], scalar1=MAGIC,
                             scalar2=None, op0=ALU.subtract)
            ve.drain().then_inc(self.sV, 1)
            tn.wait_ge(self.sV, self.inc("sV"))
            self.transpose_to(lambda i: xqbf[:, i * 128:(i + 1) * 128], 8,
                              lambda i: xqT[:, tt * 1024 + i * 128:
                                            tt * 1024 + (i + 1) * 128])

        wgT = dr["s_wgT"].rearrange("(c p) o -> p c o", p=128)
        wuT = dr["s_wuT"].rearrange("(c p) o -> p c o", p=128)
        for oc in range(8):
            if oc >= 1:
                g.wait_ge(self.sT, self.S["sT"])
            self.dma(g, wgc.rearrange("p (c o) -> p c o", c=8),
                     wgT[:, :, oc * 512:(oc + 1) * 512], "sL")
            self.dma(g, wuc.rearrange("p (c o) -> p c o", c=8),
                     wuT[:, :, oc * 512:(oc + 1) * 512], "sL")
            self.wL(tn)
            tn.wait_ge(self.sC, self.S["sC"])
            for tt in range(NT):
                pg = 2 * (tt % 2)
                for w2 in range(2):
                    bk = pg + w2
                    ps = self.psA[:, bk * 512:(bk + 1) * 512]
                    bid = ("psA", bk)
                    if bid in self.bank_ev:
                        tn.wait_ge(self.sC, self.bank_ev[bid])
                    wbuf = (wgc, wuc)[w2]
                    for ic in range(8):
                        tn.matmul(out=ps,
                                  lhsT=xqT[:, tt * 1024 + ic * 128:
                                           tt * 1024 + (ic + 1) * 128],
                                  rhs=wbuf[:, ic * 512:(ic + 1) * 512],
                                  start=(ic == 0), stop=(ic == 7))
                tcnt = self.mm_done()
                self.wS(sc)
                evg2 = (evg, self.bfA[:, 0:512])[tt % 2]
                evu2 = (evu, self.bfA[:, 512:1024])[tt % 2]
                self.bank_ev[("psA", pg)] = self.evac(
                    evg2, self.psA[:, pg * 512:(pg + 1) * 512], func=ACT.Silu,
                    scale=self.carry[:, tt:tt + 1], tcount=tcnt)
                self.bank_ev[("psA", pg + 1)] = self.evac(
                    evu2, self.psA[:, (pg + 1) * 512:(pg + 2) * 512],
                    func=ACT.Copy,
                    scale=self.carry[:, 16 + tt:17 + tt], tcount=tcnt)
                sy.wait_ge(self.sC, self.S["sC"])
                self.dma(sy, dr["l_glin"][tt][:, oc * 512:(oc + 1) * 512],
                         evg2, "sS")
                self.dma(sy, dr["l_ulin"][tt][:, oc * 512:(oc + 1) * 512],
                         evu2, "sS")

        g.wait_ge(self.sT, self.S["sT"])
        glt = self.ARb[:, 16384:20480]
        ult = self.ARb[:, 20480:24576]
        h2 = self.ARb[:, 24576:28672]
        xq2 = self.ARb[:, 16384:20480]
        for tt in range(NT):
            ve.drain().then_inc(self.sV, 1)
            sy.wait_ge(self.sV, self.inc("sV"))
            sy.wait_ge(self.sS, self.S["sS"])
            self.dma(sy, glt, dr["l_glin"][tt], "sL")
            self.dma(sy, ult, dr["l_ulin"][tt], "sL")
            self.wL(ve)
            ve.tensor_tensor(out=h2, in0=glt, in1=ult, op=ALU.mult)
            ve.drain()
            ve.reduce_max(out=self.cols[:, 23:24], in_=h2, axis=AX,
                          apply_absolute_value=True)
            ve.drain()
            ve.tensor_scalar(out=self.cols[:, 23:24], in0=self.cols[:, 23:24],
                             scalar1=1e-5, scalar2=None, op0=ALU.max)
            ve.drain()
            ve.reciprocal(out=self.cols[:, 40:41], in_=self.cols[:, 23:24])
            ve.drain()
            ve.tensor_scalar(out=self.cols[:, 40:41], in0=self.cols[:, 40:41],
                             scalar1=127.0, scalar2=None, op0=ALU.mult)
            ve.drain()
            ve.tensor_scalar(out=self.cols[:, 41:42], in0=self.cols[:, 23:24],
                             scalar1=1.0 / 127.0, scalar2=None, op0=ALU.mult)
            ve.drain()
            ve.tensor_tensor(out=self.carry[:, 32 + tt:33 + tt],
                             in0=self.cols[:, 41:42], in1=self.cols[:, 28:29],
                             op=ALU.mult)
            ve.drain()
            qtmp = self.ARf[:, 16384:18432]
            for qh in range(2):
                ve.tensor_scalar(out=qtmp, in0=h2[:, qh * 2048:(qh + 1) * 2048],
                                 scalar1=self.cols[:, 40:41],
                                 scalar2=MAGIC, op0=ALU.mult, op1=ALU.add)
                ve.drain()
                ve.tensor_scalar(out=xq2[:, qh * 2048:(qh + 1) * 2048],
                                 in0=qtmp, scalar1=MAGIC, scalar2=None,
                                 op0=ALU.subtract)
                ve.drain()
            ve.drain().then_inc(self.sV, 1)
            tn.wait_ge(self.sV, self.inc("sV"))
            self.transpose_to(lambda i: xq2[:, i * 128:(i + 1) * 128], 32,
                              lambda i: ult[:, i * 128:(i + 1) * 128])
            sy.wait_ge(self.sC, self.S["sC"])
            self.dma(sy, dr["l_xq2T"][tt], ult, "sS")

        wd = self.ARb[:, 0:16384]
        xq2s = [self.ARb[:, 16384 + i * 4096:16384 + (i + 1) * 4096]
                for i in range(2)]
        wdT = dr["s_wdT"].rearrange("(c p) o -> p c o", p=128)
        sy.wait_ge(self.sS, self.S["sS"])
        for rnd in range(2):
            g.wait_ge(self.sT, self.S["sT"])
            self.dma(g, wd.rearrange("p (c o) -> p c o", c=16),
                     wdT[:, 16 * rnd:16 * rnd + 16, :], "sL")
            self.wL(tn)
            for tt in range(NT):
                sy.wait_ge(self.sT, self.S["sT"])
                self.dma(sy, xq2s[tt % 2], dr["l_xq2T"][tt], "sL")
                self.wL(tn)
                for half in range(2):
                    ps = (self.psB, self.psC)[half]
                    if ("ps", half) in self.bank_ev:
                        tn.wait_ge(self.sC, self.bank_ev[("ps", half)])
                    for c16 in range(16):
                        cx = 16 * rnd + c16
                        tn.matmul(out=ps[:, 0:512],
                                  lhsT=xq2s[tt % 2][:, cx * 128:(cx + 1) * 128],
                                  rhs=wd[:, c16 * 1024 + half * 512:
                                         c16 * 1024 + (half + 1) * 512],
                                  start=(c16 == 0), stop=(c16 == 15))
                tcnt = self.mm_done()
                sc.wait_ge(self.sV, self.S["sV"])
                self.wS(sc)
                part = self.ARf[:, 16384 + (tt % 2) * 1024:
                                16384 + (tt % 2 + 1) * 1024]
                if rnd == 0:
                    self.bank_ev[("ps", 0)] = self.evac(
                        self.tB[:, 0:512], self.psB[:, 0:512],
                        scale=self.carry[:, 32 + tt:33 + tt], tcount=tcnt)
                    self.bank_ev[("ps", 1)] = self.evac(
                        self.tB[:, 512:1024], self.psC[:, 0:512],
                        scale=self.carry[:, 32 + tt:33 + tt], tcount=tcnt)
                    sy.wait_ge(self.sC, self.S["sC"])
                    self.dma(sy, dr["l_part"][tt], self.tB[:, 0:1024], "sS")
                else:
                    self.bank_ev[("ps", 0)] = self.evac(
                        self.tB[:, 0:512], self.psB[:, 0:512],
                        scale=self.carry[:, 32 + tt:33 + tt], tcount=tcnt)
                    self.bank_ev[("ps", 1)] = self.evac(
                        self.tB[:, 512:1024], self.psC[:, 0:512],
                        scale=self.carry[:, 32 + tt:33 + tt], tcount=tcnt)
                    sy.wait_ge(self.sC, self.S["sC"])
                    self.dma(sy, part, dr["l_part"][tt], "sL")
                    ve.wait_ge(self.sC, self.S["sC"])
                    self.wL(ve)
                    ve.tensor_tensor(out=self.tB[:, 0:1024], in0=part,
                                     in1=self.tB[:, 0:1024], op=ALU.add)
                    ve.drain().then_inc(self.sV, 1)
                    self.inc("sV")
                    mu, rc = self.ln_stats(self.tB[:, 0:1024], 1e-5)
                    self.ln_norm(self.tB[:, 0:1024], self.tC[:, 0:1024], mu, rc)
                    ve.tensor_tensor(out=xres[:, tt * 1024:(tt + 1) * 1024],
                                     in0=xres[:, tt * 1024:(tt + 1) * 1024],
                                     in1=self.tC[:, 0:1024], op=ALU.add)
                    ve.drain().then_inc(self.sV, 1)
                    self.inc("sV")

    def phase_b(self):
        dr = self.dr
        sy, g, tn, sc, ve = self.syp, self.g, self.tn, self.sc, self.ve
        g.wait_ge(self.sT, self.S["sT"])
        g.wait_ge(self.sV, self.S["sV"])
        g.wait_ge(self.sC, self.S["sC"])
        xres = self.ARf[:, 0:16384]
        xnT = self.ARb[:, 0:16384]
        wic = [self.ARb[:, 16384 + i * 1024:16384 + (i + 1) * 1024]
               for i in range(2)]
        xnbf = self.ARb[:, 18432:19456]
        self.dma(g, self.rowf[0:1, 0:D], dr["ln1_g_r"], "sL")
        self.dma(g, self.rowf[0:1, D:2 * D], dr["ln1_b_r"], "sL")
        for e in [tn, sc, ve]:
            self.wL(e)
        self.load_gbc(self.rowf[0:1, 0:D], self.rowf[0:1, D:2 * D])
        ve.memset(self.tA[:, 0:3], 0.0)
        ve.drain().then_inc(self.sV, 1)
        sy.wait_ge(self.sV, self.inc("sV"))
        for dblk in range(16):
            self.sydma2(lambda h, dblk=dblk:
                        (dr["s_u"][dblk * 128:(dblk + 1) * 128, 0:3] if h == 0
                         else dr["l_part"][0][0:128, dblk * 3:dblk * 3 + 3]),
                        self.tA[:, 0:3], "sS")

        for tt in range(NT):
            x_ap = xres[:, tt * 1024:(tt + 1) * 1024]
            self.wS(ve)
            mu, rc = self.ln_stats(x_ap, 1e-5)
            self.ln_norm(x_ap, self.tA[:, 0:1024], mu, rc)
            ve.drain().then_inc(self.sV, 1)
            sy.wait_ge(self.sV, self.inc("sV"))
            self.dma(sy, dr["l_xa2"][tt], self.tA[:, 0:1024], "sS")
            self.dma(sy, dr["l_x2"][tt], x_ap, "sS")
            mu, rc = self.ln_stats(self.tA[:, 0:1024], 1e-5)
            self.ln_norm(self.tA[:, 0:1024], self.tB[:, 0:1024], mu, rc)
            ve.tensor_tensor(out=self.tB[:, 0:1024], in0=self.tB[:, 0:1024],
                             in1=self.gbc[:, 0:1024], op=ALU.mult)
            ve.drain()
            ve.tensor_tensor(out=xnbf, in0=self.tB[:, 0:1024],
                             in1=self.gbc[:, 1024:2048], op=ALU.add)
            ve.drain().then_inc(self.sV, 1)
            tn.wait_ge(self.sV, self.inc("sV"))
            self.transpose_to(lambda i: xnbf[:, i * 128:(i + 1) * 128], 8,
                              lambda i: xnT[:, i * 2048 + tt * 128:
                                            i * 2048 + (tt + 1) * 128])

        winv = dr["w_in"].rearrange("(c p) o -> p c o", p=128)
        for oc in range(32):
            buf = wic[oc % 2]
            if oc >= 2:
                g.wait_ge(self.sT, self.S["sT"])
            self.dma(g, buf.rearrange("p (c o) -> p c o", c=8),
                     winv[:, :, oc * 128:(oc + 1) * 128], "sL")
            self.wL(tn)
            tn.wait_ge(self.sC, self.S["sC"])
            for tc in range(4):
                ps = (self.psB, self.psC)[tc % 2]
                bid = ("ps", tc % 2)
                if bid in self.bank_ev:
                    tn.wait_ge(self.sC, self.bank_ev[bid])
                for ic in range(8):
                    tn.matmul(out=ps[:, 0:512],
                              lhsT=buf[:, ic * 128:(ic + 1) * 128],
                              rhs=xnT[:, ic * 2048 + tc * 512:
                                      ic * 2048 + (tc + 1) * 512],
                              start=(ic == 0), stop=(ic == 7))
                tcnt = self.mm_done()
                self.wS(sc)
                self.bank_ev[bid] = self.evac(
                    self.tA[:, 0:512] if tc % 2 == 0 else self.tB[:, 0:512],
                    ps[:, 0:512], tcount=tcnt)
                src = self.tA[:, 0:512] if tc % 2 == 0 else self.tB[:, 0:512]
                sy.wait_ge(self.sC, self.S["sC"])
                if oc < 16:
                    self.sydma2(lambda h, oc=oc, tc=tc:
                                dr["s_u"][oc * 128:(oc + 1) * 128,
                                          3 + QBLOCKS[h][tc] * 512:
                                          3 + QBLOCKS[h][tc] * 512 + 512],
                                src, "sS")
                else:
                    self.sydma2(lambda h, oc=oc, tc=tc:
                                dr["s_z"][(oc - 16) * 128:(oc - 15) * 128,
                                          QBLOCKS[h][tc] * 512:
                                          QBLOCKS[h][tc] * 512 + 512],
                                src, "sS")

    def phase_c(self):
        dr = self.dr
        sy, g, tn, sc, ve = self.syp, self.g, self.tn, self.sc, self.ve
        g.wait_ge(self.sT, self.S["sT"])
        g.wait_ge(self.sV, self.S["sV"])
        g.wait_ge(self.sC, self.S["sC"])
        wxp = self.ARb[:, 0:1536]
        wdt = self.ARb[0:DTR, 1536:3584]
        ucb = self.ARb[:, 4096:12288]
        dtb = self.ARb[0:DTR, 3584:4096]
        ublk = self.ARf[:, 0:8320]
        self.dma(g, self.tC[:, 0:64], dr["convw"], "sL")
        self.dma(g, self.tC[:, 64:80], dr["convb"], "sL")
        self.dma(g, self.tC[:, 80:96], dr["bdt_c"], "sL")
        self.dma(g, wxp.rearrange("p (c o) -> p c o", c=16),
                 dr["w_xp"].rearrange("(c p) o -> p c o", p=128), "sL")
        self.dma(g, wdt, dr["w_dt"], "sL")
        for e in [tn, sc, ve, sy]:
            self.wL(e)

        for tc in range(4):
            sy.wait_ge(self.sV, self.S["sV"])
            for dblk in range(16):
                self.sydma2(ublk[:, dblk * 520:dblk * 520 + 515],
                            lambda h, dblk=dblk, tc=tc:
                            dr["s_u"][dblk * 128:(dblk + 1) * 128,
                                      QBLOCKS[h][tc] * 512:
                                      QBLOCKS[h][tc] * 512 + 515], "sL")
            self.wL(ve)
            for dblk in range(16):
                u0 = ublk[:, dblk * 520:dblk * 520 + 515]
                self.wS(ve)
                ve.tensor_scalar(out=self.tA[:, 0:512], in0=u0[:, 0:512],
                                 scalar1=self.tC[:, dblk * 4:dblk * 4 + 1],
                                 scalar2=None, op0=ALU.mult)
                ve.drain()
                for k in range(1, 4):
                    ve.scalar_tensor_tensor(
                        out=self.tA[:, 0:512], in0=u0[:, k:k + 512],
                        scalar=self.tC[:, dblk * 4 + k:dblk * 4 + k + 1],
                        in1=self.tA[:, 0:512], op0=ALU.mult, op1=ALU.add)
                    ve.drain()
                ve.drain().then_inc(self.sV, 1)
                sc.wait_ge(self.sV, self.inc("sV"))
                self.wS(sc)
                sc.activation(out=self.tB[:, 0:512], in_=self.tA[:, 0:512],
                              func=ACT.Silu, bias=self.tC[:, 64 + dblk:65 + dblk],
                              scale=np.float32(1.0))
                sc.drain().then_inc(self.sC, 1)
                ve.wait_ge(self.sC, self.inc("sC"))
                ve.tensor_copy(out=ucb[:, dblk * 512:(dblk + 1) * 512],
                               in_=self.tB[:, 0:512])
                ve.drain().then_inc(self.sV, 1)
                self.inc("sV")
                sy.wait_ge(self.sC, self.S["sC"])
                self.sydma2(lambda h, dblk=dblk, tc=tc:
                            dr["s_uc"][dblk * 128:(dblk + 1) * 128,
                                       QBLOCKS[h][tc] * 512:
                                       QBLOCKS[h][tc] * 512 + 512],
                            self.tB[:, 0:512], "sS")
            ve.drain().then_inc(self.sV, 1)
            tn.wait_ge(self.sV, self.inc("sV"))
            bid = ("ps", 0)
            if bid in self.bank_ev:
                tn.wait_ge(self.sC, self.bank_ev[bid])
            for dblk in range(16):
                tn.matmul(out=self.psB[0:96, 0:512],
                          lhsT=wxp[:, dblk * 96:(dblk + 1) * 96],
                          rhs=ucb[:, dblk * 512:(dblk + 1) * 512],
                          start=(dblk == 0), stop=(dblk == 15))
            tcnt = self.mm_done()
            sc.wait_ge(self.sT, tcnt)
            sc.copy(out=dtb, in_=self.psB[0:DTR, 0:512])
            sc.drain()
            sc.copy(out=self.tA[0:32, 0:512], in_=self.psB[64:96, 0:512])
            sc.drain().then_inc(self.sC, 1)
            self.bank_ev[bid] = self.inc("sC")
            sy.wait_ge(self.sC, self.S["sC"])
            self.sydma2(lambda h, tc=tc:
                        dr["s_bc"][0:32, QBLOCKS[h][tc] * 512:
                                   QBLOCKS[h][tc] * 512 + 512],
                        self.tA[0:32, 0:512], "sS")
            for dblk in range(16):
                ps = (self.psB, self.psC)[dblk % 2]
                bid = ("ps", dblk % 2)
                if bid in self.bank_ev:
                    tn.wait_ge(self.sC, self.bank_ev[bid])
                tn.matmul(out=ps[:, 0:512],
                          lhsT=wdt[:, dblk * 128:(dblk + 1) * 128],
                          rhs=dtb, start=True, stop=True)
                tcnt = self.mm_done()
                sc.wait_ge(self.sT, tcnt)
                self.wS(sc)
                bcol = self.tC[:, 80 + dblk:81 + dblk]
                sc.activation(out=self.tA[:, 0:512], in_=ps[:, 0:512],
                              func=ACT.Relu, bias=bcol, scale=np.float32(1.0))
                sc.drain()
                sc.activation(out=self.tB[:, 0:512], in_=ps[:, 0:512],
                              func=ACT.Abs, bias=bcol, scale=np.float32(1.0))
                sc.drain().then_inc(self.sC, 1)
                self.bank_ev[bid] = self.inc("sC")
                sc.activation(out=self.tB[:, 0:512], in_=self.tB[:, 0:512],
                              func=ACT.Exp, scale=np.float32(-1.0))
                sc.drain()
                sc.activation(out=self.tB[:, 0:512], in_=self.tB[:, 0:512],
                              func=ACT.Ln, bias=np.float32(1.0), scale=np.float32(1.0))
                sc.drain().then_inc(self.sC, 1)
                ve.wait_ge(self.sC, self.inc("sC"))
                self.wS(ve)
                ve.tensor_tensor(out=self.tA[:, 0:512], in0=self.tA[:, 0:512],
                                 in1=self.tB[:, 0:512], op=ALU.add)
                ve.drain().then_inc(self.sV, 1)
                self.inc("sV")
                sy.wait_ge(self.sV, self.S["sV"])
                self.sydma2(lambda h, dblk=dblk, tc=tc:
                            dr["s_dlt"][dblk * 128:(dblk + 1) * 128,
                                        QBLOCKS[h][tc] * 512:
                                        QBLOCKS[h][tc] * 512 + 512],
                            self.tA[:, 0:512], "sS")

    def phase_d(self):
        dr = self.dr
        sy, g, tn, sc, ve = self.syp, self.g, self.tn, self.sc, self.ve
        g.wait_ge(self.sT, self.S["sT"])
        g.wait_ge(self.sV, self.S["sV"])
        g.wait_ge(self.sC, self.S["sC"])
        dl = self.ARf[:, 0:1024]; uc = self.ARf[:, 1024:2048]
        du = self.ARf[:, 2048:3072]
        aa = [self.ARf[:, 3072:4096], self.ARf[:, 4096:5120]]
        hh_ = self.ARf[:, 5120:6144]; bb = self.ARf[:, 6144:7168]
        zz = self.ARf[:, 7168:8192]
        yall = self.ARf[:, 8192:16384]
        bcB = self.ARb[:, 0:8192]
        bcC = self.ARb[:, 8192:16384]
        gbf = self.bfA[:, 0:1024]
        self.dma(g, self.tC[:, 0:128], dr["alog_c"], "sL")
        self.dma(g, self.tC[:, 128:136], dr["dskip_c"], "sL")
        for e in [tn, sc, ve, sy]:
            self.wL(e)
        sc.activation(out=self.tB[:, 0:128], in_=self.tC[:, 0:128],
                      func=ACT.Exp)
        sc.drain().then_inc(self.sC, 1)
        ve.wait_ge(self.sC, self.inc("sC"))
        ve.tensor_scalar(out=self.tC[:, 0:128], in0=self.tB[:, 0:128],
                         scalar1=-1.0, scalar2=None, op0=ALU.mult)
        ve.drain()
        ve.memset(self.carry[:, 0:128], 0.0)
        ve.drain().then_inc(self.sV, 1)
        self.inc("sV")

        for tc in range(4):
            for nh in range(2):
                sc.wait_ge(self.sV, self.S["sV"])
                for n8 in range(8):
                    n = nh * 8 + n8
                    g.wait_ge(self.sT, self.S["sT"])
                    self.dma(g, self.rowf[0:1, 0:1024],
                             dr["s_bc"][n:n + 1, tc * 1024:(tc + 1) * 1024],
                             "sL")
                    self.dma(g, self.rowf[0:1, 1024:2048],
                             dr["s_bc"][16 + n:17 + n,
                                        tc * 1024:(tc + 1) * 1024], "sL")
                    self.wL(tn)
                    for q in range(4):
                        bid = ("psA", q)
                        if bid in self.bank_ev:
                            tn.wait_ge(self.sC, self.bank_ev[bid])
                    for half in range(2):
                        tn.matmul(out=self.psA[:, half * 512:(half + 1) * 512],
                                  lhsT=self.onesfr[0:1, :],
                                  rhs=self.rowf[0:1, half * 512:
                                                (half + 1) * 512],
                                  start=True, stop=True)
                        tn.matmul(out=self.psA[:, 1024 + half * 512:
                                               1024 + (half + 1) * 512],
                                  lhsT=self.onesfr[0:1, :],
                                  rhs=self.rowf[0:1, 1024 + half * 512:
                                                1024 + (half + 1) * 512],
                                  start=True, stop=True)
                    tcnt = self.mm_done()
                    self.evac(bcB[:, n8 * 1024:(n8 + 1) * 1024],
                              self.psA[:, 0:1024], tcount=tcnt)
                    c = self.evac(bcC[:, n8 * 1024:(n8 + 1) * 1024],
                                  self.psA[:, 1024:2048], tcount=tcnt)
                    for q in range(4):
                        self.bank_ev[("psA", q)] = c
                for dblk in range(8):
                    sy.wait_ge(self.sV, self.S["sV"])
                    self.sydma2(dl, lambda h, dblk=dblk, tc=tc:
                                dr["s_dlt"][h * 1024 + dblk * 128:
                                            h * 1024 + dblk * 128 + 128,
                                            tc * 1024:(tc + 1) * 1024], "sL")
                    self.sydma2(uc, lambda h, dblk=dblk, tc=tc:
                                dr["s_uc"][h * 1024 + dblk * 128:
                                           h * 1024 + dblk * 128 + 128,
                                           tc * 1024:(tc + 1) * 1024], "sL")
                    if nh == 1:
                        self.sydma2(zz, lambda h, dblk=dblk, tc=tc:
                                    dr["s_z"][h * 1024 + dblk * 128:
                                              h * 1024 + dblk * 128 + 128,
                                              tc * 1024:(tc + 1) * 1024], "sL")
                    self.wL(ve)
                    ve.tensor_tensor(out=du, in0=dl, in1=uc, op=ALU.mult)
                    ve.drain()
                    if nh == 0:
                        ve.tensor_scalar(out=yall[:, dblk * 1024:
                                                  (dblk + 1) * 1024],
                                         in0=uc,
                                         scalar1=self.tC[:, 128 + dblk:
                                                         129 + dblk],
                                         scalar2=None, op0=ALU.mult)
                        ve.drain()
                    ve.drain().then_inc(self.sV, 1)
                    self.inc("sV")
                    vbase = self.S["sV"]
                    sc.wait_ge(self.sV, vbase)
                    for n8 in range(8):
                        n = nh * 8 + n8
                        if n8 >= 2:
                            sc.wait_ge(self.sV, vbase + n8 - 1)
                        sc.activation(out=aa[n8 % 2], in_=dl, func=ACT.Exp,
                                      scale=self.tC[:, dblk * 16 + n:
                                                    dblk * 16 + n + 1])
                        sc.drain().then_inc(self.sC, 1)
                        ve.wait_ge(self.sC, self.inc("sC"))
                        ve.tensor_tensor(out=bb, in0=du,
                                         in1=bcB[:, n8 * 1024:(n8 + 1) * 1024],
                                         op=ALU.mult)
                        ve.drain()
                        ve.tensor_tensor_scan(
                            out=hh_, data0=aa[n8 % 2], data1=bb,
                            initial=self.carry[:, dblk * 16 + n:
                                               dblk * 16 + n + 1],
                            op0=ALU.mult, op1=ALU.add)
                        ve.drain()
                        ve.tensor_copy(out=self.carry[:, dblk * 16 + n:
                                                      dblk * 16 + n + 1],
                                       in_=hh_[:, 1023:1024])
                        ve.tensor_tensor(out=bb, in0=hh_,
                                         in1=bcC[:, n8 * 1024:(n8 + 1) * 1024],
                                         op=ALU.mult)
                        ve.drain()
                        ve.tensor_tensor(out=yall[:, dblk * 1024:
                                                  (dblk + 1) * 1024],
                                         in0=yall[:, dblk * 1024:
                                                  (dblk + 1) * 1024],
                                         in1=bb, op=ALU.add)
                        ve.drain().then_inc(self.sV, 1)
                        self.inc("sV")
                    if nh == 1:
                        sc.wait_ge(self.sV, self.S["sV"])
                        self.wL(sc)
                        sc.activation(out=zz, in_=zz, func=ACT.Silu)
                        sc.drain().then_inc(self.sC, 1)
                        ve.wait_ge(self.sC, self.inc("sC"))
                        self.wS(ve)
                        ve.tensor_tensor(out=gbf,
                                         in0=yall[:, dblk * 1024:
                                                  (dblk + 1) * 1024],
                                         in1=zz, op=ALU.mult)
                        ve.drain().then_inc(self.sV, 1)
                        self.inc("sV")
                        sy.wait_ge(self.sV, self.S["sV"])
                        self.sydma2(lambda h, dblk=dblk, tc=tc:
                                    dr["s_g"][h * 1024 + dblk * 128:
                                              h * 1024 + dblk * 128 + 128,
                                              tc * 1024:(tc + 1) * 1024],
                                    gbf, "sS")

    def phase_e(self, out_sh):
        dr = self.dr
        sy, g, tn, sc, ve = self.syp, self.g, self.tn, self.sc, self.ve
        g.wait_ge(self.sT, self.S["sT"])
        g.wait_ge(self.sV, self.S["sV"])
        g.wait_ge(self.sC, self.S["sC"])
        gT = self.ARb[:, 0:8192]
        wout1 = self.ARb[:, 8192:24576]
        hsT = self.ARb[:, 8192:24576]
        hsbf = self.bfA[:, 0:1024]
        wrt = self.ARf[:, 15360:15392]
        hsfT = self.ARf[:, 15488:16512]
        colsM = self.ARf[:, 16512:16576]
        xt2 = self.ARf[:, 16576:17600]
        self.dma(g, wout1.rearrange("p (c o) -> p c o", c=16),
                 dr["w_out1"].rearrange("(c p) o -> p c o", p=128), "sL")
        self.dma(g, wrt.rearrange("p (c o) -> p c o", c=8),
                 dr["w_rt"].rearrange("(c p) o -> p c o", p=128), "sL")
        for e in [tn, sc, ve, sy]:
            self.wL(e)

        for qg in range(4):
            sy.wait_ge(self.sT, self.S["sT"])
            self.sydma2(gT.rearrange("p (c t) -> p c t", c=16),
                        lambda h, qg=qg:
                        dr["s_g"].rearrange("(c p) t -> p c t", p=128)
                        [:, :, QBLOCKS[h][qg] * 512:QBLOCKS[h][qg] * 512 + 512],
                        "sL")
            self.wL(tn)
            for tq in range(4):
                tt = qg * 4 + tq
                for half in range(2):
                    ps = (self.psB, self.psC)[half]
                    bid = ("ps", half)
                    if bid in self.bank_ev:
                        tn.wait_ge(self.sC, self.bank_ev[bid])
                    for c16 in range(16):
                        tn.matmul(out=ps[:, 0:512],
                                  lhsT=gT[:, c16 * 512 + tq * 128:
                                          c16 * 512 + (tq + 1) * 128],
                                  rhs=wout1[:, c16 * 1024 + half * 512:
                                            c16 * 1024 + (half + 1) * 512],
                                  start=(c16 == 0), stop=(c16 == 15))
                tcnt = self.mm_done()
                sc.wait_ge(self.sV, self.S["sV"])
                self.bank_ev[("ps", 0)] = self.evac(
                    self.tA[:, 0:512], self.psB[:, 0:512], tcount=tcnt)
                self.bank_ev[("ps", 1)] = self.evac(
                    self.tA[:, 512:1024], self.psC[:, 0:512], tcount=tcnt)
                sy.wait_ge(self.sV, self.S["sV"])
                sy.wait_ge(self.sS, self.S["sS"])
                self.dma(sy, self.tB[:, 0:1024], dr["l_xa2"][tt], "sL")
                self.dma(sy, xt2, dr["l_x2"][tt], "sL")
                ve.wait_ge(self.sC, self.S["sC"])
                self.wL(ve)
                self.wS(ve)
                ve.tensor_tensor(out=self.tA[:, 0:1024], in0=self.tA[:, 0:1024],
                                 in1=self.tB[:, 0:1024], op=ALU.add)
                ve.drain().then_inc(self.sV, 1)
                self.inc("sV")
                mu, rc = self.ln_stats(self.tA[:, 0:1024], 1e-5)
                self.ln_norm(self.tA[:, 0:1024], self.tB[:, 0:1024], mu, rc)
                ve.tensor_tensor(out=xt2, in0=xt2, in1=self.tB[:, 0:1024],
                                 op=ALU.add)
                ve.drain().then_inc(self.sV, 1)
                sy.wait_ge(self.sV, self.inc("sV"))
                self.dma(sy, dr["l_x3"][tt], xt2, "sS")
                mu, rc = self.ln_stats(xt2, 1e-5)
                self.ln_norm(xt2, self.tB[:, 0:1024], mu, rc)
                ve.drain().then_inc(self.sV, 1)
                sy.wait_ge(self.sV, self.inc("sV"))
                self.dma(sy, dr["l_hs"][tt], self.tB[:, 0:1024], "sS")
                ve.drain().then_inc(self.sV, 1)
                tn.wait_ge(self.sV, self.inc("sV"))
                self.transpose_to(lambda i: self.tB[:, i * 128:(i + 1) * 128],
                                  8, lambda i: hsfT[:, i * 128:(i + 1) * 128],
                                  f32=True)
                tn.wait_ge(self.sC, self.S["sC"])
                for ic in range(8):
                    tn.matmul(out=self.psB[:, 0:NE],
                              lhsT=hsfT[:, ic * 128:(ic + 1) * 128],
                              rhs=wrt[:, ic * 4:(ic + 1) * 4],
                              start=(ic == 0), stop=(ic == 7))
                tcnt = self.mm_done()
                self.bank_ev[("ps", 0)] = self.evac(
                    self.tA[:, 0:NE], self.psB[:, 0:NE], tcount=tcnt)
                ve.wait_ge(self.sC, self.S["sC"])
                ve.tensor_reduce(out=self.cols[:, 23:24], in_=self.tA[:, 0:NE],
                                 axis=AX, op=ALU.max)
                ve.drain()
                ve.memset(self.cols[:, 24:25], 1.0)
                ve.drain()
                for ee in range(NE):
                    ve.tensor_scalar(out=self.cols[:, 25:26],
                                     in0=self.tA[:, ee:ee + 1],
                                     scalar1=self.cols[:, 23:24], scalar2=None,
                                     op0=ALU.is_ge)
                    ve.drain()
                    ve.tensor_tensor(out=colsM[:, tt * 4 + ee:tt * 4 + ee + 1],
                                     in0=self.cols[:, 25:26],
                                     in1=self.cols[:, 24:25], op=ALU.mult)
                    ve.drain()
                    if ee < NE - 1:
                        ve.tensor_scalar(out=self.cols[:, 25:26],
                                         in0=colsM[:, tt * 4 + ee:
                                                   tt * 4 + ee + 1],
                                         scalar1=-1.0, scalar2=1.0,
                                         op0=ALU.mult, op1=ALU.add)
                        ve.drain()
                        ve.tensor_tensor(out=self.cols[:, 24:25],
                                         in0=self.cols[:, 24:25],
                                         in1=self.cols[:, 25:26], op=ALU.mult)
                        ve.drain()
                ve.drain().then_inc(self.sV, 1)
                self.inc("sV")

        for tt in range(NT):
            ve.drain().then_inc(self.sV, 1)
            sy.wait_ge(self.sV, self.inc("sV"))
            sy.wait_ge(self.sS, self.S["sS"])
            self.dma(sy, self.tB[:, 0:1024], dr["l_hs"][tt], "sL")
            self.wL(ve)
            ve.wait_ge(self.sT, self.S["sT"])
            ve.tensor_copy(out=hsbf, in_=self.tB[:, 0:1024])
            ve.drain().then_inc(self.sV, 1)
            tn.wait_ge(self.sV, self.inc("sV"))
            self.transpose_to(lambda i: hsbf[:, i * 128:(i + 1) * 128], 8,
                              lambda i: hsT[:, tt * 1024 + i * 128:
                                            tt * 1024 + (i + 1) * 128])

        def hsT_ap(tt, ic):
            return hsT[:, tt * 1024 + ic * 128:tt * 1024 + (ic + 1) * 128]

        g.wait_ge(self.sT, self.S["sT"])
        g.wait_ge(self.sC, self.S["sC"])
        self._vq = []
        w1b = [self.ARb[:, i * 2048:(i + 1) * 2048] for i in range(2)]
        intr = self.ARb[:, 4096:8192]
        w1v = dr["w1"].rearrange("e (c p) o -> e p c o", p=128)
        for oc in range(16):
            self.wS(ve)
            for ee in range(NE):
                bslot = 6528 + ee * 256
                self.dma(g, self.rowb[0:1, bslot:bslot + 256],
                         dr["b1_r"][ee:ee + 1, oc * 256:(oc + 1) * 256], "sL")
                buf = w1b[(oc * NE + ee) % 2]
                if oc * NE + ee >= 2:
                    g.wait_ge(self.sT, self.S["sT"])
                self.dma(g, buf.rearrange("p (c o) -> p c o", c=8),
                         w1v[ee][:, :, oc * 256:(oc + 1) * 256], "sL")
                self.wL(tn)
                tn.wait_ge(self.sC, self.S["sC"])
                for tt in range(NT):
                    bk = tt % 4
                    ps = self.psA[:, bk * 512:bk * 512 + 256]
                    bid = ("psA", bk)
                    if bid in self.bank_ev:
                        tn.wait_ge(self.sC, self.bank_ev[bid])
                    tn.matmul(out=ps, lhsT=self.onesbr[0:1, :],
                              rhs=self.rowb[0:1, bslot:bslot + 256],
                              start=True, stop=False)
                    for ic in range(8):
                        tn.matmul(out=ps, lhsT=hsT_ap(tt, ic),
                                  rhs=buf[:, ic * 256:(ic + 1) * 256],
                                  start=False, stop=(ic == 7))
                    tcnt = self.mm_done()
                    dstf = (self.tA, self.tB, self.tC, self.tDmp)[bk][:, 0:256]
                    if len(self._vq) >= 4:
                        sc.wait_ge(self.sV, self._vq[-4])
                    self.bank_ev[bid] = self.evac(dstf, ps,
                                                  func=ACT.Gelu, tcount=tcnt)
                    ve.wait_ge(self.sC, self.S["sC"])
                    mc = colsM[:, tt * 4 + ee:tt * 4 + ee + 1]
                    if ee == 0:
                        ve.tensor_scalar(out=intr[:, tt * 256:(tt + 1) * 256],
                                         in0=dstf, scalar1=mc, scalar2=None,
                                         op0=ALU.mult)
                    else:
                        ve.scalar_tensor_tensor(
                            out=intr[:, tt * 256:(tt + 1) * 256], in0=dstf,
                            scalar=mc, in1=intr[:, tt * 256:(tt + 1) * 256],
                            op0=ALU.mult, op1=ALU.add)
                    ve.drain().then_inc(self.sV, 1)
                    self._vq.append(self.inc("sV"))
            sy.wait_ge(self.sV, self.S["sV"])
            for tt in range(NT):
                self.dma(sy, dr["l_inter"][oc][tt],
                         intr[:, tt * 256:(tt + 1) * 256], "sS")

        g.wait_ge(self.sT, self.S["sT"])
        g.wait_ge(self.sV, self.S["sV"])
        self.wS(g)
        wo = self.ARb[:, 0:16384]
        intT = self.ARb[:, 24576:28672]
        intf = self.bfB[:, 0:4096]
        part = self.ARf[:, 0:16384]
        self.dma(g, self.rowf[0:1, 0:D], dr["ln2_g_r"], "sL")
        self.dma(g, self.rowf[0:1, D:2 * D], dr["ln2_b_r"], "sL")
        for e in [tn, sc, ve]:
            self.wL(e)
        self.load_gbc(self.rowf[0:1, 0:D], self.rowf[0:1, D:2 * D])
        wov = dr["wo"].rearrange("(c p) o -> p c o", p=128)
        for rnd in range(2):
            g.wait_ge(self.sT, self.S["sT"])
            self.dma(g, wo.rearrange("p (c o) -> p c o", c=16),
                     wov[:, 16 * rnd:16 * rnd + 16, :], "sL")
            self.wL(tn)
            for tt in range(NT):
                ve.drain().then_inc(self.sV, 1)
                sy.wait_ge(self.sV, self.inc("sV"))
                sy.wait_ge(self.sS, self.S["sS"])
                for oc in range(16):
                    self.dma(sy, intf[:, oc * 256:(oc + 1) * 256],
                             dr["l_inter"][oc][tt], "sL")
                self.wL(ve)
                ve.drain().then_inc(self.sV, 1)
                tn.wait_ge(self.sV, self.inc("sV"))
                self.transpose_to(
                    lambda i: intf[:, (16 * rnd + i) * 128:
                                   (16 * rnd + i + 1) * 128], 16,
                    lambda i: intT[:, i * 128:(i + 1) * 128])
                tn.wait_ge(self.sC, self.S["sC"])
                for half in range(2):
                    ps = (self.psB, self.psC)[half]
                    bid = ("ps", half)
                    if bid in self.bank_ev:
                        tn.wait_ge(self.sC, self.bank_ev[bid])
                    if rnd == 0:
                        tn.matmul(out=ps[:, 0:512],
                                  lhsT=self.onesbr[0:1, :],
                                  rhs=self.rowb[0:1, 5504 + half * 512:
                                                5504 + (half + 1) * 512],
                                  start=True, stop=False)
                    for c16 in range(16):
                        tn.matmul(out=ps[:, 0:512],
                                  lhsT=intT[:, c16 * 128:(c16 + 1) * 128],
                                  rhs=wo[:, c16 * 1024 + half * 512:
                                         c16 * 1024 + (half + 1) * 512],
                                  start=(rnd == 1 and c16 == 0),
                                  stop=(c16 == 15))
                tcnt = self.mm_done()
                pslot = part[:, tt * 1024:(tt + 1) * 1024]
                self.bank_ev[("ps", 0)] = self.evac(
                    self.tA[:, 0:512], self.psB[:, 0:512], tcount=tcnt)
                self.bank_ev[("ps", 1)] = self.evac(
                    self.tA[:, 512:1024], self.psC[:, 0:512], tcount=tcnt)
                ve.wait_ge(self.sC, self.S["sC"])
                if rnd == 0:
                    ve.tensor_copy(out=pslot, in_=self.tA[:, 0:1024])
                    ve.drain().then_inc(self.sV, 1)
                    self.inc("sV")
                    continue
                ve.tensor_tensor(out=self.tA[:, 0:1024], in0=pslot,
                                 in1=self.tA[:, 0:1024], op=ALU.add)
                ve.drain().then_inc(self.sV, 1)
                sy.wait_ge(self.sV, self.inc("sV"))
                self.dma(sy, self.tB[:, 0:1024], dr["l_hs"][tt], "sL")
                self.wL(ve)
                ve.tensor_tensor(out=self.tA[:, 0:1024], in0=self.tA[:, 0:1024],
                                 in1=self.tB[:, 0:1024], op=ALU.add)
                ve.drain().then_inc(self.sV, 1)
                self.inc("sV")
                mu, rc = self.ln_stats(self.tA[:, 0:1024], 1e-12)
                self.ln_norm(self.tA[:, 0:1024], self.tB[:, 0:1024], mu, rc)
                ve.tensor_tensor(out=self.tB[:, 0:1024], in0=self.tB[:, 0:1024],
                                 in1=self.gbc[:, 0:1024], op=ALU.mult)
                ve.drain()
                ve.tensor_tensor(out=self.tB[:, 0:1024], in0=self.tB[:, 0:1024],
                                 in1=self.gbc[:, 1024:2048], op=ALU.add)
                ve.drain().then_inc(self.sV, 1)
                self.inc("sV")
                mu, rc = self.ln_stats(self.tB[:, 0:1024], 1e-5)
                self.ln_norm(self.tB[:, 0:1024], self.tC[:, 0:1024], mu, rc)
                ve.drain().then_inc(self.sV, 1)
                sy.wait_ge(self.sV, self.inc("sV"))
                self.dma(sy, self.tA[:, 0:1024], dr["l_x3"][tt], "sL")
                self.wL(ve)
                ve.tensor_tensor(out=self.tC[:, 0:1024], in0=self.tC[:, 0:1024],
                                 in1=self.tA[:, 0:1024], op=ALU.add)
                ve.drain().then_inc(self.sV, 1)
                sy.wait_ge(self.sV, self.inc("sV"))
                self.dma(sy, out_sh[tt * 128:(tt + 1) * 128, :],
                         self.tC[:, 0:1024], "sS")
        sy.wait_ge(self.sS, self.S["sS"])

_NC_CACHE = {}


def _get_nc(nphase=NPH):
    if nphase not in _NC_CACHE:
        _NC_CACHE[nphase] = Build(nphase).build()
    return _NC_CACHE[nphase]


def _mk_inputs(inputs, nphase=NPH):
    f32 = np.float32
    I = {k: np.asarray(v) for k, v in inputs.items()}
    x = np.ascontiguousarray(I["L0_attn_x"], dtype=f32)
    id_bf = np.eye(128, dtype=bf16)
    id_f32 = np.eye(128, dtype=f32)
    onesb_r = np.ones((1, 128), dtype=bf16)
    onesf_r = np.ones((1, 128), dtype=f32)
    onesf_c = np.ones((128, 1), dtype=f32)
    dm = np.zeros((128, 2048), dtype=bf16)
    for dj in range(4):
        for p in range(128):
            for qq in range(512):
                if dj * 128 + p <= qq:
                    dm[p, dj * 512 + qq] = 1.0
    alog = np.ascontiguousarray(I["L1_attn_A_log"], dtype=f32)
    dskip = np.ascontiguousarray(I["L1_attn_D_skip"], dtype=f32)
    convw = np.ascontiguousarray(I["L1_attn_conv_w"], dtype=f32)
    convb = np.ascontiguousarray(I["L1_attn_conv_b"], dtype=f32)
    bdt = np.ascontiguousarray(I["L1_attn_b_dt"], dtype=f32)
    maps = []
    for c in range(8):
        b, h = c // 2, c % 2
        rows = np.concatenate([x[b, gb * 512:(gb + 1) * 512, :]
                               for gb in QBLOCKS[h]], axis=0)
        invq2 = np.zeros((128, NT), dtype=f32)
        for tt in range(NT):
            tg = tglob(tt, h)
            q = np.arange(tg, tg + 128, dtype=np.float64) + 1.0
            invq2[:, tt] = (1.0 / (q * q)).astype(f32)
        al = np.zeros((128, 128), dtype=f32)
        for j in range(8):
            al[:, j * 16:(j + 1) * 16] = alog[h * 1024 + j * 128:
                                              h * 1024 + (j + 1) * 128, :]
        dsk = np.zeros((128, 8), dtype=f32)
        for j in range(8):
            dsk[:, j] = dskip[h * 1024 + j * 128:h * 1024 + (j + 1) * 128]
        cw = np.zeros((128, 64), dtype=f32)
        cb = np.zeros((128, 16), dtype=f32)
        bd = np.zeros((128, 16), dtype=f32)
        for j in range(16):
            cw[:, j * 4:(j + 1) * 4] = convw[j * 128:(j + 1) * 128, :]
            cb[:, j] = convb[j * 128:(j + 1) * 128]
            bd[:, j] = bdt[j * 128:(j + 1) * 128]
        m = {
            "x_sh": np.ascontiguousarray(rows),
            "w_hid": np.ascontiguousarray(I["L0_attn_W_hidden"], f32),
            "w_qk": np.ascontiguousarray(I["L0_attn_W_qk"], f32),
            "osg0": np.ascontiguousarray(I["L0_attn_os_gamma"][0].reshape(128, 1), f32),
            "osb0": np.ascontiguousarray(I["L0_attn_os_beta"][0].reshape(128, 1), f32),
            "osg1": np.ascontiguousarray(I["L0_attn_os_gamma"][1].reshape(128, 1), f32),
            "osb1": np.ascontiguousarray(I["L0_attn_os_beta"][1].reshape(128, 1), f32),
            "w_out0": np.ascontiguousarray(I["L0_attn_W_out"], f32),
            "wg_half": np.ascontiguousarray(I["L0_ffn_w_gate"][h * 2048:(h + 1) * 2048], f32),
            "wu_half": np.ascontiguousarray(I["L0_ffn_w_up"][h * 2048:(h + 1) * 2048], f32),
            "wd_half": np.ascontiguousarray(I["L0_ffn_w_down"][h * 512:(h + 1) * 512], f32),
            "w_in": np.ascontiguousarray(I["L1_attn_W_in"], f32),
            "convw": cw, "convb": cb, "bdt_c": bd,
            "w_xp": np.ascontiguousarray(I["L1_attn_W_xproj"], f32),
            "w_dt": np.ascontiguousarray(I["L1_attn_W_dt"], f32),
            "alog_c": al, "dskip_c": dsk,
            "w_out1": np.ascontiguousarray(I["L1_attn_W_out"], f32),
            "w_rt": np.ascontiguousarray(I["L1_ffn_W_router"], f32),
            "w1": np.ascontiguousarray(I["L1_ffn_W1"], f32),
            "wo": np.ascontiguousarray(I["L1_ffn_Wo"], f32),
            "b_hid_r": np.ascontiguousarray(I["L0_attn_b_hidden"].reshape(1, -1), f32),
            "b_qk_r": np.ascontiguousarray(I["L0_attn_b_qk"].reshape(1, -1), f32),
            "b_out0_r": np.ascontiguousarray(I["L0_attn_b_out"].reshape(1, -1), f32),
            "b1_r": np.ascontiguousarray(I["L1_ffn_b1"], f32),
            "bo_r": np.ascontiguousarray(I["L1_ffn_bo"].reshape(1, -1), f32),
            "ln_g_r": np.ascontiguousarray(I["L0_attn_ln_g"].reshape(1, -1), f32),
            "ln_b_r": np.ascontiguousarray(I["L0_attn_ln_b"].reshape(1, -1), f32),
            "ln1_g_r": np.ascontiguousarray(I["L1_attn_ln_gamma"].reshape(1, -1), f32),
            "ln1_b_r": np.ascontiguousarray(I["L1_attn_ln_beta"].reshape(1, -1), f32),
            "ln2_g_r": np.ascontiguousarray(I["L1_ffn_ln_gamma"].reshape(1, -1), f32),
            "ln2_b_r": np.ascontiguousarray(I["L1_ffn_ln_beta"].reshape(1, -1), f32),
            "id_bf": id_bf, "id_f32": id_f32,
            "onesb_r": onesb_r, "onesf_r": onesf_r, "onesf_c": onesf_c,
            "invq2_c": invq2, "dmask": dm,
        }
        maps.append(m)
    return maps


def kernel(**inputs):
    nphase = NPH
    nc = _get_nc(nphase)
    maps = _mk_inputs(inputs, nphase)
    res = run_bass_kernel_spmd(nc, maps, core_ids=list(range(8)))
    out = np.zeros((B, T, D), dtype=np.float32)
    for c in range(8):
        b, h = c // 2, c % 2
        o = res.results[c]["out_sh"]
        for bi, gb in enumerate(QBLOCKS[h]):
            out[b, gb * 512:(gb + 1) * 512, :] = o[bi * 512:(bi + 1) * 512, :]
    return out


# version 05
# speedup vs baseline: 1.0618x; 1.0086x over previous
import numpy as np
import ml_dtypes

import concourse.bass as bass
import concourse.mybir as mybir
from concourse.bass_utils import run_bass_kernel_spmd

F32 = mybir.dt.float32
BF16 = mybir.dt.bfloat16
ACT = mybir.ActivationFunctionType
ALU = mybir.AluOpType
AX = mybir.AxisListType.X

MAGIC = float(np.float32(12582912.0))
B, T, D = 4, 4096, 1024
HID, QK, INNER = 2048, 128, 4096
DI, DS, DTR, NE = 2048, 16, 64, 4
QBLOCKS = [[0, 3, 4, 7], [1, 2, 5, 6]]
NT, TS = 16, 2048
NPH = 99
bf16 = ml_dtypes.bfloat16


def tglob(tt, h):
    return QBLOCKS[h][tt // 4] * 512 + (tt % 4) * 128


class Build:
    def __init__(self, nphase=NPH):
        self.nphase = nphase
        self.nc = bass.Bass("TRN2", target_bir_lowering=False, debug=False)
        self.S = {}
        self._cm = []
        self._pid = {}
        self.bank_ev = {}
        self.syq = {0: [], 1: []}
        self._roleq = None

    def enter(self, cm):
        self._cm.append(cm)
        return cm.__enter__()

    def close(self):
        for cm in reversed(self._cm):
            cm.__exit__(None, None, None)

    def sem(self, name):
        setattr(self, name, self.enter(self.nc.semaphore(name)))
        self.S[name] = 0

    def wL(self, eng):
        if eng is not self.syp:
            self.syp.wait_ge(self.sL, self.S["sL"])
        eng.wait_ge(self.sL, self.S["sL"])
        eng.wait_ge(self.sLg, self.S["sLg"])

    def wS(self, eng):
        if eng is not self.syp:
            self.syp.wait_ge(self.sS, self.S["sS"])
        eng.wait_ge(self.sS, self.S["sS"])

    def inc(self, n, by=1):
        self.S[n] += by
        return self.S[n]

    def dma(self, eng, out, in_, sname):
        if eng is self.syp:
            self.sydma(out, in_, sname)
            return
        if eng is self.g and sname == "sL":
            sname = "sLg"
        eng.dma_start(out=out, in_=in_).then_inc(getattr(self, sname), 16)
        self.inc(sname, 16)
        if sname == "sLg":
            eng.wait_ge(self.sLg, self.S["sLg"])

    def pidmod2(self, eng):
        return self.nc.partition_id([eng.engine]) % 2

    class _RecInstr:
        def __init__(self, op):
            self.op = op

        def then_inc(self, sem, n):
            self.op[3].append((sem, n))
            return self

    class _Rec:
        def __init__(self, b, name):
            self._b = b
            self._name = name
            self.pre = []
            self.role = {0: [], 1: []}
            self.post = []
            self.mode = "pre"
            self.rolecur = 0

        def _q(self):
            if self.mode == "role":
                return self.role[self.rolecur]
            return self.pre if self.mode == "pre" else self.post

        def __getattr__(self, attr):
            if attr.startswith("_") or attr in ("pre", "role", "post", "mode",
                                                "rolecur"):
                raise AttributeError(attr)

            def call(*a, **k):
                op = [attr, a, k, []]
                self._q().append(op)
                return Build._RecInstr(op)
            return call

    @staticmethod
    def _replay_ops(eng, ops):
        for (name, a, k, incs) in ops:
            inst = getattr(eng, name)(*a, **k)
            for (sem, n) in incs:
                inst.then_inc(sem, n)

    def replay_all(self, blk):
        nc = self.nc
        for nm in ["g", "tn", "sc", "ve"]:
            rec = getattr(self, nm)
            dec = {"g": blk.gpsimd, "tn": blk.tensor, "sc": blk.scalar,
                   "ve": blk.vector}[nm]

            def body(eng, rec=rec):
                self._replay_ops(eng, rec.pre)
                if rec.role[0] or rec.role[1]:
                    idx = nc.partition_id([eng.engine]) % 2
                    for h in eng.Switch(index=idx, n=2):
                        self._replay_ops(eng, rec.role[h])
                self._replay_ops(eng, rec.post)
            dec(body)

        def sybody(sy):
            idx = nc.partition_id([sy.engine]) % 2
            for h in sy.Switch(index=idx, n=2):
                for op in self.syq[h]:
                    if op[0] == "w":
                        sy.wait_ge(op[1], op[2])
                    else:
                        sy.dma_start(out=op[1], in_=op[2]).then_inc(op[3], 16)
        blk.sync(sybody)

    def role_region(self, recorders):
        snap = end0 = None
        bsnap = bend0 = None
        for h in (0, 1):
            for r in recorders:
                r.mode = "role"
                r.rolecur = h
            self._roleq = h
            if h == 0:
                snap = dict(self.S)
                bsnap = dict(self.bank_ev)
            else:
                end0 = dict(self.S)
                bend0 = dict(self.bank_ev)
                self.S = dict(snap)
                self.bank_ev = dict(bsnap)
            yield h
            if h == 1:
                assert self.S == end0, f"role mismatch {self.S} vs {end0}"
                assert self.bank_ev == bend0, "bank_ev role mismatch"
        for r in recorders:
            r.mode = "post"
        self._roleq = None

    class _SyP:
        def __init__(self, b):
            self.b = b

        def wait_ge(self, sem, val):
            for q in self.b._syq_targets():
                q.append(("w", sem, val))

    def _syq_targets(self):
        if self._roleq is None:
            return [self.syq[0], self.syq[1]]
        return [self.syq[self._roleq]]

    def sydma(self, out, in_, sname):
        sem = getattr(self, sname)
        for q in self._syq_targets():
            q.append(("d", out, in_, sem))
        self.inc(sname, 16)

    def sydma2(self, out_fn, in_fn, sname):
        assert self._roleq is None
        sem = getattr(self, sname)
        for h in (0, 1):
            o = out_fn(h) if callable(out_fn) else out_fn
            i = in_fn(h) if callable(in_fn) else in_fn
            self.syq[h].append(("d", o, i, sem))
        self.inc(sname, 16)

    def flush_syq(self):
        sy = self.sy
        for h in self.rswitch_noassert([sy]):
            for op in self.syq[h]:
                if op[0] == "w":
                    sy.wait_ge(op[1], op[2])
                else:
                    sy.dma_start(out=op[1], in_=op[2]).then_inc(op[3], 16)

    def rswitch_noassert(self, engines):
        idx = [self.pidmod2(e) for e in engines]
        gen = self.nc.Switch(engines=engines, index=idx, n=2)
        for h in gen:
            yield h

    def rswitch(self, engines):
        idx = [self.pidmod2(e) for e in engines]
        gen = self.nc.Switch(engines=engines, index=idx, n=2)
        snap = end0 = None
        for h in gen:
            if h == 0:
                snap = dict(self.S)
            else:
                end0 = dict(self.S)
                self.S = dict(snap)
            yield h
        if end0 is not None:
            assert self.S == end0, f"role sem mismatch: {self.S} != {end0}"

    def build(self):
        nc = self.nc
        dr = {}

        def din(name, shape, dtype=F32):
            dr[name] = nc.dram_tensor(name, list(shape), dtype,
                                      kind="ExternalInput").ap()

        def dsh(name, shape, dtype=F32):
            dr[name] = nc.dram_tensor(name, list(shape), dtype, kind="Internal",
                                      addr_space="Shared").ap()

        def dlo(name, shape, dtype=F32):
            dr[name] = nc.dram_tensor(name, list(shape), dtype,
                                      kind="Internal").ap()

        din("x_sh", [TS, D])
        din("w_hid", [D, 2 * HID]); din("w_qk", [D, QK])
        din("osg0", [128, 1]); din("osb0", [128, 1])
        din("osg1", [128, 1]); din("osb1", [128, 1])
        din("w_out0", [HID, D])
        din("wg_half", [HID, D]); din("wu_half", [HID, D])
        din("wd_half", [512, INNER])
        din("w_in", [D, 2 * DI])
        din("convw", [128, 64]); din("convb", [128, 16])
        din("w_xp", [DI, 96]); din("w_dt", [DTR, DI]); din("bdt_c", [128, 16])
        din("alog_c", [128, 128]); din("dskip_c", [128, 8])
        din("w_out1", [DI, D]); din("w_rt", [D, NE])
        din("w1", [NE, D, INNER]); din("wo", [INNER, D])
        din("b_hid_r", [1, 2 * HID]); din("b_qk_r", [1, QK])
        din("b_out0_r", [1, D]); din("b1_r", [NE, INNER]); din("bo_r", [1, D])
        din("ln_g_r", [1, D]); din("ln_b_r", [1, D])
        din("ln1_g_r", [1, D]); din("ln1_b_r", [1, D])
        din("ln2_g_r", [1, D]); din("ln2_b_r", [1, D])
        din("id_bf", [128, 128], BF16); din("id_f32", [128, 128], F32)
        din("onesb_r", [1, 128], BF16); din("onesf_r", [1, 128], F32)
        din("onesf_c", [128, 1], F32)
        din("invq2_c", [128, NT]); din("dmask", [128, 2048], BF16)
        out_sh = nc.dram_tensor("out_sh", [TS, D], F32,
                                kind="ExternalOutput").ap()

        dsh("s_v", [T, HID], BF16); dsh("s_kT", [QK, T], BF16)
        dsh("s_wsum", [2, 16])
        dsh("s_wgT", [D, INNER], BF16); dsh("s_wuT", [D, INNER], BF16)
        dsh("s_wdT", [INNER, D], BF16)
        dsh("s_u", [DI, T + 3]); dsh("s_z", [DI, T])
        dsh("s_dlt", [DI, T]); dsh("s_uc", [DI, T])
        dsh("s_bc", [32, T]); dsh("s_g", [DI, T], BF16)
        dlo("l_gate", [NT, 128, HID], BF16)
        dlo("l_xa", [NT, 128, D]); dlo("l_x2", [NT, 128, D])
        dlo("l_xa2", [NT, 128, D])
        dlo("l_glin", [NT, 128, INNER], BF16)
        dlo("l_ulin", [NT, 128, INNER], BF16)
        dlo("l_xq2T", [NT, 128, INNER], BF16)
        dlo("l_x3", [NT, 128, D]); dlo("l_hs", [NT, 128, D])
        dlo("l_part", [NT, 128, D])
        dlo("l_inter", [16, NT, 128, 256], BF16)
        for i in range(8):
            dlo(f"cc_i{i}", [1, 8]); dlo(f"cc_o{i}", [1, 8])
        self.dr = dr

        for s in ["sL", "sLg", "sS", "sT", "sV", "sC", "sB"]:
            self.sem(s)

        t = lambda nm, sh, dt=F32: self.enter(nc.sbuf_tensor(nm, list(sh), dt))
        self.cst = t("cst", [128, 256], BF16)
        self.cstf = t("cstf", [128, 256])
        self.onesbr = t("onesbr", [1, 128], BF16)
        self.onesfr = t("onesfr", [1, 128])
        self.cols = t("cols", [128, 64])
        self.rowb = t("rowb", [1, 8192], BF16)
        self.rowf = t("rowf", [1, 2048])
        self.srow = t("srow", [1, 64])
        self.dmsk = t("dmsk", [128, 2048], BF16)
        self.carry = t("carry", [128, 130])
        self.gbc = t("gbc", [128, 2 * D])
        self.tA = t("tA", [128, 1024])
        self.tB = t("tB", [128, 1024])
        self.tC = t("tC", [128, 1024])
        self.tDmp = t("tDmp", [128, 1024])
        self.bfA = t("bfA", [128, 4096], BF16)
        self.bfB = t("bfB", [128, 4096], BF16)
        self.wsm = t("wsm", [128, 1024], BF16)
        self.ARf = t("ARf", [128, 19456])
        self.ARb = t("ARb", [128, 29696], BF16)
        p = lambda nm, sh, dt=F32: self.enter(nc.psum_tensor(nm, list(sh), dt))
        self.psA = p("psA", [128, 2048])
        self.psB = p("psB", [128, 512])
        self.psC = p("psC", [128, 512])
        self.psTa = p("psTa", [128, 1024], BF16)
        self.psTb = p("psTb", [128, 1024], BF16)

        self.g = self._Rec(self, "g")
        self.tn = self._Rec(self, "tn")
        self.sc = self._Rec(self, "sc")
        self.ve = self._Rec(self, "ve")
        self.syp = self._SyP(self)

        self.emit(out_sh)

        blk = self.enter(nc.Block())
        self.replay_all(blk)
        self.close()
        return nc

    def barrier(self, idx):
        g, dr = self.g, self.dr
        self.wS(g)
        g.memset(self.srow[0:1, 32:40], 1.0)
        g.drain()
        g.dma_start(out=dr[f"cc_i{idx}"][0:1, 0:8],
                    in_=self.srow[0:1, 32:40]).then_inc(self.sLg, 16)
        self.inc("sLg", 16)
        g.wait_ge(self.sLg, self.S["sLg"])
        g.collective_compute(
            "AllReduce", ALU.add,
            replica_groups=[[0, 1], [2, 3], [4, 5], [6, 7]],
            ins=[dr[f"cc_i{idx}"]], outs=[dr[f"cc_o{idx}"]],
        ).then_inc(self.sB, 1)
        self.inc("sB")
        for e in [self.syp, self.g, self.tn, self.sc, self.ve]:
            e.wait_ge(self.sB, self.S["sB"])

    def mm_done(self):
        q = self.tn._q()
        q[-1][3].append((self.sT, 1))
        return self.inc("sT")

    def evac(self, out_ap, psum_ap, func=ACT.Copy, scale=1.0, bias=0.0,
             tcount=None):
        sc = self.sc
        if func == ACT.Copy:
            bias = float(bias)
            if isinstance(scale, (float, np.floating)):
                scale = float(scale)
        else:
            if isinstance(scale, (float, np.floating)):
                scale = np.float32(scale)
            if isinstance(bias, (float, np.floating)):
                bias = np.float32(bias)
        sc.wait_ge(self.sT, tcount if tcount is not None else self.S["sT"])
        sc.activation(out=out_ap, in_=psum_ap, func=func, scale=scale,
                      bias=bias)
        sc.drain().then_inc(self.sC, 1)
        return self.inc("sC")

    def ln_stats(self, x_ap, eps, n=1024):
        sc, ve = self.sc, self.ve
        mu = self.cols[:, 20:21]
        rc = self.cols[:, 21:22]
        tc = self.cols[:, 22:23]
        sc.wait_ge(self.sV, self.S["sV"])
        sc.activation(out=self.tDmp[:, 0:n], in_=x_ap, func=ACT.Copy,
                      accum_out=mu)
        sc.drain()
        sc.activation(out=self.tDmp[:, 0:n], in_=x_ap, func=ACT.Square,
                      accum_out=tc)
        sc.drain().then_inc(self.sC, 1)
        ve.wait_ge(self.sC, self.inc("sC"))
        ve.tensor_scalar(out=mu, in0=mu, scalar1=1.0 / n, scalar2=None,
                         op0=ALU.mult)
        ve.drain()
        ve.tensor_scalar(out=tc, in0=tc, scalar1=1.0 / n, scalar2=None,
                         op0=ALU.mult)
        ve.drain()
        ve.scalar_tensor_tensor(out=rc, in0=mu, scalar=-1.0, in1=mu,
                                op0=ALU.mult, op1=ALU.mult)
        ve.drain()
        ve.tensor_tensor(out=tc, in0=tc, in1=rc, op=ALU.add)
        ve.drain().then_inc(self.sV, 1)
        sc.wait_ge(self.sV, self.inc("sV"))
        sc.activation(out=rc, in_=tc, func=ACT.Sqrt, bias=np.float32(eps), scale=np.float32(1.0))
        sc.drain().then_inc(self.sC, 1)
        ve.wait_ge(self.sC, self.inc("sC"))
        ve.reciprocal(out=rc, in_=rc)
        ve.drain()
        return mu, rc

    def ln_norm(self, x_ap, out_ap, mu, rc):
        self.ve.tensor_scalar(out=out_ap, in0=x_ap, scalar1=mu, scalar2=rc,
                              op0=ALU.subtract, op1=ALU.mult)
        self.ve.drain()

    def bcast_f32(self, row_ap, out_ap, n):
        tn = self.tn
        tn.wait_ge(self.sC, self.S["sC"])
        tn.matmul(out=self.psB[:, 0:n], lhsT=self.onesfr[0:1, :],
                  rhs=row_ap, start=True, stop=True)
        self.evac(out_ap, self.psB[:, 0:n], tcount=self.mm_done())

    def load_gbc(self, grow, brow):
        for hh in range(2):
            self.bcast_f32(grow[0:1, hh * 512:(hh + 1) * 512],
                           self.gbc[:, hh * 512:(hh + 1) * 512], 512)
            self.bcast_f32(brow[0:1, hh * 512:(hh + 1) * 512],
                           self.gbc[:, D + hh * 512:D + (hh + 1) * 512], 512)

    def transpose_to(self, src_fn, nblk, dst_fn, f32=False):
        tn, sc = self.tn, self.sc
        ident = self.cstf[:, 0:128] if f32 else self.cst[:, 0:128]
        pos, bank = 0, 0
        grp = 4 if f32 else 8
        while pos < nblk:
            cnt = min(grp, nblk - pos)
            if f32:
                ps = self.psA[:, bank * 512:(bank + 1) * 512]
            else:
                ps = (self.psTa, self.psTb)[bank][:, 0:1024]
            tn.wait_ge(self.sC, self.S["sC"])
            for i in range(cnt):
                tn.transpose(out=ps[:, i * 128:(i + 1) * 128],
                             in_=src_fn(pos + i), identity=ident)
            tcnt = self.mm_done()
            sc.wait_ge(self.sT, tcnt)
            for i in range(cnt):
                sc.copy(out=dst_fn(pos + i), in_=ps[:, i * 128:(i + 1) * 128])
            sc.drain().then_inc(self.sC, 1)
            self.inc("sC")
            pos += cnt
            bank ^= 1

    def emit(self, out_sh):
        dr = self.dr
        sy, g, tn, sc, ve = self.syp, self.g, self.tn, self.sc, self.ve
        xres = self.ARf[:, 0:16384]

        xsv = dr["x_sh"].rearrange("(a p) m -> p a m", p=128)
        for gg in range(4):
            self.dma(sy, xres[:, gg * 4096:(gg + 1) * 4096]
                     .rearrange("p (a m) -> p a m", a=4),
                     xsv[:, gg * 4:(gg + 1) * 4, :], "sL")
        self.dma(sy, self.cstf[:, 0:128], dr["id_f32"], "sL")
        self.dma(sy, self.cstf[:, 128:129], dr["onesf_c"], "sL")
        self.dma(sy, self.onesfr[:, :], dr["onesf_r"], "sL")
        self.dma(sy, self.dmsk[:, :], dr["dmask"], "sL")
        for i, nm in enumerate(["osg0", "osb0", "osg1", "osb1"]):
            self.dma(sy, self.cols[:, i:i + 1], dr[nm], "sL")
        self.dma(sy, self.cols[:, 4:4 + NT], dr["invq2_c"], "sL")
        self.dma(sy, self.rowf[0:1, 0:D], dr["ln_g_r"], "sL")
        self.dma(sy, self.rowf[0:1, D:2 * D], dr["ln_b_r"], "sL")
        self.dma(g, self.cst[:, 0:128], dr["id_bf"], "sL")
        self.dma(g, self.onesbr[:, :], dr["onesb_r"], "sL")
        self.dma(g, self.wsm[:, 0:8 * QK].rearrange("p (c o) -> p c o", c=8),
                 dr["w_qk"].rearrange("(c p) o -> p c o", p=128), "sL")
        self.dma(g, self.rowb[0:1, 0:2 * HID], dr["b_hid_r"], "sL")
        self.dma(g, self.rowb[0:1, 4096:4096 + QK], dr["b_qk_r"], "sL")
        self.dma(g, self.rowb[0:1, 4224:4224 + D], dr["b_out0_r"], "sL")
        self.dma(g, self.rowb[0:1, 5504:5504 + D], dr["bo_r"], "sL")
        for e in [tn, sc, ve]:
            self.wL(e)
        if self.nphase >= 0:
            self.load_gbc(self.rowf[0:1, 0:D], self.rowf[0:1, D:2 * D])

        if self.nphase <= 0:
            return self.finish(out_sh, xres)
        self.phase_a1()
        if self.nphase == 1:
            return self.finish(out_sh, xres)
        self.barrier(0)
        self.phase_prep()
        self.barrier(1)
        self.phase_a2()
        if self.nphase == 2:
            return self.finish(out_sh, xres)
        self.phase_a3()
        if self.nphase == 3:
            return self.finish(out_sh, xres)
        self.phase_b()
        self.barrier(2)
        self.phase_c()
        self.barrier(3)
        if self.nphase == 4:
            return self.finish(out_sh, xres, "l_x2")
        self.phase_d()
        self.barrier(4)
        if self.nphase == 5:
            return self.finish(out_sh, xres, "l_x2")
        self.phase_e(out_sh)

    def finish(self, out_sh, xres, reload=None):
        sy = self.syp
        sy.wait_ge(self.sV, self.S["sV"])
        sy.wait_ge(self.sC, self.S["sC"])
        sy.wait_ge(self.sS, self.S["sS"])
        if reload:
            xi = self.dr[reload].rearrange("a p m -> p a m")
            for gg in range(4):
                self.dma(sy, xres[:, gg * 4096:(gg + 1) * 4096]
                         .rearrange("p (a m) -> p a m", a=4),
                         xi[:, gg * 4:(gg + 1) * 4, :], "sL")
            self.wL(sy)
        xo = out_sh.rearrange("(a p) m -> p a m", p=128)
        for gg in range(4):
            self.dma(sy, xo[:, gg * 4:(gg + 1) * 4, :],
                     xres[:, gg * 4096:(gg + 1) * 4096]
                     .rearrange("p (a m) -> p a m", a=4), "sS")
        sy.wait_ge(self.sS, self.S["sS"])

    def phase_a1(self):
        dr = self.dr
        sy, g, tn, sc, ve = self.syp, self.g, self.tn, self.sc, self.ve
        xres = self.ARf[:, 0:16384]
        naT = self.ARb[:, 0:16384]
        QT = self.ARb[:, 16384:18432]
        kTl = self.ARb[:, 18432:20480]
        wch = [self.ARb[:, 20480:24576], self.ARb[:, 24576:28672]]
        nabf = self.bfA[:, 1024:2048]
        qkbf = self.bfB[:, 2048:2176]
        qkT = self.bfB[:, 2176:2304]
        evb = self.bfB[:, 2304:2816]
        psB, psA = self.psB, self.psA

        for tt in range(NT):
            x_ap = xres[:, tt * 1024:(tt + 1) * 1024]
            xa = self.tA[:, 0:1024]
            self.wS(ve)
            mu, rc = self.ln_stats(x_ap, 1e-5)
            self.ln_norm(x_ap, xa, mu, rc)
            ve.drain().then_inc(self.sV, 1)
            sy.wait_ge(self.sV, self.inc("sV"))
            self.dma(sy, dr["l_xa"][tt], xa, "sS")
            mu, rc = self.ln_stats(xa, 1e-5)
            self.ln_norm(xa, self.tB[:, 0:1024], mu, rc)
            ve.tensor_tensor(out=self.tB[:, 0:1024], in0=self.tB[:, 0:1024],
                             in1=self.gbc[:, 0:1024], op=ALU.mult)
            ve.drain()
            ve.tensor_tensor(out=nabf, in0=self.tB[:, 0:1024],
                             in1=self.gbc[:, 1024:2048], op=ALU.add)
            ve.drain().then_inc(self.sV, 1)
            tn.wait_ge(self.sV, self.inc("sV"))
            self.transpose_to(lambda i: nabf[:, i * 128:(i + 1) * 128], 8,
                              lambda i: naT[:, tt * 1024 + i * 128:
                                            tt * 1024 + (i + 1) * 128])
            tn.wait_ge(self.sC, self.S["sC"])
            tn.matmul(out=psB[:, 0:QK], lhsT=self.onesbr[0:1, :],
                      rhs=self.rowb[0:1, 4096:4096 + QK], start=True, stop=False)
            for ic in range(8):
                tn.matmul(out=psB[:, 0:QK],
                          lhsT=naT[:, tt * 1024 + ic * 128:
                                    tt * 1024 + (ic + 1) * 128],
                          rhs=self.wsm[:, ic * QK:(ic + 1) * QK],
                          start=False, stop=(ic == 7))
            self.evac(qkbf, psB[:, 0:QK], func=ACT.Silu, tcount=self.mm_done())
            self.transpose_to(lambda i: qkbf, 1, lambda i: qkT)
            ve.wait_ge(self.sC, self.S["sC"])
            ve.tensor_scalar(out=QT[:, tt * 128:(tt + 1) * 128], in0=qkT,
                             scalar1=self.cols[:, 0:1], scalar2=self.cols[:, 1:2],
                             op0=ALU.mult, op1=ALU.add)
            ve.drain()
            ve.tensor_scalar(out=kTl[:, tt * 128:(tt + 1) * 128], in0=qkT,
                             scalar1=self.cols[:, 2:3], scalar2=self.cols[:, 3:4],
                             op0=ALU.mult, op1=ALU.add)
            ve.drain().then_inc(self.sV, 1)
            self.inc("sV")

        sy.wait_ge(self.sV, self.S["sV"])
        for bi in range(4):
            self.sydma2(lambda h, bi=bi: dr["s_kT"][:, QBLOCKS[h][bi] * 512:
                                                    QBLOCKS[h][bi] * 512 + 512],
                        kTl[:, bi * 512:(bi + 1) * 512], "sS")

        whid = dr["w_hid"].rearrange("(c p) o -> p c o", p=128)
        for oc in range(8):
            buf = wch[oc % 2]
            if oc >= 2:
                g.wait_ge(self.sT, self.S["sT"])
            self.dma(g, buf.rearrange("p (c o) -> p c o", c=8),
                     whid[:, :, oc * 512:(oc + 1) * 512], "sL")
            self.wL(tn)
            for tt in range(NT):
                bid = ("psA", tt % 4)
                pslot = psA[:, (tt % 4) * 512:((tt % 4) + 1) * 512]
                if bid in self.bank_ev:
                    tn.wait_ge(self.sC, self.bank_ev[bid])
                tn.matmul(out=pslot, lhsT=self.onesbr[0:1, :],
                          rhs=self.rowb[0:1, oc * 512:(oc + 1) * 512],
                          start=True, stop=False)
                for ic in range(8):
                    tn.matmul(out=pslot,
                              lhsT=naT[:, tt * 1024 + ic * 128:
                                        tt * 1024 + (ic + 1) * 128],
                              rhs=buf[:, ic * 512:(ic + 1) * 512],
                              start=False, stop=(ic == 7))
                tcnt = self.mm_done()
                self.wS(sc)
                self.bank_ev[bid] = self.evac(evb, pslot, func=ACT.Silu,
                                              tcount=tcnt)
                sy.wait_ge(self.sC, self.S["sC"])
                if oc < 4:
                    self.sydma2(lambda h, tt=tt, oc=oc:
                                dr["s_v"][tglob(tt, h):tglob(tt, h) + 128,
                                          oc * 512:(oc + 1) * 512],
                                evb, "sS")
                else:
                    self.dma(sy, dr["l_gate"][tt][:, (oc - 4) * 512:
                                                  (oc - 3) * 512], evb, "sS")

        acc, red = self.cols[:, 24:25], self.cols[:, 25:26]
        for wi, nm in enumerate(["wg_half", "wu_half", "wd_half"]):
            ve.wait_ge(self.sT, self.S["sT"])
            ve.memset(acc, 0.0)
            ve.drain()
            if wi < 2:
                wap = dr[nm].rearrange("(a p) m -> p a m", p=128)
                loads = [(wap[:, ti:ti + 1, :],) for ti in range(16)]
            else:
                loads = [(dr[nm][ti * 128:(ti + 1) * 128,
                                 q4 * 1024:(q4 + 1) * 1024],)
                         for ti in range(4) for q4 in range(4)]
            for (src_ap,) in loads:
                ve.drain().then_inc(self.sV, 1)
                sy.wait_ge(self.sV, self.inc("sV"))
                self.dma(sy, self.tA[:, 0:1024], src_ap, "sL")
                self.wL(ve)
                ve.tensor_reduce(out=red, in_=self.tA[:, 0:1024], axis=AX,
                                 op=ALU.add, apply_absolute_value=True)
                ve.drain()
                ve.tensor_tensor(out=acc, in0=acc, in1=red, op=ALU.add)
                ve.drain()
            ve.drain().then_inc(self.sV, 1)
            tn.wait_ge(self.sV, self.inc("sV"))
            tn.wait_ge(self.sC, self.S["sC"])
            tn.matmul(out=self.psB[0:1, 0:1], lhsT=acc,
                      rhs=self.cstf[:, 128:129], start=True, stop=True)
            self.evac(self.srow[0:1, wi:wi + 1], self.psB[0:1, 0:1],
                      tcount=self.mm_done())
        sy.wait_ge(self.sC, self.S["sC"])
        self.sydma2(lambda h: dr["s_wsum"][h:h + 1, 0:3],
                    self.srow[0:1, 0:3], "sS")

    def phase_prep(self):
        dr = self.dr
        sy, g, tn, sc, ve = self.syp, self.g, self.tn, self.sc, self.ve
        self.dma(g, self.srow[0:1, 8:11], dr["s_wsum"][0:1, 0:3], "sL")
        self.dma(g, self.srow[0:1, 16:19], dr["s_wsum"][1:2, 0:3], "sL")
        self.wL(ve)
        ve.tensor_tensor(out=self.srow[0:1, 12:15], in0=self.srow[0:1, 8:11],
                         in1=self.srow[0:1, 16:19], op=ALU.add)
        ve.drain()
        ve.tensor_scalar(out=self.srow[0:1, 12:15], in0=self.srow[0:1, 12:15],
                         scalar1=1.0 / (4096.0 * 1024.0), scalar2=1e-5,
                         op0=ALU.mult, op1=ALU.max)
        ve.drain()
        ve.reciprocal(out=self.srow[0:1, 20:23], in_=self.srow[0:1, 12:15])
        ve.drain().then_inc(self.sV, 1)
        tn.wait_ge(self.sV, self.inc("sV"))
        for j in range(3):
            self.bcast_f32(self.srow[0:1, 12 + j:13 + j],
                           self.cols[:, 26 + j:27 + j], 1)
            self.bcast_f32(self.srow[0:1, 20 + j:21 + j],
                           self.cols[:, 29 + j:30 + j], 1)
        qt = [self.ARb[:, j * 4096:(j + 1) * 4096] for j in range(4)]
        ev = self.ARb[:, 16384:16896]
        scratch = self.ARf[:, 16384:18432]
        for (nm, dst, O, I, swc) in [("wg_half", "s_wgT", 2048, 1024, 29),
                                     ("wu_half", "s_wuT", 2048, 1024, 30),
                                     ("wd_half", "s_wdT", 512, 4096, 31)]:
            for og in range(O // 512):
                for j4 in range(4):
                    nh = 1 if I == 1024 else 2
                    for hh in range(nh):
                        w = I // nh
                        src = scratch[:, 0:w]
                        ve.drain().then_inc(self.sV, 1)
                        sy.wait_ge(self.sV, self.inc("sV"))
                        self.dma(sy, src,
                                 dr[nm][og * 512 + j4 * 128:
                                        og * 512 + (j4 + 1) * 128,
                                        hh * w:(hh + 1) * w], "sL")
                        self.wL(ve)
                        ve.wait_ge(self.sT, self.S["sT"])
                        self._quant3(src, qt[j4][:, hh * w:(hh + 1) * w], swc)
                ve.drain().then_inc(self.sV, 1)
                tn.wait_ge(self.sV, self.inc("sV"))
                for ic in range(I // 128):
                    ps = (self.psTa, self.psTb)[ic % 2][:, 0:512]
                    tn.wait_ge(self.sC, self.S["sC"])
                    for j4 in range(4):
                        tn.transpose(out=ps[:, j4 * 128:(j4 + 1) * 128],
                                     in_=qt[j4][:, ic * 128:(ic + 1) * 128],
                                     identity=self.cst[:, 0:128])
                    tcnt = self.mm_done()
                    sc.wait_ge(self.sT, tcnt)
                    self.wS(sc)
                    sc.copy(out=ev, in_=ps[:, 0:512])
                    sc.drain().then_inc(self.sC, 1)
                    self.inc("sC")
                    sy.wait_ge(self.sC, self.S["sC"])
                    self.sydma2(
                        lambda h, dst=dst, ic=ic, og=og, O=O:
                        dr[dst][ic * 128:(ic + 1) * 128,
                                ((h * 2048 + og * 512) if O == 2048
                                 else h * 512):
                                ((h * 2048 + og * 512) if O == 2048
                                 else h * 512) + 512],
                        ev, "sS")

    def _quant3(self, src, dst_bf, swc):
        ve = self.ve
        ve.tensor_scalar(out=src, in0=src, scalar1=self.cols[:, swc:swc + 1],
                         scalar2=MAGIC, op0=ALU.mult, op1=ALU.add)
        ve.drain()
        ve.tensor_scalar(out=src, in0=src, scalar1=MAGIC, scalar2=1.0,
                         op0=ALU.subtract, op1=ALU.min)
        ve.drain()
        ve.tensor_scalar(out=dst_bf, in0=src, scalar1=-1.0, scalar2=None,
                         op0=ALU.max)
        ve.drain()

    def phase_a2(self):
        dr = self.dr
        sy, g, tn, sc, ve = self.syp, self.g, self.tn, self.sc, self.ve
        xres = self.ARf[:, 0:16384]
        ogf = self.ARf[:, 16384:18432]
        xat = self.ARf[:, 18432:19456]
        wout0 = self.ARb[:, 0:16384]
        QT = self.ARb[:, 16384:18432]
        kTf = self.ARb[:, 18432:22528]
        vsl = [self.ARb[:, 22528:24576], self.ARb[:, 24576:26624],
               self.bfB[:, 2048:4096], self.bfA[:, 2048:4096]]
        ogT = self.ARb[:, 26624:28672]
        attn = self.ARb[:, 28672:29696]
        og = self.bfB[:, 0:2048]
        gatet = self.bfA[:, 0:2048]

        self.dma(g, wout0.rearrange("p (c o) -> p c o", c=16),
                 dr["w_out0"].rearrange("(c p) o -> p c o", p=128), "sL")
        self.dma(g, kTf, dr["s_kT"], "sL")
        for e in [tn, sc, ve, sy]:
            self.wL(e)

        for h in self.role_region([tn, sc, ve]):
            self._rmq = []
            for tt in range(NT):
                gb = QBLOCKS[h][tt // 4]
                nk = gb * 4 + (tt % 4) + 1
                sy.wait_ge(self.sV, self.S["sV"])
                self.dma(sy, gatet, dr["l_gate"][tt], "sL")
                self.dma(sy, xat, dr["l_xa"][tt], "sL")
                first_av = True
                for rnd in range((nk + 1) // 2):
                    ks = [k for k in (2 * rnd, 2 * rnd + 1) if k < nk]
                    if len(self._rmq) >= 2:
                        sy.wait_ge(self.sT, self._rmq[-2])
                    for kj in ks:
                        self.dma(sy, vsl[kj % 4],
                                 dr["s_v"][kj * 128:(kj + 1) * 128, :], "sL")
                    self.wL(tn)
                    for kj in ks:
                        ps = (self.psB, self.psC)[kj % 2]
                        bid = ("ps", kj % 2)
                        if bid in self.bank_ev:
                            tn.wait_ge(self.sC, self.bank_ev[bid])
                        tn.matmul(out=ps[:, 0:128],
                                  lhsT=kTf[:, kj * 128:(kj + 1) * 128],
                                  rhs=QT[:, tt * 128:(tt + 1) * 128],
                                  start=True, stop=True)
                        tcnt = self.mm_done()
                        sc.wait_ge(self.sT, tcnt)
                        sc.activation(out=self.tA[:, 0:128], in_=ps[:, 0:128],
                                      func=ACT.Relu)
                        sc.drain()
                        sl = attn[:, (kj % 8) * 128:(kj % 8 + 1) * 128]
                        sc.activation(out=sl, in_=self.tA[:, 0:128],
                                      func=ACT.Square)
                        sc.drain().then_inc(self.sC, 1)
                        self.bank_ev[bid] = self.inc("sC")
                        if gb * 4 <= kj < gb * 4 + 4:
                            ve.wait_ge(self.sC, self.S["sC"])
                            dj = kj - gb * 4
                            ve.tensor_tensor(
                                out=sl, in0=sl,
                                in1=self.dmsk[:, dj * 512 + (tt % 4) * 128:
                                              dj * 512 + (tt % 4 + 1) * 128],
                                op=ALU.mult)
                            ve.drain().then_inc(self.sV, 1)
                            self.inc("sV")
                            tn.wait_ge(self.sV, self.S["sV"])
                        else:
                            tn.wait_ge(self.sC, self.S["sC"])
                    if first_av:
                        for q in range(4):
                            bid = ("psA", q)
                            if bid in self.bank_ev:
                                tn.wait_ge(self.sC, self.bank_ev[bid])
                        first_av = False
                    for kj in ks:
                        for oc in range(4):
                            tn.matmul(out=self.psA[:, oc * 512:(oc + 1) * 512],
                                      lhsT=attn[:, (kj % 8) * 128:
                                                (kj % 8 + 1) * 128],
                                      rhs=vsl[kj % 4][:, oc * 512:
                                                      (oc + 1) * 512],
                                      start=(kj == 0), stop=(kj == nk - 1))
                    tcnt = self.mm_done()
                    self._rmq.append(tcnt)
                c = self.evac(ogf, self.psA[:, 0:2048], func=ACT.Copy,
                              scale=self.cols[:, 4 + tt:5 + tt], tcount=tcnt)
                for q in range(4):
                    self.bank_ev[("psA", q)] = c
                ve.wait_ge(self.sC, c)
                self.wL(ve)
                ve.tensor_tensor(out=og, in0=ogf, in1=gatet, op=ALU.mult)
                ve.drain().then_inc(self.sV, 1)
                tn.wait_ge(self.sV, self.inc("sV"))
                self.transpose_to(lambda i: og[:, i * 128:(i + 1) * 128], 16,
                                  lambda i: ogT[:, i * 128:(i + 1) * 128])
                tn.wait_ge(self.sC, self.S["sC"])
                for half in range(2):
                    ps = (self.psB, self.psC)[half]
                    bid = ("ps", half)
                    if bid in self.bank_ev:
                        tn.wait_ge(self.sC, self.bank_ev[bid])
                    tn.matmul(out=ps[:, 0:512], lhsT=self.onesbr[0:1, :],
                              rhs=self.rowb[0:1, 4224 + half * 512:
                                            4224 + (half + 1) * 512],
                              start=True, stop=False)
                    for c16 in range(16):
                        tn.matmul(out=ps[:, 0:512],
                                  lhsT=ogT[:, c16 * 128:(c16 + 1) * 128],
                                  rhs=wout0[:, c16 * 1024 + half * 512:
                                            c16 * 1024 + (half + 1) * 512],
                                  start=False, stop=(c16 == 15))
                tcnt = self.mm_done()
                self.bank_ev[("ps", 0)] = self.evac(
                    self.tA[:, 0:512], self.psB[:, 0:512], tcount=tcnt)
                self.bank_ev[("ps", 1)] = self.evac(
                    self.tA[:, 512:1024], self.psC[:, 0:512], tcount=tcnt)
                ve.wait_ge(self.sC, self.S["sC"])
                ve.tensor_tensor(out=self.tA[:, 0:1024], in0=self.tA[:, 0:1024],
                                 in1=xat, op=ALU.add)
                ve.drain().then_inc(self.sV, 1)
                self.inc("sV")
                mu, rc = self.ln_stats(self.tA[:, 0:1024], 1e-5)
                self.ln_norm(self.tA[:, 0:1024], self.tB[:, 0:1024], mu, rc)
                ve.tensor_tensor(out=xres[:, tt * 1024:(tt + 1) * 1024],
                                 in0=xres[:, tt * 1024:(tt + 1) * 1024],
                                 in1=self.tB[:, 0:1024], op=ALU.add)
                ve.drain().then_inc(self.sV, 1)
                self.inc("sV")

    def phase_a3(self):
        dr = self.dr
        sy, g, tn, sc, ve = self.syp, self.g, self.tn, self.sc, self.ve
        g.wait_ge(self.sT, self.S["sT"])
        g.wait_ge(self.sV, self.S["sV"])
        xres = self.ARf[:, 0:16384]
        xqT = self.ARb[:, 0:16384]
        wgc = self.ARb[:, 16384:20480]
        wuc = self.ARb[:, 20480:24576]
        xqbf = self.ARb[:, 24576:25600]
        evg = self.ARb[:, 25600:26112]
        evu = self.ARb[:, 26112:26624]
        for tt in range(NT):
            x_ap = xres[:, tt * 1024:(tt + 1) * 1024]
            mu, rc = self.ln_stats(x_ap, 1e-5)
            self.ln_norm(x_ap, self.tA[:, 0:1024], mu, rc)
            ve.reduce_max(out=self.cols[:, 23:24], in_=self.tA[:, 0:1024],
                          axis=AX, apply_absolute_value=True)
            ve.drain()
            ve.tensor_scalar(out=self.cols[:, 23:24], in0=self.cols[:, 23:24],
                             scalar1=1e-5, scalar2=None, op0=ALU.max)
            ve.drain()
            ve.reciprocal(out=self.cols[:, 40:41], in_=self.cols[:, 23:24])
            ve.drain()
            ve.tensor_scalar(out=self.cols[:, 40:41], in0=self.cols[:, 40:41],
                             scalar1=127.0, scalar2=None, op0=ALU.mult)
            ve.drain()
            ve.tensor_scalar(out=self.cols[:, 41:42], in0=self.cols[:, 23:24],
                             scalar1=1.0 / 127.0, scalar2=None, op0=ALU.mult)
            ve.drain()
            ve.tensor_tensor(out=self.carry[:, tt:tt + 1],
                             in0=self.cols[:, 41:42], in1=self.cols[:, 26:27],
                             op=ALU.mult)
            ve.drain()
            ve.tensor_tensor(out=self.carry[:, 16 + tt:17 + tt],
                             in0=self.cols[:, 41:42], in1=self.cols[:, 27:28],
                             op=ALU.mult)
            ve.drain()
            ve.tensor_scalar(out=self.tA[:, 0:1024], in0=self.tA[:, 0:1024],
                             scalar1=self.cols[:, 40:41], scalar2=MAGIC,
                             op0=ALU.mult, op1=ALU.add)
            ve.drain()
            ve.tensor_scalar(out=xqbf, in0=self.tA[:, 0:1024], scalar1=MAGIC,
                             scalar2=None, op0=ALU.subtract)
            ve.drain().then_inc(self.sV, 1)
            tn.wait_ge(self.sV, self.inc("sV"))
            self.transpose_to(lambda i: xqbf[:, i * 128:(i + 1) * 128], 8,
                              lambda i: xqT[:, tt * 1024 + i * 128:
                                            tt * 1024 + (i + 1) * 128])

        wgT = dr["s_wgT"].rearrange("(c p) o -> p c o", p=128)
        wuT = dr["s_wuT"].rearrange("(c p) o -> p c o", p=128)
        for oc in range(8):
            if oc >= 1:
                g.wait_ge(self.sT, self.S["sT"])
            self.dma(g, wgc.rearrange("p (c o) -> p c o", c=8),
                     wgT[:, :, oc * 512:(oc + 1) * 512], "sL")
            self.dma(g, wuc.rearrange("p (c o) -> p c o", c=8),
                     wuT[:, :, oc * 512:(oc + 1) * 512], "sL")
            self.wL(tn)
            tn.wait_ge(self.sC, self.S["sC"])
            for tt in range(NT):
                pg = 2 * (tt % 2)
                for w2 in range(2):
                    bk = pg + w2
                    ps = self.psA[:, bk * 512:(bk + 1) * 512]
                    bid = ("psA", bk)
                    if bid in self.bank_ev:
                        tn.wait_ge(self.sC, self.bank_ev[bid])
                    wbuf = (wgc, wuc)[w2]
                    for ic in range(8):
                        tn.matmul(out=ps,
                                  lhsT=xqT[:, tt * 1024 + ic * 128:
                                           tt * 1024 + (ic + 1) * 128],
                                  rhs=wbuf[:, ic * 512:(ic + 1) * 512],
                                  start=(ic == 0), stop=(ic == 7))
                tcnt = self.mm_done()
                self.wS(sc)
                evg2 = (evg, self.bfA[:, 0:512])[tt % 2]
                evu2 = (evu, self.bfA[:, 512:1024])[tt % 2]
                self.bank_ev[("psA", pg)] = self.evac(
                    evg2, self.psA[:, pg * 512:(pg + 1) * 512], func=ACT.Silu,
                    scale=self.carry[:, tt:tt + 1], tcount=tcnt)
                self.bank_ev[("psA", pg + 1)] = self.evac(
                    evu2, self.psA[:, (pg + 1) * 512:(pg + 2) * 512],
                    func=ACT.Copy,
                    scale=self.carry[:, 16 + tt:17 + tt], tcount=tcnt)
                sy.wait_ge(self.sC, self.S["sC"])
                self.dma(sy, dr["l_glin"][tt][:, oc * 512:(oc + 1) * 512],
                         evg2, "sS")
                self.dma(sy, dr["l_ulin"][tt][:, oc * 512:(oc + 1) * 512],
                         evu2, "sS")

        g.wait_ge(self.sT, self.S["sT"])
        glt = self.ARb[:, 16384:20480]
        ult = self.ARb[:, 20480:24576]
        h2 = self.ARb[:, 24576:28672]
        xq2 = self.ARb[:, 16384:20480]
        for tt in range(NT):
            ve.drain().then_inc(self.sV, 1)
            sy.wait_ge(self.sV, self.inc("sV"))
            sy.wait_ge(self.sS, self.S["sS"])
            self.dma(sy, glt, dr["l_glin"][tt], "sL")
            self.dma(sy, ult, dr["l_ulin"][tt], "sL")
            self.wL(ve)
            ve.tensor_tensor(out=h2, in0=glt, in1=ult, op=ALU.mult)
            ve.drain()
            ve.reduce_max(out=self.cols[:, 23:24], in_=h2, axis=AX,
                          apply_absolute_value=True)
            ve.drain()
            ve.tensor_scalar(out=self.cols[:, 23:24], in0=self.cols[:, 23:24],
                             scalar1=1e-5, scalar2=None, op0=ALU.max)
            ve.drain()
            ve.reciprocal(out=self.cols[:, 40:41], in_=self.cols[:, 23:24])
            ve.drain()
            ve.tensor_scalar(out=self.cols[:, 40:41], in0=self.cols[:, 40:41],
                             scalar1=127.0, scalar2=None, op0=ALU.mult)
            ve.drain()
            ve.tensor_scalar(out=self.cols[:, 41:42], in0=self.cols[:, 23:24],
                             scalar1=1.0 / 127.0, scalar2=None, op0=ALU.mult)
            ve.drain()
            ve.tensor_tensor(out=self.carry[:, 32 + tt:33 + tt],
                             in0=self.cols[:, 41:42], in1=self.cols[:, 28:29],
                             op=ALU.mult)
            ve.drain()
            qtmp = self.ARf[:, 16384:18432]
            for qh in range(2):
                ve.tensor_scalar(out=qtmp, in0=h2[:, qh * 2048:(qh + 1) * 2048],
                                 scalar1=self.cols[:, 40:41],
                                 scalar2=MAGIC, op0=ALU.mult, op1=ALU.add)
                ve.drain()
                ve.tensor_scalar(out=xq2[:, qh * 2048:(qh + 1) * 2048],
                                 in0=qtmp, scalar1=MAGIC, scalar2=None,
                                 op0=ALU.subtract)
                ve.drain()
            ve.drain().then_inc(self.sV, 1)
            tn.wait_ge(self.sV, self.inc("sV"))
            self.transpose_to(lambda i: xq2[:, i * 128:(i + 1) * 128], 32,
                              lambda i: ult[:, i * 128:(i + 1) * 128])
            sy.wait_ge(self.sC, self.S["sC"])
            self.dma(sy, dr["l_xq2T"][tt], ult, "sS")

        wd = self.ARb[:, 0:16384]
        xq2s = [self.ARb[:, 16384 + i * 4096:16384 + (i + 1) * 4096]
                for i in range(2)]
        wdT = dr["s_wdT"].rearrange("(c p) o -> p c o", p=128)
        sy.wait_ge(self.sS, self.S["sS"])
        for rnd in range(2):
            g.wait_ge(self.sT, self.S["sT"])
            self.dma(g, wd.rearrange("p (c o) -> p c o", c=16),
                     wdT[:, 16 * rnd:16 * rnd + 16, :], "sL")
            self.wL(tn)
            for tt in range(NT):
                sy.wait_ge(self.sT, self.S["sT"])
                self.dma(sy, xq2s[tt % 2], dr["l_xq2T"][tt], "sL")
                self.wL(tn)
                for half in range(2):
                    ps = (self.psB, self.psC)[half]
                    if ("ps", half) in self.bank_ev:
                        tn.wait_ge(self.sC, self.bank_ev[("ps", half)])
                    for c16 in range(16):
                        cx = 16 * rnd + c16
                        tn.matmul(out=ps[:, 0:512],
                                  lhsT=xq2s[tt % 2][:, cx * 128:(cx + 1) * 128],
                                  rhs=wd[:, c16 * 1024 + half * 512:
                                         c16 * 1024 + (half + 1) * 512],
                                  start=(c16 == 0), stop=(c16 == 15))
                tcnt = self.mm_done()
                sc.wait_ge(self.sV, self.S["sV"])
                self.wS(sc)
                part = self.ARf[:, 16384 + (tt % 2) * 1024:
                                16384 + (tt % 2 + 1) * 1024]
                if rnd == 0:
                    self.bank_ev[("ps", 0)] = self.evac(
                        self.tB[:, 0:512], self.psB[:, 0:512],
                        scale=self.carry[:, 32 + tt:33 + tt], tcount=tcnt)
                    self.bank_ev[("ps", 1)] = self.evac(
                        self.tB[:, 512:1024], self.psC[:, 0:512],
                        scale=self.carry[:, 32 + tt:33 + tt], tcount=tcnt)
                    sy.wait_ge(self.sC, self.S["sC"])
                    self.dma(sy, dr["l_part"][tt], self.tB[:, 0:1024], "sS")
                else:
                    self.bank_ev[("ps", 0)] = self.evac(
                        self.tB[:, 0:512], self.psB[:, 0:512],
                        scale=self.carry[:, 32 + tt:33 + tt], tcount=tcnt)
                    self.bank_ev[("ps", 1)] = self.evac(
                        self.tB[:, 512:1024], self.psC[:, 0:512],
                        scale=self.carry[:, 32 + tt:33 + tt], tcount=tcnt)
                    sy.wait_ge(self.sC, self.S["sC"])
                    self.dma(sy, part, dr["l_part"][tt], "sL")
                    ve.wait_ge(self.sC, self.S["sC"])
                    self.wL(ve)
                    ve.tensor_tensor(out=self.tB[:, 0:1024], in0=part,
                                     in1=self.tB[:, 0:1024], op=ALU.add)
                    ve.drain().then_inc(self.sV, 1)
                    self.inc("sV")
                    mu, rc = self.ln_stats(self.tB[:, 0:1024], 1e-5)
                    self.ln_norm(self.tB[:, 0:1024], self.tC[:, 0:1024], mu, rc)
                    ve.tensor_tensor(out=xres[:, tt * 1024:(tt + 1) * 1024],
                                     in0=xres[:, tt * 1024:(tt + 1) * 1024],
                                     in1=self.tC[:, 0:1024], op=ALU.add)
                    ve.drain().then_inc(self.sV, 1)
                    self.inc("sV")

    def phase_b(self):
        dr = self.dr
        sy, g, tn, sc, ve = self.syp, self.g, self.tn, self.sc, self.ve
        g.wait_ge(self.sT, self.S["sT"])
        g.wait_ge(self.sV, self.S["sV"])
        g.wait_ge(self.sC, self.S["sC"])
        xres = self.ARf[:, 0:16384]
        xnT = self.ARb[:, 0:16384]
        wic = [self.ARb[:, 16384 + i * 1024:16384 + (i + 1) * 1024]
               for i in range(2)]
        xnbf = self.ARb[:, 18432:19456]
        self.dma(g, self.rowf[0:1, 0:D], dr["ln1_g_r"], "sL")
        self.dma(g, self.rowf[0:1, D:2 * D], dr["ln1_b_r"], "sL")
        for e in [tn, sc, ve]:
            self.wL(e)
        self.load_gbc(self.rowf[0:1, 0:D], self.rowf[0:1, D:2 * D])
        ve.memset(self.tA[:, 0:3], 0.0)
        ve.drain().then_inc(self.sV, 1)
        sy.wait_ge(self.sV, self.inc("sV"))
        for dblk in range(16):
            self.sydma2(lambda h, dblk=dblk:
                        (dr["s_u"][dblk * 128:(dblk + 1) * 128, 0:3] if h == 0
                         else dr["l_part"][0][0:128, dblk * 3:dblk * 3 + 3]),
                        self.tA[:, 0:3], "sS")

        for tt in range(NT):
            x_ap = xres[:, tt * 1024:(tt + 1) * 1024]
            self.wS(ve)
            mu, rc = self.ln_stats(x_ap, 1e-5)
            self.ln_norm(x_ap, self.tA[:, 0:1024], mu, rc)
            ve.drain().then_inc(self.sV, 1)
            sy.wait_ge(self.sV, self.inc("sV"))
            self.dma(sy, dr["l_xa2"][tt], self.tA[:, 0:1024], "sS")
            self.dma(sy, dr["l_x2"][tt], x_ap, "sS")
            mu, rc = self.ln_stats(self.tA[:, 0:1024], 1e-5)
            self.ln_norm(self.tA[:, 0:1024], self.tB[:, 0:1024], mu, rc)
            ve.tensor_tensor(out=self.tB[:, 0:1024], in0=self.tB[:, 0:1024],
                             in1=self.gbc[:, 0:1024], op=ALU.mult)
            ve.drain()
            ve.tensor_tensor(out=xnbf, in0=self.tB[:, 0:1024],
                             in1=self.gbc[:, 1024:2048], op=ALU.add)
            ve.drain().then_inc(self.sV, 1)
            tn.wait_ge(self.sV, self.inc("sV"))
            self.transpose_to(lambda i: xnbf[:, i * 128:(i + 1) * 128], 8,
                              lambda i: xnT[:, i * 2048 + tt * 128:
                                            i * 2048 + (tt + 1) * 128])

        winv = dr["w_in"].rearrange("(c p) o -> p c o", p=128)
        for oc in range(32):
            buf = wic[oc % 2]
            if oc >= 2:
                g.wait_ge(self.sT, self.S["sT"])
            self.dma(g, buf.rearrange("p (c o) -> p c o", c=8),
                     winv[:, :, oc * 128:(oc + 1) * 128], "sL")
            self.wL(tn)
            tn.wait_ge(self.sC, self.S["sC"])
            for tc in range(4):
                ps = (self.psB, self.psC)[tc % 2]
                bid = ("ps", tc % 2)
                if bid in self.bank_ev:
                    tn.wait_ge(self.sC, self.bank_ev[bid])
                for ic in range(8):
                    tn.matmul(out=ps[:, 0:512],
                              lhsT=buf[:, ic * 128:(ic + 1) * 128],
                              rhs=xnT[:, ic * 2048 + tc * 512:
                                      ic * 2048 + (tc + 1) * 512],
                              start=(ic == 0), stop=(ic == 7))
                tcnt = self.mm_done()
                self.wS(sc)
                self.bank_ev[bid] = self.evac(
                    self.tA[:, 0:512] if tc % 2 == 0 else self.tB[:, 0:512],
                    ps[:, 0:512], tcount=tcnt)
                src = self.tA[:, 0:512] if tc % 2 == 0 else self.tB[:, 0:512]
                sy.wait_ge(self.sC, self.S["sC"])
                if oc < 16:
                    self.sydma2(lambda h, oc=oc, tc=tc:
                                dr["s_u"][oc * 128:(oc + 1) * 128,
                                          3 + QBLOCKS[h][tc] * 512:
                                          3 + QBLOCKS[h][tc] * 512 + 512],
                                src, "sS")
                else:
                    self.sydma2(lambda h, oc=oc, tc=tc:
                                dr["s_z"][(oc - 16) * 128:(oc - 15) * 128,
                                          QBLOCKS[h][tc] * 512:
                                          QBLOCKS[h][tc] * 512 + 512],
                                src, "sS")

    def phase_c(self):
        dr = self.dr
        sy, g, tn, sc, ve = self.syp, self.g, self.tn, self.sc, self.ve
        g.wait_ge(self.sT, self.S["sT"])
        g.wait_ge(self.sV, self.S["sV"])
        g.wait_ge(self.sC, self.S["sC"])
        wxp = self.ARb[:, 0:1536]
        wdt = self.ARb[0:DTR, 1536:3584]
        ucb = self.ARb[:, 4096:12288]
        dtb = self.ARb[0:DTR, 3584:4096]
        ublk = self.ARf[:, 0:8320]
        self.dma(g, self.tC[:, 0:64], dr["convw"], "sL")
        self.dma(g, self.tC[:, 64:80], dr["convb"], "sL")
        self.dma(g, self.tC[:, 80:96], dr["bdt_c"], "sL")
        self.dma(g, wxp.rearrange("p (c o) -> p c o", c=16),
                 dr["w_xp"].rearrange("(c p) o -> p c o", p=128), "sL")
        self.dma(g, wdt, dr["w_dt"], "sL")
        for e in [tn, sc, ve, sy]:
            self.wL(e)

        for tc in range(4):
            sy.wait_ge(self.sV, self.S["sV"])
            for dblk in range(16):
                self.sydma2(ublk[:, dblk * 520:dblk * 520 + 515],
                            lambda h, dblk=dblk, tc=tc:
                            dr["s_u"][dblk * 128:(dblk + 1) * 128,
                                      QBLOCKS[h][tc] * 512:
                                      QBLOCKS[h][tc] * 512 + 515], "sL")
            self.wL(ve)
            for dblk in range(16):
                u0 = ublk[:, dblk * 520:dblk * 520 + 515]
                self.wS(ve)
                ve.tensor_scalar(out=self.tA[:, 0:512], in0=u0[:, 0:512],
                                 scalar1=self.tC[:, dblk * 4:dblk * 4 + 1],
                                 scalar2=None, op0=ALU.mult)
                ve.drain()
                for k in range(1, 4):
                    ve.scalar_tensor_tensor(
                        out=self.tA[:, 0:512], in0=u0[:, k:k + 512],
                        scalar=self.tC[:, dblk * 4 + k:dblk * 4 + k + 1],
                        in1=self.tA[:, 0:512], op0=ALU.mult, op1=ALU.add)
                    ve.drain()
                ve.drain().then_inc(self.sV, 1)
                sc.wait_ge(self.sV, self.inc("sV"))
                self.wS(sc)
                sc.activation(out=self.tB[:, 0:512], in_=self.tA[:, 0:512],
                              func=ACT.Silu, bias=self.tC[:, 64 + dblk:65 + dblk],
                              scale=np.float32(1.0))
                sc.drain().then_inc(self.sC, 1)
                ve.wait_ge(self.sC, self.inc("sC"))
                ve.tensor_copy(out=ucb[:, dblk * 512:(dblk + 1) * 512],
                               in_=self.tB[:, 0:512])
                ve.drain().then_inc(self.sV, 1)
                self.inc("sV")
                sy.wait_ge(self.sC, self.S["sC"])
                self.sydma2(lambda h, dblk=dblk, tc=tc:
                            dr["s_uc"][dblk * 128:(dblk + 1) * 128,
                                       QBLOCKS[h][tc] * 512:
                                       QBLOCKS[h][tc] * 512 + 512],
                            self.tB[:, 0:512], "sS")
            ve.drain().then_inc(self.sV, 1)
            tn.wait_ge(self.sV, self.inc("sV"))
            bid = ("ps", 0)
            if bid in self.bank_ev:
                tn.wait_ge(self.sC, self.bank_ev[bid])
            for dblk in range(16):
                tn.matmul(out=self.psB[0:96, 0:512],
                          lhsT=wxp[:, dblk * 96:(dblk + 1) * 96],
                          rhs=ucb[:, dblk * 512:(dblk + 1) * 512],
                          start=(dblk == 0), stop=(dblk == 15))
            tcnt = self.mm_done()
            sc.wait_ge(self.sT, tcnt)
            sc.copy(out=dtb, in_=self.psB[0:DTR, 0:512])
            sc.drain()
            sc.copy(out=self.tA[0:32, 0:512], in_=self.psB[64:96, 0:512])
            sc.drain().then_inc(self.sC, 1)
            self.bank_ev[bid] = self.inc("sC")
            sy.wait_ge(self.sC, self.S["sC"])
            self.sydma2(lambda h, tc=tc:
                        dr["s_bc"][0:32, QBLOCKS[h][tc] * 512:
                                   QBLOCKS[h][tc] * 512 + 512],
                        self.tA[0:32, 0:512], "sS")
            for dblk in range(16):
                ps = (self.psB, self.psC)[dblk % 2]
                bid = ("ps", dblk % 2)
                if bid in self.bank_ev:
                    tn.wait_ge(self.sC, self.bank_ev[bid])
                tn.matmul(out=ps[:, 0:512],
                          lhsT=wdt[:, dblk * 128:(dblk + 1) * 128],
                          rhs=dtb, start=True, stop=True)
                tcnt = self.mm_done()
                sc.wait_ge(self.sT, tcnt)
                self.wS(sc)
                bcol = self.tC[:, 80 + dblk:81 + dblk]
                sc.activation(out=self.tA[:, 0:512], in_=ps[:, 0:512],
                              func=ACT.Relu, bias=bcol, scale=np.float32(1.0))
                sc.drain()
                sc.activation(out=self.tB[:, 0:512], in_=ps[:, 0:512],
                              func=ACT.Abs, bias=bcol, scale=np.float32(1.0))
                sc.drain().then_inc(self.sC, 1)
                self.bank_ev[bid] = self.inc("sC")
                sc.activation(out=self.tB[:, 0:512], in_=self.tB[:, 0:512],
                              func=ACT.Exp, scale=np.float32(-1.0))
                sc.drain()
                sc.activation(out=self.tB[:, 0:512], in_=self.tB[:, 0:512],
                              func=ACT.Ln, bias=np.float32(1.0), scale=np.float32(1.0))
                sc.drain().then_inc(self.sC, 1)
                ve.wait_ge(self.sC, self.inc("sC"))
                self.wS(ve)
                ve.tensor_tensor(out=self.tA[:, 0:512], in0=self.tA[:, 0:512],
                                 in1=self.tB[:, 0:512], op=ALU.add)
                ve.drain().then_inc(self.sV, 1)
                self.inc("sV")
                sy.wait_ge(self.sV, self.S["sV"])
                self.sydma2(lambda h, dblk=dblk, tc=tc:
                            dr["s_dlt"][dblk * 128:(dblk + 1) * 128,
                                        QBLOCKS[h][tc] * 512:
                                        QBLOCKS[h][tc] * 512 + 512],
                            self.tA[:, 0:512], "sS")

    def phase_d(self):
        dr = self.dr
        sy, g, tn, sc, ve = self.syp, self.g, self.tn, self.sc, self.ve
        g.wait_ge(self.sT, self.S["sT"])
        g.wait_ge(self.sV, self.S["sV"])
        g.wait_ge(self.sC, self.S["sC"])
        dlp = [self.ARf[:, 0:1024], self.ARf[:, 16384:17408]]
        ucp = [self.ARf[:, 1024:2048], self.ARf[:, 17408:18432]]
        zzp = [self.ARf[:, 7168:8192], self.ARf[:, 18432:19456]]
        du = self.ARf[:, 2048:3072]
        aa = [self.ARf[:, 3072:4096], self.ARf[:, 4096:5120]]
        hh_ = self.ARf[:, 5120:6144]; bb = self.ARf[:, 6144:7168]
        yall = self.ARf[:, 8192:16384]
        dmarks = []
        bcB = self.ARb[:, 0:8192]
        bcC = self.ARb[:, 8192:16384]
        gbf = self.bfA[:, 0:1024]
        self.dma(g, self.tC[:, 0:128], dr["alog_c"], "sL")
        self.dma(g, self.tC[:, 128:136], dr["dskip_c"], "sL")
        for e in [tn, sc, ve, sy]:
            self.wL(e)
        sc.activation(out=self.tB[:, 0:128], in_=self.tC[:, 0:128],
                      func=ACT.Exp)
        sc.drain().then_inc(self.sC, 1)
        ve.wait_ge(self.sC, self.inc("sC"))
        ve.tensor_scalar(out=self.tC[:, 0:128], in0=self.tB[:, 0:128],
                         scalar1=-1.0, scalar2=None, op0=ALU.mult)
        ve.drain()
        ve.memset(self.carry[:, 0:128], 0.0)
        ve.drain().then_inc(self.sV, 1)
        self.inc("sV")

        for tc in range(4):
            for nh in range(2):
                sc.wait_ge(self.sV, self.S["sV"])
                for n8 in range(8):
                    n = nh * 8 + n8
                    g.wait_ge(self.sT, self.S["sT"])
                    self.dma(g, self.rowf[0:1, 0:1024],
                             dr["s_bc"][n:n + 1, tc * 1024:(tc + 1) * 1024],
                             "sL")
                    self.dma(g, self.rowf[0:1, 1024:2048],
                             dr["s_bc"][16 + n:17 + n,
                                        tc * 1024:(tc + 1) * 1024], "sL")
                    self.wL(tn)
                    for q in range(4):
                        bid = ("psA", q)
                        if bid in self.bank_ev:
                            tn.wait_ge(self.sC, self.bank_ev[bid])
                    for half in range(2):
                        tn.matmul(out=self.psA[:, half * 512:(half + 1) * 512],
                                  lhsT=self.onesfr[0:1, :],
                                  rhs=self.rowf[0:1, half * 512:
                                                (half + 1) * 512],
                                  start=True, stop=True)
                        tn.matmul(out=self.psA[:, 1024 + half * 512:
                                               1024 + (half + 1) * 512],
                                  lhsT=self.onesfr[0:1, :],
                                  rhs=self.rowf[0:1, 1024 + half * 512:
                                                1024 + (half + 1) * 512],
                                  start=True, stop=True)
                    tcnt = self.mm_done()
                    self.evac(bcB[:, n8 * 1024:(n8 + 1) * 1024],
                              self.psA[:, 0:1024], tcount=tcnt)
                    c = self.evac(bcC[:, n8 * 1024:(n8 + 1) * 1024],
                                  self.psA[:, 1024:2048], tcount=tcnt)
                    for q in range(4):
                        self.bank_ev[("psA", q)] = c
                for dblk in range(8):
                    pb = dblk % 2
                    dl, uc, zz = dlp[pb], ucp[pb], zzp[pb]
                    if len(dmarks) >= 2:
                        sy.wait_ge(self.sC, dmarks[-2][0])
                        sy.wait_ge(self.sV, dmarks[-2][1])
                    else:
                        sy.wait_ge(self.sV, self.S["sV"])
                    self.sydma2(dl, lambda h, dblk=dblk, tc=tc:
                                dr["s_dlt"][h * 1024 + dblk * 128:
                                            h * 1024 + dblk * 128 + 128,
                                            tc * 1024:(tc + 1) * 1024], "sL")
                    self.sydma2(uc, lambda h, dblk=dblk, tc=tc:
                                dr["s_uc"][h * 1024 + dblk * 128:
                                           h * 1024 + dblk * 128 + 128,
                                           tc * 1024:(tc + 1) * 1024], "sL")
                    if nh == 1:
                        self.sydma2(zz, lambda h, dblk=dblk, tc=tc:
                                    dr["s_z"][h * 1024 + dblk * 128:
                                              h * 1024 + dblk * 128 + 128,
                                              tc * 1024:(tc + 1) * 1024], "sL")
                    self.wL(ve)
                    ve.tensor_tensor(out=du, in0=dl, in1=uc, op=ALU.mult)
                    ve.drain()
                    if nh == 0:
                        ve.tensor_scalar(out=yall[:, dblk * 1024:
                                                  (dblk + 1) * 1024],
                                         in0=uc,
                                         scalar1=self.tC[:, 128 + dblk:
                                                         129 + dblk],
                                         scalar2=None, op0=ALU.mult)
                        ve.drain()
                    ve.drain().then_inc(self.sV, 1)
                    self.inc("sV")
                    vbase = self.S["sV"]
                    sc.wait_ge(self.sV, vbase)
                    for n8 in range(8):
                        n = nh * 8 + n8
                        if n8 >= 2:
                            sc.wait_ge(self.sV, vbase + n8 - 1)
                        sc.activation(out=aa[n8 % 2], in_=dl, func=ACT.Exp,
                                      scale=self.tC[:, dblk * 16 + n:
                                                    dblk * 16 + n + 1])
                        sc.drain().then_inc(self.sC, 1)
                        ve.wait_ge(self.sC, self.inc("sC"))
                        ve.tensor_tensor(out=bb, in0=du,
                                         in1=bcB[:, n8 * 1024:(n8 + 1) * 1024],
                                         op=ALU.mult)
                        ve.drain()
                        ve.tensor_tensor_scan(
                            out=hh_, data0=aa[n8 % 2], data1=bb,
                            initial=self.carry[:, dblk * 16 + n:
                                               dblk * 16 + n + 1],
                            op0=ALU.mult, op1=ALU.add)
                        ve.drain()
                        ve.tensor_copy(out=self.carry[:, dblk * 16 + n:
                                                      dblk * 16 + n + 1],
                                       in_=hh_[:, 1023:1024])
                        ve.tensor_tensor(out=bb, in0=hh_,
                                         in1=bcC[:, n8 * 1024:(n8 + 1) * 1024],
                                         op=ALU.mult)
                        ve.drain()
                        ve.tensor_tensor(out=yall[:, dblk * 1024:
                                                  (dblk + 1) * 1024],
                                         in0=yall[:, dblk * 1024:
                                                  (dblk + 1) * 1024],
                                         in1=bb, op=ALU.add)
                        ve.drain().then_inc(self.sV, 1)
                        self.inc("sV")
                    if nh == 1:
                        sc.wait_ge(self.sV, self.S["sV"])
                        self.wL(sc)
                        sc.activation(out=zz, in_=zz, func=ACT.Silu)
                        sc.drain().then_inc(self.sC, 1)
                        ve.wait_ge(self.sC, self.inc("sC"))
                        self.wS(ve)
                        ve.tensor_tensor(out=gbf,
                                         in0=yall[:, dblk * 1024:
                                                  (dblk + 1) * 1024],
                                         in1=zz, op=ALU.mult)
                        ve.drain().then_inc(self.sV, 1)
                        self.inc("sV")
                        sy.wait_ge(self.sV, self.S["sV"])
                        self.sydma2(lambda h, dblk=dblk, tc=tc:
                                    dr["s_g"][h * 1024 + dblk * 128:
                                              h * 1024 + dblk * 128 + 128,
                                              tc * 1024:(tc + 1) * 1024],
                                    gbf, "sS")
                    dmarks.append((self.S["sC"], self.S["sV"]))

    def phase_e(self, out_sh):
        dr = self.dr
        sy, g, tn, sc, ve = self.syp, self.g, self.tn, self.sc, self.ve
        g.wait_ge(self.sT, self.S["sT"])
        g.wait_ge(self.sV, self.S["sV"])
        g.wait_ge(self.sC, self.S["sC"])
        gT = self.ARb[:, 0:8192]
        wout1 = self.ARb[:, 8192:24576]
        hsT = self.ARb[:, 8192:24576]
        hsbf = self.bfA[:, 0:1024]
        wrt = self.ARf[:, 15360:15392]
        hsfT = self.ARf[:, 15488:16512]
        colsM = self.ARf[:, 16512:16576]
        xt2 = self.ARf[:, 16576:17600]
        self.dma(g, wout1.rearrange("p (c o) -> p c o", c=16),
                 dr["w_out1"].rearrange("(c p) o -> p c o", p=128), "sL")
        self.dma(g, wrt.rearrange("p (c o) -> p c o", c=8),
                 dr["w_rt"].rearrange("(c p) o -> p c o", p=128), "sL")
        for e in [tn, sc, ve, sy]:
            self.wL(e)

        for qg in range(4):
            sy.wait_ge(self.sT, self.S["sT"])
            self.sydma2(gT.rearrange("p (c t) -> p c t", c=16),
                        lambda h, qg=qg:
                        dr["s_g"].rearrange("(c p) t -> p c t", p=128)
                        [:, :, QBLOCKS[h][qg] * 512:QBLOCKS[h][qg] * 512 + 512],
                        "sL")
            self.wL(tn)
            for tq in range(4):
                tt = qg * 4 + tq
                for half in range(2):
                    ps = (self.psB, self.psC)[half]
                    bid = ("ps", half)
                    if bid in self.bank_ev:
                        tn.wait_ge(self.sC, self.bank_ev[bid])
                    for c16 in range(16):
                        tn.matmul(out=ps[:, 0:512],
                                  lhsT=gT[:, c16 * 512 + tq * 128:
                                          c16 * 512 + (tq + 1) * 128],
                                  rhs=wout1[:, c16 * 1024 + half * 512:
                                            c16 * 1024 + (half + 1) * 512],
                                  start=(c16 == 0), stop=(c16 == 15))
                tcnt = self.mm_done()
                sc.wait_ge(self.sV, self.S["sV"])
                self.bank_ev[("ps", 0)] = self.evac(
                    self.tA[:, 0:512], self.psB[:, 0:512], tcount=tcnt)
                self.bank_ev[("ps", 1)] = self.evac(
                    self.tA[:, 512:1024], self.psC[:, 0:512], tcount=tcnt)
                sy.wait_ge(self.sV, self.S["sV"])
                sy.wait_ge(self.sS, self.S["sS"])
                self.dma(sy, self.tB[:, 0:1024], dr["l_xa2"][tt], "sL")
                self.dma(sy, xt2, dr["l_x2"][tt], "sL")
                ve.wait_ge(self.sC, self.S["sC"])
                self.wL(ve)
                self.wS(ve)
                ve.tensor_tensor(out=self.tA[:, 0:1024], in0=self.tA[:, 0:1024],
                                 in1=self.tB[:, 0:1024], op=ALU.add)
                ve.drain().then_inc(self.sV, 1)
                self.inc("sV")
                mu, rc = self.ln_stats(self.tA[:, 0:1024], 1e-5)
                self.ln_norm(self.tA[:, 0:1024], self.tB[:, 0:1024], mu, rc)
                ve.tensor_tensor(out=xt2, in0=xt2, in1=self.tB[:, 0:1024],
                                 op=ALU.add)
                ve.drain().then_inc(self.sV, 1)
                sy.wait_ge(self.sV, self.inc("sV"))
                self.dma(sy, dr["l_x3"][tt], xt2, "sS")
                mu, rc = self.ln_stats(xt2, 1e-5)
                self.ln_norm(xt2, self.tB[:, 0:1024], mu, rc)
                ve.drain().then_inc(self.sV, 1)
                sy.wait_ge(self.sV, self.inc("sV"))
                self.dma(sy, dr["l_hs"][tt], self.tB[:, 0:1024], "sS")
                ve.drain().then_inc(self.sV, 1)
                tn.wait_ge(self.sV, self.inc("sV"))
                self.transpose_to(lambda i: self.tB[:, i * 128:(i + 1) * 128],
                                  8, lambda i: hsfT[:, i * 128:(i + 1) * 128],
                                  f32=True)
                tn.wait_ge(self.sC, self.S["sC"])
                for ic in range(8):
                    tn.matmul(out=self.psB[:, 0:NE],
                              lhsT=hsfT[:, ic * 128:(ic + 1) * 128],
                              rhs=wrt[:, ic * 4:(ic + 1) * 4],
                              start=(ic == 0), stop=(ic == 7))
                tcnt = self.mm_done()
                self.bank_ev[("ps", 0)] = self.evac(
                    self.tA[:, 0:NE], self.psB[:, 0:NE], tcount=tcnt)
                ve.wait_ge(self.sC, self.S["sC"])
                ve.tensor_reduce(out=self.cols[:, 23:24], in_=self.tA[:, 0:NE],
                                 axis=AX, op=ALU.max)
                ve.drain()
                ve.memset(self.cols[:, 24:25], 1.0)
                ve.drain()
                for ee in range(NE):
                    ve.tensor_scalar(out=self.cols[:, 25:26],
                                     in0=self.tA[:, ee:ee + 1],
                                     scalar1=self.cols[:, 23:24], scalar2=None,
                                     op0=ALU.is_ge)
                    ve.drain()
                    ve.tensor_tensor(out=colsM[:, tt * 4 + ee:tt * 4 + ee + 1],
                                     in0=self.cols[:, 25:26],
                                     in1=self.cols[:, 24:25], op=ALU.mult)
                    ve.drain()
                    if ee < NE - 1:
                        ve.tensor_scalar(out=self.cols[:, 25:26],
                                         in0=colsM[:, tt * 4 + ee:
                                                   tt * 4 + ee + 1],
                                         scalar1=-1.0, scalar2=1.0,
                                         op0=ALU.mult, op1=ALU.add)
                        ve.drain()
                        ve.tensor_tensor(out=self.cols[:, 24:25],
                                         in0=self.cols[:, 24:25],
                                         in1=self.cols[:, 25:26], op=ALU.mult)
                        ve.drain()
                ve.drain().then_inc(self.sV, 1)
                self.inc("sV")

        for tt in range(NT):
            ve.drain().then_inc(self.sV, 1)
            sy.wait_ge(self.sV, self.inc("sV"))
            sy.wait_ge(self.sS, self.S["sS"])
            self.dma(sy, self.tB[:, 0:1024], dr["l_hs"][tt], "sL")
            self.wL(ve)
            ve.wait_ge(self.sT, self.S["sT"])
            ve.tensor_copy(out=hsbf, in_=self.tB[:, 0:1024])
            ve.drain().then_inc(self.sV, 1)
            tn.wait_ge(self.sV, self.inc("sV"))
            self.transpose_to(lambda i: hsbf[:, i * 128:(i + 1) * 128], 8,
                              lambda i: hsT[:, tt * 1024 + i * 128:
                                            tt * 1024 + (i + 1) * 128])

        def hsT_ap(tt, ic):
            return hsT[:, tt * 1024 + ic * 128:tt * 1024 + (ic + 1) * 128]

        g.wait_ge(self.sT, self.S["sT"])
        g.wait_ge(self.sC, self.S["sC"])
        self._vq = []
        w1b = [self.ARb[:, i * 2048:(i + 1) * 2048] for i in range(2)]
        intr = self.ARb[:, 4096:8192]
        w1v = dr["w1"].rearrange("e (c p) o -> e p c o", p=128)
        for oc in range(16):
            self.wS(ve)
            for ee in range(NE):
                bslot = 6528 + ee * 256
                self.dma(g, self.rowb[0:1, bslot:bslot + 256],
                         dr["b1_r"][ee:ee + 1, oc * 256:(oc + 1) * 256], "sL")
                buf = w1b[(oc * NE + ee) % 2]
                if oc * NE + ee >= 2:
                    g.wait_ge(self.sT, self.S["sT"])
                self.dma(g, buf.rearrange("p (c o) -> p c o", c=8),
                         w1v[ee][:, :, oc * 256:(oc + 1) * 256], "sL")
                self.wL(tn)
                tn.wait_ge(self.sC, self.S["sC"])
                for tt in range(NT):
                    bk = tt % 4
                    ps = self.psA[:, bk * 512:bk * 512 + 256]
                    bid = ("psA", bk)
                    if bid in self.bank_ev:
                        tn.wait_ge(self.sC, self.bank_ev[bid])
                    tn.matmul(out=ps, lhsT=self.onesbr[0:1, :],
                              rhs=self.rowb[0:1, bslot:bslot + 256],
                              start=True, stop=False)
                    for ic in range(8):
                        tn.matmul(out=ps, lhsT=hsT_ap(tt, ic),
                                  rhs=buf[:, ic * 256:(ic + 1) * 256],
                                  start=False, stop=(ic == 7))
                    tcnt = self.mm_done()
                    dstf = (self.tA, self.tB, self.tC, self.tDmp)[bk][:, 0:256]
                    if len(self._vq) >= 4:
                        sc.wait_ge(self.sV, self._vq[-4])
                    self.bank_ev[bid] = self.evac(dstf, ps,
                                                  func=ACT.Gelu, tcount=tcnt)
                    ve.wait_ge(self.sC, self.S["sC"])
                    mc = colsM[:, tt * 4 + ee:tt * 4 + ee + 1]
                    if ee == 0:
                        ve.tensor_scalar(out=intr[:, tt * 256:(tt + 1) * 256],
                                         in0=dstf, scalar1=mc, scalar2=None,
                                         op0=ALU.mult)
                    else:
                        ve.scalar_tensor_tensor(
                            out=intr[:, tt * 256:(tt + 1) * 256], in0=dstf,
                            scalar=mc, in1=intr[:, tt * 256:(tt + 1) * 256],
                            op0=ALU.mult, op1=ALU.add)
                    ve.drain().then_inc(self.sV, 1)
                    self._vq.append(self.inc("sV"))
            sy.wait_ge(self.sV, self.S["sV"])
            for tt in range(NT):
                self.dma(sy, dr["l_inter"][oc][tt],
                         intr[:, tt * 256:(tt + 1) * 256], "sS")

        g.wait_ge(self.sT, self.S["sT"])
        g.wait_ge(self.sV, self.S["sV"])
        self.wS(g)
        wo = self.ARb[:, 0:16384]
        intT = self.ARb[:, 24576:28672]
        intf = self.bfB[:, 0:4096]
        part = self.ARf[:, 0:16384]
        self.dma(g, self.rowf[0:1, 0:D], dr["ln2_g_r"], "sL")
        self.dma(g, self.rowf[0:1, D:2 * D], dr["ln2_b_r"], "sL")
        for e in [tn, sc, ve]:
            self.wL(e)
        self.load_gbc(self.rowf[0:1, 0:D], self.rowf[0:1, D:2 * D])
        wov = dr["wo"].rearrange("(c p) o -> p c o", p=128)
        for rnd in range(2):
            g.wait_ge(self.sT, self.S["sT"])
            self.dma(g, wo.rearrange("p (c o) -> p c o", c=16),
                     wov[:, 16 * rnd:16 * rnd + 16, :], "sL")
            self.wL(tn)
            for tt in range(NT):
                ve.drain().then_inc(self.sV, 1)
                sy.wait_ge(self.sV, self.inc("sV"))
                sy.wait_ge(self.sS, self.S["sS"])
                for oc in range(16):
                    self.dma(sy, intf[:, oc * 256:(oc + 1) * 256],
                             dr["l_inter"][oc][tt], "sL")
                self.wL(ve)
                ve.drain().then_inc(self.sV, 1)
                tn.wait_ge(self.sV, self.inc("sV"))
                self.transpose_to(
                    lambda i: intf[:, (16 * rnd + i) * 128:
                                   (16 * rnd + i + 1) * 128], 16,
                    lambda i: intT[:, i * 128:(i + 1) * 128])
                tn.wait_ge(self.sC, self.S["sC"])
                for half in range(2):
                    ps = (self.psB, self.psC)[half]
                    bid = ("ps", half)
                    if bid in self.bank_ev:
                        tn.wait_ge(self.sC, self.bank_ev[bid])
                    if rnd == 0:
                        tn.matmul(out=ps[:, 0:512],
                                  lhsT=self.onesbr[0:1, :],
                                  rhs=self.rowb[0:1, 5504 + half * 512:
                                                5504 + (half + 1) * 512],
                                  start=True, stop=False)
                    for c16 in range(16):
                        tn.matmul(out=ps[:, 0:512],
                                  lhsT=intT[:, c16 * 128:(c16 + 1) * 128],
                                  rhs=wo[:, c16 * 1024 + half * 512:
                                         c16 * 1024 + (half + 1) * 512],
                                  start=(rnd == 1 and c16 == 0),
                                  stop=(c16 == 15))
                tcnt = self.mm_done()
                pslot = part[:, tt * 1024:(tt + 1) * 1024]
                self.bank_ev[("ps", 0)] = self.evac(
                    self.tA[:, 0:512], self.psB[:, 0:512], tcount=tcnt)
                self.bank_ev[("ps", 1)] = self.evac(
                    self.tA[:, 512:1024], self.psC[:, 0:512], tcount=tcnt)
                ve.wait_ge(self.sC, self.S["sC"])
                if rnd == 0:
                    ve.tensor_copy(out=pslot, in_=self.tA[:, 0:1024])
                    ve.drain().then_inc(self.sV, 1)
                    self.inc("sV")
                    continue
                ve.tensor_tensor(out=self.tA[:, 0:1024], in0=pslot,
                                 in1=self.tA[:, 0:1024], op=ALU.add)
                ve.drain().then_inc(self.sV, 1)
                sy.wait_ge(self.sV, self.inc("sV"))
                self.dma(sy, self.tB[:, 0:1024], dr["l_hs"][tt], "sL")
                self.wL(ve)
                ve.tensor_tensor(out=self.tA[:, 0:1024], in0=self.tA[:, 0:1024],
                                 in1=self.tB[:, 0:1024], op=ALU.add)
                ve.drain().then_inc(self.sV, 1)
                self.inc("sV")
                mu, rc = self.ln_stats(self.tA[:, 0:1024], 1e-12)
                self.ln_norm(self.tA[:, 0:1024], self.tB[:, 0:1024], mu, rc)
                ve.tensor_tensor(out=self.tB[:, 0:1024], in0=self.tB[:, 0:1024],
                                 in1=self.gbc[:, 0:1024], op=ALU.mult)
                ve.drain()
                ve.tensor_tensor(out=self.tB[:, 0:1024], in0=self.tB[:, 0:1024],
                                 in1=self.gbc[:, 1024:2048], op=ALU.add)
                ve.drain().then_inc(self.sV, 1)
                self.inc("sV")
                mu, rc = self.ln_stats(self.tB[:, 0:1024], 1e-5)
                self.ln_norm(self.tB[:, 0:1024], self.tC[:, 0:1024], mu, rc)
                ve.drain().then_inc(self.sV, 1)
                sy.wait_ge(self.sV, self.inc("sV"))
                self.dma(sy, self.tA[:, 0:1024], dr["l_x3"][tt], "sL")
                self.wL(ve)
                ve.tensor_tensor(out=self.tC[:, 0:1024], in0=self.tC[:, 0:1024],
                                 in1=self.tA[:, 0:1024], op=ALU.add)
                ve.drain().then_inc(self.sV, 1)
                sy.wait_ge(self.sV, self.inc("sV"))
                self.dma(sy, out_sh[tt * 128:(tt + 1) * 128, :],
                         self.tC[:, 0:1024], "sS")
        sy.wait_ge(self.sS, self.S["sS"])

_NC_CACHE = {}


def _get_nc(nphase=NPH):
    if nphase not in _NC_CACHE:
        _NC_CACHE[nphase] = Build(nphase).build()
    return _NC_CACHE[nphase]


def _mk_inputs(inputs, nphase=NPH):
    f32 = np.float32
    I = {k: np.asarray(v) for k, v in inputs.items()}
    x = np.ascontiguousarray(I["L0_attn_x"], dtype=f32)
    id_bf = np.eye(128, dtype=bf16)
    id_f32 = np.eye(128, dtype=f32)
    onesb_r = np.ones((1, 128), dtype=bf16)
    onesf_r = np.ones((1, 128), dtype=f32)
    onesf_c = np.ones((128, 1), dtype=f32)
    dm = np.zeros((128, 2048), dtype=bf16)
    for dj in range(4):
        for p in range(128):
            for qq in range(512):
                if dj * 128 + p <= qq:
                    dm[p, dj * 512 + qq] = 1.0
    alog = np.ascontiguousarray(I["L1_attn_A_log"], dtype=f32)
    dskip = np.ascontiguousarray(I["L1_attn_D_skip"], dtype=f32)
    convw = np.ascontiguousarray(I["L1_attn_conv_w"], dtype=f32)
    convb = np.ascontiguousarray(I["L1_attn_conv_b"], dtype=f32)
    bdt = np.ascontiguousarray(I["L1_attn_b_dt"], dtype=f32)
    maps = []
    for c in range(8):
        b, h = c // 2, c % 2
        rows = np.concatenate([x[b, gb * 512:(gb + 1) * 512, :]
                               for gb in QBLOCKS[h]], axis=0)
        invq2 = np.zeros((128, NT), dtype=f32)
        for tt in range(NT):
            tg = tglob(tt, h)
            q = np.arange(tg, tg + 128, dtype=np.float64) + 1.0
            invq2[:, tt] = (1.0 / (q * q)).astype(f32)
        al = np.zeros((128, 128), dtype=f32)
        for j in range(8):
            al[:, j * 16:(j + 1) * 16] = alog[h * 1024 + j * 128:
                                              h * 1024 + (j + 1) * 128, :]
        dsk = np.zeros((128, 8), dtype=f32)
        for j in range(8):
            dsk[:, j] = dskip[h * 1024 + j * 128:h * 1024 + (j + 1) * 128]
        cw = np.zeros((128, 64), dtype=f32)
        cb = np.zeros((128, 16), dtype=f32)
        bd = np.zeros((128, 16), dtype=f32)
        for j in range(16):
            cw[:, j * 4:(j + 1) * 4] = convw[j * 128:(j + 1) * 128, :]
            cb[:, j] = convb[j * 128:(j + 1) * 128]
            bd[:, j] = bdt[j * 128:(j + 1) * 128]
        m = {
            "x_sh": np.ascontiguousarray(rows),
            "w_hid": np.ascontiguousarray(I["L0_attn_W_hidden"], f32),
            "w_qk": np.ascontiguousarray(I["L0_attn_W_qk"], f32),
            "osg0": np.ascontiguousarray(I["L0_attn_os_gamma"][0].reshape(128, 1), f32),
            "osb0": np.ascontiguousarray(I["L0_attn_os_beta"][0].reshape(128, 1), f32),
            "osg1": np.ascontiguousarray(I["L0_attn_os_gamma"][1].reshape(128, 1), f32),
            "osb1": np.ascontiguousarray(I["L0_attn_os_beta"][1].reshape(128, 1), f32),
            "w_out0": np.ascontiguousarray(I["L0_attn_W_out"], f32),
            "wg_half": np.ascontiguousarray(I["L0_ffn_w_gate"][h * 2048:(h + 1) * 2048], f32),
            "wu_half": np.ascontiguousarray(I["L0_ffn_w_up"][h * 2048:(h + 1) * 2048], f32),
            "wd_half": np.ascontiguousarray(I["L0_ffn_w_down"][h * 512:(h + 1) * 512], f32),
            "w_in": np.ascontiguousarray(I["L1_attn_W_in"], f32),
            "convw": cw, "convb": cb, "bdt_c": bd,
            "w_xp": np.ascontiguousarray(I["L1_attn_W_xproj"], f32),
            "w_dt": np.ascontiguousarray(I["L1_attn_W_dt"], f32),
            "alog_c": al, "dskip_c": dsk,
            "w_out1": np.ascontiguousarray(I["L1_attn_W_out"], f32),
            "w_rt": np.ascontiguousarray(I["L1_ffn_W_router"], f32),
            "w1": np.ascontiguousarray(I["L1_ffn_W1"], f32),
            "wo": np.ascontiguousarray(I["L1_ffn_Wo"], f32),
            "b_hid_r": np.ascontiguousarray(I["L0_attn_b_hidden"].reshape(1, -1), f32),
            "b_qk_r": np.ascontiguousarray(I["L0_attn_b_qk"].reshape(1, -1), f32),
            "b_out0_r": np.ascontiguousarray(I["L0_attn_b_out"].reshape(1, -1), f32),
            "b1_r": np.ascontiguousarray(I["L1_ffn_b1"], f32),
            "bo_r": np.ascontiguousarray(I["L1_ffn_bo"].reshape(1, -1), f32),
            "ln_g_r": np.ascontiguousarray(I["L0_attn_ln_g"].reshape(1, -1), f32),
            "ln_b_r": np.ascontiguousarray(I["L0_attn_ln_b"].reshape(1, -1), f32),
            "ln1_g_r": np.ascontiguousarray(I["L1_attn_ln_gamma"].reshape(1, -1), f32),
            "ln1_b_r": np.ascontiguousarray(I["L1_attn_ln_beta"].reshape(1, -1), f32),
            "ln2_g_r": np.ascontiguousarray(I["L1_ffn_ln_gamma"].reshape(1, -1), f32),
            "ln2_b_r": np.ascontiguousarray(I["L1_ffn_ln_beta"].reshape(1, -1), f32),
            "id_bf": id_bf, "id_f32": id_f32,
            "onesb_r": onesb_r, "onesf_r": onesf_r, "onesf_c": onesf_c,
            "invq2_c": invq2, "dmask": dm,
        }
        maps.append(m)
    return maps


def kernel(**inputs):
    nphase = NPH
    nc = _get_nc(nphase)
    maps = _mk_inputs(inputs, nphase)
    res = run_bass_kernel_spmd(nc, maps, core_ids=list(range(8)))
    out = np.zeros((B, T, D), dtype=np.float32)
    for c in range(8):
        b, h = c // 2, c % 2
        o = res.results[c]["out_sh"]
        for bi, gb in enumerate(QBLOCKS[h]):
            out[b, gb * 512:(gb + 1) * 512, :] = o[bi * 512:(bi + 1) * 512, :]
    return out
